# version 09
# baseline (speedup 1.0000x reference)
import contextlib

import numpy as np
import ml_dtypes
import concourse.bass as bass
import concourse.mybir as mybir
from concourse.bass_utils import run_bass_kernel_spmd

F32 = mybir.dt.float32
BF16 = mybir.dt.bfloat16
I32 = mybir.dt.int32
AF = mybir.ActivationFunctionType
ALU = mybir.AluOpType
AX = mybir.AxisListType
ds = bass.ds

B, T, D, KD, V, HD = 4, 4096, 1024, 128, 2048, 4096
NT = T // 128
NQ = 16
TAU = float(np.sqrt(np.float32(KD)))
EPS1, EPS2 = 1e-5, 1e-6
PW = 4608
PR = PW - 1
NEG = -1e30

DEBUG = False


ENGS = ("tensor", "scalar", "vector", "sync", "gpsimd")
NLANES = 4
QSEM = {"sync": "q_s", "scalar": "q_a", "gpsimd": "q_g", "vector": "q_v", "tensor": "q_t"}
QLANES = {e: [f"{q}{i}" for i in range(NLANES)] for e, q in QSEM.items()}
ESEM = {"tensor": "s_pe", "scalar": "s_act", "vector": "s_dve", "sync": "s_sp", "gpsimd": "s_pl"}
ALL_SEMS = sorted({s for ls in QLANES.values() for s in ls} | set(ESEM.values()))


class Step:
    __slots__ = ("i", "eng", "emit", "deps", "raw", "dma", "phase", "inc", "tag")

    def __init__(self, i, eng, emit, deps, raw, dma, phase, tag):
        self.i = i
        self.eng = eng
        self.emit = emit
        self.deps = deps
        self.raw = raw
        self.dma = dma
        self.phase = phase
        self.inc = None
        self.tag = tag


class Buf:
    __slots__ = ("w", "rs")

    def __init__(self):
        self.w = None
        self.rs = []


class Prog:
    def __init__(self):
        self.steps = []
        self.phase = 0

    def next_phase(self):
        self.phase += 1

    def fence(self):
        dmas = [st for st in self.steps if st.dma]
        for eng in ENGS:
            self.step(eng, lambda e: None, deps=dmas, tag="fence")

    def step(self, eng, emit, reads=(), writes=(), deps=(), dma=False, tag=""):
        dd = []
        raw = set()
        for b in reads:
            if b.w is not None:
                dd.append(b.w)
                raw.add(b.w.i)
        for b in writes:
            if b.w is not None:
                dd.append(b.w)
            dd.extend(b.rs)
        for d in deps:
            if d is not None:
                dd.append(d)
                raw.add(d.i)
        st = Step(len(self.steps), eng, emit, dd, raw, dma, self.phase, tag)
        self.steps.append(st)
        for b in reads:
            b.rs.append(st)
        for b in writes:
            b.w = st
            b.rs = []
        return st

    def dma(self, eng, out_ap, in_ap, reads=(), writes=(), deps=(), tag=""):
        def emit(e, o=out_ap, i=in_ap):
            return e.dma_start(out=o() if callable(o) else o, in_=i() if callable(i) else i)

        return self.step(eng, emit, reads=reads, writes=writes, deps=deps, dma=True, tag=tag)

    def _needs_wait(self, st, d):
        if d.inc is None:
            return False
        if d.dma:
            return True
        if d.phase != st.phase:
            return False
        if d.eng != st.eng:
            return True
        if st.dma:
            return True
        return d.i in st.raw and st.eng in ("vector", "scalar", "gpsimd")

    def finalize(self, nc, sems):
        consumed = [False] * len(self.steps)
        for st in self.steps:
            for d in st.deps:
                if d.dma:
                    consumed[d.i] = True
                elif d.phase != st.phase:
                    pass
                elif d.eng != st.eng or st.dma or (
                    d.i in st.raw and st.eng in ("vector", "scalar", "gpsimd")
                ):
                    consumed[d.i] = True
        cnt = {name: 0 for name in sems}
        qrr = {e: 0 for e in QSEM}
        for st in self.steps:
            if st.dma:
                lane = QLANES[st.eng][qrr[st.eng] % NLANES]
                qrr[st.eng] += 1
                cnt[lane] += 16
                st.inc = (lane, cnt[lane])
            elif consumed[st.i]:
                s = ESEM[st.eng]
                cnt[s] += 1
                st.inc = (s, cnt[s])
        nph = self.phase + 1
        by_phase = [[] for _ in range(nph)]
        for st in self.steps:
            by_phase[st.phase].append(st)
        for ph in range(nph):
            plist = by_phase[ph]
            if not plist:
                continue
            with nc.Block() as blk:
                for eng in ENGS:
                    mine = [st for st in plist if st.eng == eng]
                    if not mine:
                        continue

                    def make(mine=mine, eng=eng):
                        def f(e):
                            seen = {}
                            for st in mine:
                                for d in st.deps:
                                    if not self._needs_wait(st, d):
                                        continue
                                    sname, val = d.inc
                                    if seen.get(sname, -1) >= val:
                                        continue
                                    e.wait_ge(sems[sname], val)
                                    seen[sname] = val
                                if st.dma:
                                    lname, lval = st.inc
                                    if lval > 16 and seen.get(lname, -1) < lval - 16:
                                        e.wait_ge(sems[lname], lval - 16)
                                        seen[lname] = lval - 16
                                ins = st.emit(e)
                                if st.inc is not None:
                                    amt = 16 if st.dma else 1
                                    ins.then_inc(sems[st.inc[0]], amt)

                        return f

                    getattr(blk, eng)(make())
        return cnt



NSIN = 4352

def _sinusoid_fm(parity):
    LAM = 10000.0
    pos = np.arange(T, dtype=np.float32)
    inv_lams = (1.0 / LAM ** (np.arange(0, D, 2, dtype=np.float32) / D)).astype(np.float32)
    pre = pos[:, None] * inv_lams[None, :]
    cat = np.concatenate([np.sin(pre), np.cos(pre)], axis=-1).astype(np.float32)
    S = np.flip(cat, axis=0)
    rolled = np.roll(S, 128 * parity, axis=0)
    table = np.concatenate([rolled, rolled[:NSIN - T]], axis=0)
    return np.ascontiguousarray(table.T).reshape(8, 128, NSIN).astype(ml_dtypes.bfloat16)


def _mask256(parity):
    pp = np.arange(128)[:, None]
    cc = np.arange(256)[None, :]
    keep = cc <= pp + 128 * parity
    return np.where(keep, np.float32(0.0), np.float32(NEG)).astype(np.float32)



def build_program():
    nc = bass.Bass("TRN2", target_bir_lowering=False)

    x_full = nc.dram_tensor("x_full", [T, D], F32, kind="ExternalInput").ap()
    x_own = nc.dram_tensor("x_own", [NQ * 128, D], F32, kind="ExternalInput").ap()
    wq = nc.dram_tensor("wq", [D, KD], F32, kind="ExternalInput").ap()
    wk = nc.dram_tensor("wk", [D, KD], F32, kind="ExternalInput").ap()
    wr = nc.dram_tensor("wr", [D, KD], F32, kind="ExternalInput").ap()
    wv = nc.dram_tensor("wv", [D, V], F32, kind="ExternalInput").ap()
    wg = nc.dram_tensor("wg", [D, V], F32, kind="ExternalInput").ap()
    wo = nc.dram_tensor("wo", [V, D], F32, kind="ExternalInput").ap()
    w1 = nc.dram_tensor("w1", [D, HD], F32, kind="ExternalInput").ap()
    w2 = nc.dram_tensor("w2", [HD, D], F32, kind="ExternalInput").ap()
    g_lnL = nc.dram_tensor("g_lnL", [128, 8], F32, kind="ExternalInput").ap()
    xl_u = nc.dram_tensor("xl_u", [128, 1], F32, kind="ExternalInput").ap()
    xl_v = nc.dram_tensor("xl_v", [128, 1], F32, kind="ExternalInput").ap()
    r1a1 = nc.dram_tensor("r1a1", [D, 32], F32, kind="ExternalInput").ap()
    b1f = nc.dram_tensor("b1f", [24, HD], F32, kind="ExternalInput").ap()
    b1L = nc.dram_tensor("b1L", [128, 32], F32, kind="ExternalInput").ap()
    r2a2 = nc.dram_tensor("r2a2", [HD, 32], F32, kind="ExternalInput").ap()
    b2f = nc.dram_tensor("b2f", [24, D], F32, kind="ExternalInput").ap()
    b2L = nc.dram_tensor("b2L", [1, D], F32, kind="ExternalInput").ap()
    sin_fm = nc.dram_tensor("sin_fm", [8, 128, NSIN], BF16, kind="ExternalInput").ap()
    mask256 = nc.dram_tensor("mask256", [128, 256], F32, kind="ExternalInput").ap()

    out_own = nc.dram_tensor("out_own", [NQ * 128, D], F32, kind="ExternalOutput").ap()

    ident = nc.inline_tensor(np.eye(128, dtype=ml_dtypes.bfloat16), name="ident").ap()
    ones1 = nc.inline_tensor(np.ones((1, 128), dtype=ml_dtypes.bfloat16), name="ones1").ap()

    g_park = nc.dram_tensor("g_park", [NQ * 128, V], BF16, kind="Internal").ap()
    z_park = nc.dram_tensor("z_park", [NQ * 128, V], BF16, kind="Internal").ap()
    z_park2 = nc.dram_tensor("z_park2", [NQ * 128, V], BF16, kind="Internal").ap()
    xnew_d = nc.dram_tensor("xnew_d", [NQ * 128, D], F32, kind="Internal").ap()
    wo_st = nc.dram_tensor("wo_st", [V, D], BF16, kind="Internal").ap()
    w1_st = nc.dram_tensor("w1_st", [D, HD], BF16, kind="Internal").ap()
    w2_st = nc.dram_tensor("w2_st", [HD, D], BF16, kind="Internal").ap()
    bounce = nc.dram_tensor("bounce", [129 * PW], F32, kind="Internal")
    bwr = bounce.ap()[0 : 128 * PW].rearrange("(p w) -> p w", w=PW)
    brd = bounce.ap()[0 : 129 * PR].rearrange("(p w) -> p w", w=PR)

    dbg = {}
    if DEBUG:
        dbg["xt"] = nc.dram_tensor("dbg_xt", [T, D], BF16, kind="ExternalOutput").ap()
        dbg["k"] = nc.dram_tensor("dbg_k", [128, T], BF16, kind="ExternalOutput").ap()
        dbg["g"] = nc.dram_tensor("dbg_g", [NQ * 128, V], BF16, kind="ExternalOutput").ap()
        dbg["qu"] = nc.dram_tensor("dbg_qu", [128, NQ * 128], BF16, kind="ExternalOutput").ap()
        dbg["qv"] = nc.dram_tensor("dbg_qv", [128, NQ * 128], BF16, kind="ExternalOutput").ap()
        dbg["xlr"] = nc.dram_tensor("dbg_xlr", [128, NSIN], BF16, kind="ExternalOutput").ap()
        dbg["xnew"] = nc.dram_tensor("dbg_xnew", [NQ * 128, D], F32, kind="ExternalOutput").ap()
        dbg["z"] = nc.dram_tensor("dbg_z", [NQ * 128, V], BF16, kind="ExternalOutput").ap()
        dbg["z2"] = nc.dram_tensor("dbg_z2", [NQ * 128, V], BF16, kind="ExternalOutput").ap()
        dbg["a15"] = nc.dram_tensor("dbg_a15", [128, T], F32, kind="ExternalOutput").ap()
        dbg["c15"] = nc.dram_tensor("dbg_c15", [128, T], BF16, kind="ExternalOutput").ap()
        dbg["bb15"] = nc.dram_tensor("dbg_bb15", [128, T], F32, kind="ExternalOutput").ap()
        dbg["px15"] = nc.dram_tensor("dbg_px15", [128, D], BF16, kind="ExternalOutput").ap()
        dbg["pxf0"] = nc.dram_tensor("dbg_pxf0", [128, D], BF16, kind="ExternalOutput").ap()
        dbg["gt0"] = nc.dram_tensor("dbg_gt0", [128, V], BF16, kind="ExternalOutput").ap()
        dbg["wv0"] = nc.dram_tensor("dbg_wv0", [128, V], F32, kind="ExternalOutput").ap()

    p = Prog()
    ctx = contextlib.ExitStack()
    sems = {n: ctx.enter_context(nc.semaphore(n)) for n in ALL_SEMS}

    ident_s = ctx.enter_context(nc.sbuf_tensor("ident_s", [128, 128], BF16))
    mask_s = ctx.enter_context(nc.sbuf_tensor("mask_s", [128, 256], F32))
    g_sb = ctx.enter_context(nc.sbuf_tensor("g_sb", [128, 8], F32))
    xlu_s = ctx.enter_context(nc.sbuf_tensor("xlu_s", [128, 1], F32))
    xlv_s = ctx.enter_context(nc.sbuf_tensor("xlv_s", [128, 1], F32))
    CONSTS = [EPS1 * EPS2, (1.0 + EPS2) / D, 1e-5, 1.0 / D, -0.5, 1.0 / TAU]
    consts_sb = ctx.enter_context(nc.sbuf_tensor("consts_sb", [128, len(CONSTS)], F32))
    for j, val in enumerate(CONSTS):
        nc.const_aps.register(F32, float(val), consts_sb[:, j : j + 1])

    bctx = contextlib.ExitStack()
    xt_tm = bctx.enter_context(nc.sbuf_tensor("xt_tm", [128, NT, D], BF16))
    k_fm = bctx.enter_context(nc.sbuf_tensor("k_fm", [128, T], BF16))
    qu_fm = bctx.enter_context(nc.sbuf_tensor("qu_fm", [128, NQ * 128], BF16))
    qv_fm = bctx.enter_context(nc.sbuf_tensor("qv_fm", [128, NQ * 128], BF16))
    xlr = bctx.enter_context(nc.sbuf_tensor("xlr", [128, NSIN], BF16))
    wv_b = bctx.enter_context(nc.sbuf_tensor("wv_b", [128, 8, V], BF16))
    wstg = bctx.enter_context(nc.sbuf_tensor("wstg", [128, V], BF16))
    wq_b = bctx.enter_context(nc.sbuf_tensor("wq_b", [128, 8, KD], BF16))
    wk_b = bctx.enter_context(nc.sbuf_tensor("wk_b", [128, 8, KD], BF16))

    b_misc = Buf()
    b_ident, b_mask, b_zeros, b_gsb, b_xlu, b_xlv, b_ro, b_consts = (
        Buf(), Buf(), Buf(), Buf(), Buf(), Buf(), Buf(), Buf())
    b_wq, b_wk, b_wv, b_wr, b_wg = Buf(), Buf(), Buf(), Buf(), Buf()
    b_xt = [Buf() for _ in range(NT)]
    b_kfm = [Buf() for _ in range(8)]
    b_qu = [Buf() for _ in range(4)]
    b_xlr = Buf()
    b_gpark = [Buf() for _ in range(NQ)]
    b_zpark = [Buf() for _ in range(NQ)]
    b_xnew = [Buf() for _ in range(NQ)]

    for dst, src, bf_ in ((ident_s[:], ident, b_ident), (mask_s[:], mask256, b_mask),
                          (g_sb[:], g_lnL, b_gsb), (xlu_s[:], xl_u, b_xlu),
                          (xlv_s[:], xl_v, b_xlv)):
        p.dma("sync", dst, src, writes=(bf_,), tag="p0")
    for j, val in enumerate(CONSTS):
        p.step("vector", lambda e, j=j, val=val: e.memset(consts_sb[:, j : j + 1], float(val)),
               writes=(b_consts,), tag=f"p0c{j}")

    p.next_phase()


    with nc.sbuf_tensor("x4", [128, 2, 4, D], F32) as x4, \
         nc.sbuf_tensor("xfm4", [128, 2, 8, 512], BF16) as xfm4, \
         nc.sbuf_tensor("lnsc", [128, 4, 4], F32) as lnsc, \
         nc.sbuf_tensor("junk", [128, D], BF16) as junk, \
         nc.sbuf_tensor("wr_b", [128, 8, KD], BF16) as wr_b, \
         nc.sbuf_tensor("sin_sb", [128, 2, T], BF16) as sin_sb, \
         nc.sbuf_tensor("sin_rem", [128, 8, NSIN - T], BF16) as sin_rem, \
         nc.sbuf_tensor("_p1a_end", [1, 4], F32) as _p1a_end:

        b_x4 = [Buf(), Buf()]
        b_xfm4 = [Buf(), Buf()]
        b_pstr1 = [Buf(), Buf()]
        b_psqk = [Buf(), Buf()]
        b_lnj = [Buf() for _ in range(4)]
        b_junk = Buf()
        b_sin = [Buf(), Buf()]
        b_sinr = Buf()
        b_psxl = [Buf(), Buf()]

        p.dma("gpsimd", wr_b[:], wr.rearrange("(c p) f -> p c f", p=128), writes=(b_wr,), tag="ldwr")
        b_wstg = Buf()
        p.dma("gpsimd", wstg[:, 0 : 8 * KD].rearrange("p (c f) -> p c f", f=KD),
              wq.rearrange("(c p) f -> p c f", p=128), writes=(b_wstg,), tag="ldwqr")
        for c in range(8):
            p.step("vector", lambda e, c=c: e.tensor_scalar_mul(
                wq_b[:, c, :], wstg[:, c * KD : (c + 1) * KD], g_sb[:, c : c + 1]),
                reads=(b_wstg, b_gsb), writes=(b_wq,), tag=f"foldq{c}")
        p.dma("gpsimd", wstg[:, 8 * KD : 16 * KD].rearrange("p (c f) -> p c f", f=KD),
              wk.rearrange("(c p) f -> p c f", p=128), writes=(b_wstg,), tag="ldwkr")
        for c in range(8):
            p.step("vector", lambda e, c=c: e.tensor_scalar_mul(
                wk_b[:, c, :], wstg[:, (8 + c) * KD : (9 + c) * KD], g_sb[:, c : c + 1]),
                reads=(b_wstg, b_gsb), writes=(b_wk,), tag=f"foldk{c}")
        with nc.psum_tensor("ps_xlr8", [128, 8, 512], F32) as ps_xlr8:
            b_psxl = Buf()
            for c in range(8):
                cb = c % 2
                p.dma("scalar", sin_sb[:, cb], sin_fm[c, :, 0:T], writes=(b_sin[cb],), tag=f"ldsin{c}")
                p.dma("scalar", sin_rem[:, c, :], sin_fm[c, :, T:NSIN], writes=(b_sinr,), tag=f"ldsinr{c}")
                for n in range(8):
                    p.step("tensor", lambda e, c=c, n=n, cb=cb: e.matmul(
                        ps_xlr8[:, n], wr_b[:, c, :], sin_sb[:, cb, n * 512 : (n + 1) * 512],
                        start=(c == 0), stop=(c == 7), skip_group_check=True),
                        reads=(b_wr, b_sin[cb]), writes=(b_psxl,), tag=f"xlrmm{c}.{n}")
            for n in range(8):
                p.step("scalar", lambda e, n=n: e.activation(
                    out=xlr[:, n * 512 : (n + 1) * 512], in_=ps_xlr8[:, n], func=AF.Identity),
                    reads=(b_psxl,), writes=(b_xlr,), tag=f"xlrev{n}")
            for c in range(8):
                p.step("tensor", lambda e, c=c: e.matmul(
                    ps_xlr8[:, 0, 0 : NSIN - T], wr_b[:, c, :], sin_rem[:, c, :],
                    start=(c == 0), stop=(c == 7), skip_group_check=True),
                    reads=(b_wr, b_sinr), writes=(b_psxl,), tag=f"xlrmr{c}")
            p.step("scalar", lambda e: e.activation(
                out=xlr[:, T:NSIN], in_=ps_xlr8[:, 0, 0 : NSIN - T], func=AF.Identity),
                reads=(b_psxl,), writes=(b_xlr,), tag="xlrevr")

        p1actx = contextlib.ExitStack()
        ps_tr1 = p1actx.enter_context(nc.psum_tensor("ps_tr1", [128, 2, 8, 128], BF16))
        ps_qk = p1actx.enter_context(nc.psum_tensor("ps_qk", [128, 2, 512], F32))

        for g in range(8):
            gb = g % 2
            src0 = x_full[g * 512 : (g + 1) * 512, :].rearrange("(j p) d -> p j d", p=128)
            p.dma("sync", x4[:, gb], src0, writes=(b_x4[gb],), tag=f"ldx{g}")
            for j in range(4):
                p.step("vector", lambda e, gb=gb, j=j: e.reduce_sum(
                    lnsc[:, j, 0:1], x4[:, gb, j, :], axis=AX.X),
                    reads=(b_x4[gb],), writes=(b_lnj[j],), tag=f"sum{g}.{j}")
            for j in range(4):
                p.step("vector", lambda e, j=j: e.tensor_scalar_mul(
                    lnsc[:, j, 1:2], lnsc[:, j, 0:1], -1.0 / D),
                    reads=(b_lnj[j],), writes=(b_lnj[j],), tag=f"negm{g}.{j}")
            for j in range(4):
                p.step("scalar", lambda e, gb=gb, j=j: e.activation(
                    out=junk[:], in_=x4[:, gb, j, :], func=AF.Square, bias=lnsc[:, j, 1:2],
                    scale=1.0, accum_out=lnsc[:, j, 2:3]),
                    reads=(b_x4[gb], b_lnj[j]), writes=(b_lnj[j], b_junk), tag=f"sq{g}.{j}")
            for j in range(4):
                p.step("scalar", lambda e, j=j: e.activation(
                    out=lnsc[:, j, 3:4], in_=lnsc[:, j, 2:3], func=AF.Ln,
                    bias=EPS1 * EPS2, scale=(1.0 + EPS2) / D),
                    reads=(b_lnj[j], b_consts), writes=(b_lnj[j],), tag=f"ln{g}.{j}")
            for j in range(4):
                p.step("scalar", lambda e, j=j: e.activation(
                    out=lnsc[:, j, 3:4], in_=lnsc[:, j, 3:4], func=AF.Exp, scale=-0.5),
                    reads=(b_lnj[j], b_consts), writes=(b_lnj[j],), tag=f"exp{g}.{j}")
            for j in range(4):
                p.step("vector", lambda e, j=j: e.tensor_mul(
                    lnsc[:, j, 1:2], lnsc[:, j, 1:2], lnsc[:, j, 3:4]),
                    reads=(b_lnj[j],), writes=(b_lnj[j],), tag=f"negms{g}.{j}")
            for j in range(4):
                t = 4 * g + j
                p.step("scalar", lambda e, gb=gb, j=j, t=t: e.activation(
                    out=xt_tm[:, t, :], in_=x4[:, gb, j, :], func=AF.Identity,
                    bias=lnsc[:, j, 1:2], scale=lnsc[:, j, 3:4]),
                    reads=(b_lnj[j], b_x4[gb]), writes=(b_xt[t],), tag=f"xt{t}")
            for j in range(4):
                t = 4 * g + j
                for c in range(8):
                    p.step("tensor", lambda e, t=t, c=c, gb=gb: e.transpose(
                        ps_tr1[:, gb, c, :], xt_tm[:, t, c * 128 : (c + 1) * 128], ident_s[:]),
                        reads=(b_xt[t], b_ident), writes=(b_pstr1[gb],), tag=f"tr{t}.{c}")
                p.step("vector", lambda e, gb=gb, j=j: e.tensor_copy(
                    xfm4[:, gb, :, j * 128 : (j + 1) * 128], ps_tr1[:, gb]),
                    reads=(b_pstr1[gb],), writes=(b_xfm4[gb],), tag=f"trev{t}")
            for c in range(8):
                p.step("tensor", lambda e, c=c, gb=gb: e.matmul(
                    ps_qk[:, gb], wk_b[:, c, :], xfm4[:, gb, c, :],
                    start=(c == 0), stop=(c == 7), skip_group_check=True),
                    reads=(b_wk, b_xfm4[gb]), writes=(b_psqk[gb],), tag=f"kmm{g}.{c}")
            p.step("scalar", lambda e, g=g, gb=gb: e.activation(
                out=k_fm[:, g * 512 : (g + 1) * 512], in_=ps_qk[:, gb], func=AF.Identity),
                reads=(b_psqk[gb],), writes=(b_kfm[g],), tag=f"kev{g}")

        for c in range(8):
            p.dma("gpsimd", wstg[:, 0:V], wv[c * 128 : (c + 1) * 128, :],
                  writes=(b_wstg,), tag=f"ldwv{c}")
            p.step("vector", lambda e, c=c: e.tensor_scalar_mul(
                wv_b[:, c, :], wstg[:, 0:V], g_sb[:, c : c + 1]),
                reads=(b_wstg, b_gsb), writes=(b_wv,), tag=f"foldv{c}")

        if DEBUG:
            for t in range(NT):
                p.dma("scalar", dbg["xt"][t * 128 : (t + 1) * 128, :], xt_tm[:, t, :], reads=(b_xt[t],), tag="dbgxt")
            for g in range(8):
                p.dma("scalar", dbg["k"][:, g * 512 : (g + 1) * 512], k_fm[:, g * 512 : (g + 1) * 512],
                      reads=(b_kfm[g],), tag="dbgk")
            p.dma("scalar", dbg["xlr"], xlr[:], reads=(b_xlr,), tag="dbgxlr")
        p1actx.close()

    p.next_phase()
    p.fence()
    with nc.sbuf_tensor("wg_b", [128, 8, V], BF16) as wg_b, \
         nc.sbuf_tensor("xo4", [128, 4, D], F32) as xo4, \
         nc.sbuf_tensor("xof4", [128, 2, 8, 512], BF16) as xof4, \
         nc.sbuf_tensor("xob", [128, 2, D], BF16) as xob, \
         nc.sbuf_tensor("gt", [128, 2, V], BF16) as gt, \
         nc.sbuf_tensor("lnst", [128, NQ, 2], F32) as lnst, \
         nc.sbuf_tensor("lnsg", [128, NQ, 2], F32) as lnsg, \
         nc.psum_tensor("ps_tr2", [128, 2, 8, 128], BF16) as ps_tr2, \
         nc.psum_tensor("ps_g", [128, 2, 512], F32) as ps_g, \
         nc.psum_tensor("ps_q", [128, 2, 512], F32) as ps_q:

        b_wstg2 = Buf()
        for c in range(8):
            p.dma("gpsimd", wstg[:, 0:V], wg[c * 128 : (c + 1) * 128, :],
                  writes=(b_wstg2,), tag=f"ldwg{c}")
            p.step("vector", lambda e, c=c: e.tensor_scalar_mul(
                wg_b[:, c, :], wstg[:, 0:V], g_sb[:, c : c + 1]),
                reads=(b_wstg2, b_gsb), writes=(b_wg,), tag=f"foldg{c}")

        b_xo = [Buf()]
        b_xof = [Buf(), Buf()]
        b_xob = [Buf(), Buf()]
        b_gt = [Buf(), Buf()]
        b_pstr2 = [Buf(), Buf()]
        b_psg = [Buf(), Buf()]
        b_psq = [Buf(), Buf()]
        b_lnk = [Buf() for _ in range(NQ)]
        b_lns = Buf()

        for g2 in range(4):
            src2 = x_own[g2 * 512 : (g2 + 1) * 512, :].rearrange("(j p) d -> p j d", p=128)
            p.dma("sync", xo4[:], src2, writes=(b_xo[0],), tag=f"ldxo{g2}")
            for j in range(4):
                k = 4 * g2 + j
                p.step("vector", lambda e, j=j, k=k: e.reduce_sum(
                    lnst[:, k, 0:1], xo4[:, j, :], axis=AX.X),
                    reads=(b_xo[0],), writes=(b_lnk[k],), tag=f"bsum{k}")
            for j in range(4):
                k = 4 * g2 + j
                p.step("vector", lambda e, k=k: e.tensor_scalar_mul(
                    lnst[:, k, 0:1], lnst[:, k, 0:1], -1.0 / D),
                    reads=(b_lnk[k],), writes=(b_lnk[k],), tag=f"bnegm{k}")
            for j in range(4):
                k = 4 * g2 + j
                p.step("scalar", lambda e, j=j, k=k: e.activation(
                    out=xob[:, 0], in_=xo4[:, j, :], func=AF.Square, bias=lnst[:, k, 0:1],
                    scale=1.0, accum_out=lnst[:, k, 1:2]),
                    reads=(b_xo[0], b_lnk[k]), writes=(b_lnk[k], b_xob[0]), tag=f"bsq{k}")
        p.step("scalar", lambda e: e.activation(
            out=lnsg[:, :, 1], in_=lnst[:, :, 1], func=AF.Ln, bias=EPS1 * EPS2,
            scale=(1.0 + EPS2) / D),
            reads=(b_consts,), deps=[b.w for b in b_lnk], writes=(b_lns,), tag="bln")
        p.step("scalar", lambda e: e.activation(
            out=lnsg[:, :, 1], in_=lnsg[:, :, 1], func=AF.Exp, scale=-0.5),
            reads=(b_lns, b_consts), writes=(b_lns,), tag="bexp")
        p.step("vector", lambda e: e.tensor_mul(lnsg[:, :, 0], lnst[:, :, 0], lnsg[:, :, 1]),
               reads=(b_lns,), deps=[b.w for b in b_lnk], writes=(b_lns,), tag="bnegms")

        def p1b_apply(k):
            g2, j = k // 4, k % 4
            kb2 = k % 2
            if j == 0:
                src2 = x_own[g2 * 512 : (g2 + 1) * 512, :].rearrange("(j p) d -> p j d", p=128)
                p.dma("sync", xo4[:], src2, writes=(b_xo[0],), tag=f"ldxo2_{g2}")
            p.step("scalar", lambda e, j=j, k=k, kb2=kb2: e.activation(
                out=xob[:, kb2], in_=xo4[:, j, :], func=AF.Identity, bias=lnsg[:, k, 0:1],
                scale=lnsg[:, k, 1:2]),
                reads=(b_lns, b_xo[0]), writes=(b_xob[kb2],), tag=f"bxt{k}")

        def p1b_tr(k):
            g2, j = k // 4, k % 4
            kb2 = k % 2
            gb = g2 % 2
            for c in range(8):
                p.step("tensor", lambda e, gb=gb, c=c, kb2=kb2: e.transpose(
                    ps_tr2[:, gb, c, :], xob[:, kb2, c * 128 : (c + 1) * 128], ident_s[:]),
                    reads=(b_xob[kb2], b_ident), writes=(b_pstr2[gb],), tag=f"btr{k}.{c}")
            p.step("vector", lambda e, gb=gb, j=j: e.tensor_copy(
                xof4[:, gb, :, j * 128 : (j + 1) * 128], ps_tr2[:, gb]),
                reads=(b_pstr2[gb],), writes=(b_xof[gb],), tag=f"btrev{k}")

        p1b_apply(0)
        p1b_tr(0)
        for k in range(NQ):
            g2, j = k // 4, k % 4
            gb = g2 % 2
            if k + 1 < NQ:
                p1b_apply(k + 1)
            for n in range(4):
                nb = n % 2
                for c in range(8):
                    p.step("tensor", lambda e, n=n, c=c, gb=gb, j=j, nb=nb: e.matmul(
                        ps_g[:, nb], xof4[:, gb, c, j * 128 : (j + 1) * 128],
                        wg_b[:, c, n * 512 : (n + 1) * 512],
                        start=(c == 0), stop=(c == 7), skip_group_check=True),
                        reads=(b_wg, b_xof[gb]), writes=(b_psg[nb],), tag=f"gmm{k}.{n}.{c}")
                p.step("scalar", lambda e, n=n, gb=gb, nb=nb: e.activation(
                    out=gt[:, gb, n * 512 : (n + 1) * 512], in_=ps_g[:, nb], func=AF.Silu),
                    reads=(b_psg[nb],), writes=(b_gt[gb],), tag=f"gev{k}.{n}")
            if k + 1 < NQ:
                p1b_tr(k + 1)
            p.dma("scalar", g_park[k * 128 : (k + 1) * 128, :], gt[:, gb], reads=(b_gt[gb],),
                  writes=(b_gpark[k],), tag=f"gpark{k}")
            if k % 4 == 3:
                for c in range(8):
                    p.step("tensor", lambda e, c=c, gb=gb: e.matmul(
                        ps_q[:, gb], wq_b[:, c, :], xof4[:, gb, c, :], start=(c == 0), stop=(c == 7),
                        skip_group_check=True),
                        reads=(b_wq, b_xof[gb]), writes=(b_psq[gb],), tag=f"qmm{g2}.{c}")
                p.step("scalar", lambda e, g2=g2, gb=gb: e.activation(
                    out=qu_fm[:, g2 * 512 : (g2 + 1) * 512], in_=ps_q[:, gb], func=AF.Identity,
                    bias=xlu_s[:], scale=1.0 / TAU),
                    reads=(b_psq[gb], b_xlu, b_consts), writes=(b_qu[g2],), tag=f"quev{g2}")
                p.step("scalar", lambda e, g2=g2, gb=gb: e.activation(
                    out=qv_fm[:, g2 * 512 : (g2 + 1) * 512], in_=ps_q[:, gb], func=AF.Identity,
                    bias=xlv_s[:], scale=1.0 / TAU),
                    reads=(b_psq[gb], b_xlv, b_consts), writes=(b_qu[g2],), tag=f"qvev{g2}")

        if DEBUG:
            for k in range(NQ):
                p.dma("scalar", dbg["g"][k * 128 : (k + 1) * 128, :], g_park[k * 128 : (k + 1) * 128, :],
                      deps=(b_gpark[k].w,), tag="dbgg")
            p.dma("scalar", dbg["qu"], qu_fm[:], deps=[b.w for b in b_qu[:4]], tag="dbgqu")
            p.dma("scalar", dbg["qv"], qv_fm[:], deps=[b.w for b in b_qu[:4]], tag="dbgqv")

    p.next_phase()
    p.fence()
    b_wost, b_w1st, b_w2st = Buf(), Buf(), Buf()
    p.dma("gpsimd", wo_st, wo, writes=(b_wost,), tag="stgwo")
    p.dma("gpsimd", w1_st, w1, writes=(b_w1st,), tag="stgw1")
    p.dma("gpsimd", w2_st, w2, writes=(b_w2st,), tag="stgw2")
    with nc.sbuf_tensor("dst_s", [128, 4224], F32) as dst_s, \
         nc.sbuf_tensor("a_s", [128, T], F32) as a_s, \
         nc.sbuf_tensor("bb_s", [128, T], F32) as bb_s, \
         nc.sbuf_tensor("mb_s", [128, 256], F32) as mb_s, \
         nc.sbuf_tensor("c_s", [128, T], BF16) as c_s, \
         nc.sbuf_tensor("pt_s", [128, 32, 128], BF16) as pt_s, \
         nc.sbuf_tensor("pxn_s", [128, D], BF16) as pxn_s, \
         nc.sbuf_tensor("pxf_s", [128, 8, 128], BF16) as pxf_s, \
         nc.sbuf_tensor("gt2_s", [128, V], BF16) as gt2_s, \
         nc.sbuf_tensor("zt_s", [128, V], BF16) as zt_s, \
         nc.sbuf_tensor("sm_s", [128, 4], F32) as sm_s, \
         nc.psum_tensor("ps_ba", [128, 2, 512], F32) as ps_ba, \
         nc.psum_tensor("ps_px", [128, 2, 512], F32) as ps_px, \
         nc.psum_tensor("ps_wv", [128, 4, 512], F32) as ps_wv:

        b_dst = Buf()
        b_a = Buf()
        b_bb = Buf()
        b_mb = Buf()
        b_c = Buf()
        b_pt = Buf()
        b_pxn = Buf()
        b_pxf = Buf()
        b_gt2 = Buf()
        b_zt = Buf()
        b_sm = Buf()
        b_psba = [Buf(), Buf()]
        b_pspx = Buf()
        b_pswv = Buf()
        b_bounce = Buf()

        def bd_block(kk):
            k = kk + 1
            Wb = 256 * k + 128
            Wq_ = 256 * k
            ce_lo = 4224 - 256 * k
            nbd = (Wb + 511) // 512
            for m in range(nbd):
                n0 = m * 512
                n1 = min(n0 + 512, Wb)
                mb2 = m % 2
                p.step("tensor", lambda e, kk=kk, mb2=mb2, n0=n0, n1=n1, ce_lo=ce_lo: e.matmul(
                    ps_ba[:, mb2, 0 : n1 - n0], qv_fm[:, kk * 128 : (kk + 1) * 128],
                    xlr[:, ce_lo + n0 : ce_lo + n1], start=True, stop=True,
                    skip_group_check=True),
                    reads=(b_qu[kk // 4], b_xlr), writes=(b_psba[mb2],), tag=f"bdmm{kk}.{m}")
                p.step("scalar", lambda e, mb2=mb2, n0=n0, n1=n1: e.activation(
                    out=dst_s[:, n0:n1], in_=ps_ba[:, mb2, 0 : n1 - n0], func=AF.Identity),
                    reads=(b_psba[mb2],), writes=(b_dst,), tag=f"bdev{kk}.{m}")
            p.dma("sync", bwr[:, 128 : 128 + Wb], dst_s[:, 0:Wb], reads=(b_dst,), writes=(b_bounce,), tag=f"bwr{kk}")
            p.dma("sync", lambda Wq_=Wq_: bb_s[:, 0:Wq_],
                  lambda Wq_=Wq_: brd[0:128, 255 : 255 + Wq_],
                  reads=(b_bounce,), writes=(b_bb,), tag=f"bdiag{kk}")

        def wv_mms(kk):
            for c in range(8):
                for n2 in range(4):
                    p.step("tensor", lambda e, kk=kk, c=c, n2=n2: e.matmul(
                        ps_wv[:, n2, :], pxf_s[:, c, :], wv_b[:, c, n2 * 512 : (n2 + 1) * 512],
                        start=(c == 0), stop=(c == 7), skip_group_check=True),
                        reads=(b_pxf, b_wv), writes=(b_pswv,), tag=f"wvmm{kk}.{c}.{n2}")

        def wv_gate(kk):
            p.step("vector", lambda e: e.tensor_mul(
                zt_s[:], ps_wv[:, :, :].rearrange("p a b -> p (a b)"), gt2_s[:]),
                reads=(b_pswv, b_gt2), writes=(b_zt,), tag=f"gate{kk}")
            p.dma("scalar", z_park[kk * 128 : (kk + 1) * 128, :], zt_s[:],
                  reads=(b_zt,), writes=(b_zpark[kk],), tag=f"zpark{kk}")

        bd_block(0)

        for kk in range(NQ):
            k = kk + 1
            Wq_ = 256 * k
            kb = kk % 2
            nch = 2 * k
            nac = (Wq_ + 511) // 512
            p.step("vector", lambda e, Wq_=Wq_: e.tensor_add(
                mb_s[:], bb_s[:, Wq_ - 256 : Wq_], mask_s[:]),
                reads=(b_bb, b_mask), writes=(b_mb,), tag=f"mbadd{kk}")
            for m in range(nac):
                n0 = m * 512
                n1 = min(n0 + 512, Wq_)
                mb2 = m % 2
                p.step("tensor", lambda e, kk=kk, mb2=mb2, n0=n0, n1=n1: e.matmul(
                    ps_ba[:, mb2, 0 : n1 - n0], qu_fm[:, kk * 128 : (kk + 1) * 128],
                    k_fm[:, n0:n1], start=True, stop=True, skip_group_check=True),
                    reads=(b_qu[kk // 4], b_kfm[n0 // 512]), writes=(b_psba[mb2],), tag=f"acmm{kk}.{m}")
                if n1 <= Wq_ - 256:
                    p.step("vector", lambda e, mb2=mb2, n0=n0, n1=n1: e.tensor_add(
                        a_s[:, n0:n1], ps_ba[:, mb2, 0 : n1 - n0], bb_s[:, n0:n1]),
                        reads=(b_psba[mb2], b_bb), writes=(b_a,), tag=f"acev{kk}.{m}")
                else:
                    lo = n0
                    mid = max(n0, Wq_ - 256)
                    if mid > lo:
                        p.step("vector", lambda e, mb2=mb2, lo=lo, mid=mid, n0=n0: e.tensor_add(
                            a_s[:, lo:mid], ps_ba[:, mb2, lo - n0 : mid - n0], bb_s[:, lo:mid]),
                            reads=(b_psba[mb2], b_bb), writes=(b_a,), tag=f"acevl{kk}.{m}")
                    p.step("vector", lambda e, mb2=mb2, mid=mid, n0=n0, n1=n1, Wq_=Wq_: e.tensor_add(
                        a_s[:, mid:n1], ps_ba[:, mb2, mid - n0 : n1 - n0],
                        mb_s[:, mid - (Wq_ - 256) : n1 - (Wq_ - 256)]),
                        reads=(b_psba[mb2], b_mb), writes=(b_a,), tag=f"acevm{kk}.{m}")
            if DEBUG and kk == 5:
                p.dma("sync", dbg["bb15"], bb_s[:], reads=(b_bb,), tag="dbgbb")
                p.dma("sync", dbg["a15"], a_s[:], reads=(b_a,), tag="dbga")
            if kk + 1 < NQ:
                bd_block(kk + 1)
            if kk > 0:
                wv_mms(kk - 1)
            p.step("vector", lambda e, Wq_=Wq_: e.tensor_reduce(
                sm_s[:, 0:1], a_s[:, 0:Wq_], axis=AX.X, op=ALU.max, negate=True),
                reads=(b_a,), writes=(b_sm,), tag=f"max{kk}")
            p.step("scalar", lambda e, Wq_=Wq_: e.activation(
                out=c_s[:, 0:Wq_], in_=a_s[:, 0:Wq_], func=AF.Exp, bias=sm_s[:, 0:1],
                accum_out=sm_s[:, 1:2]),
                reads=(b_a, b_sm), writes=(b_c, b_sm), tag=f"exp{kk}")
            p.step("vector", lambda e: e.reciprocal(sm_s[:, 2:3], sm_s[:, 1:2]),
                   reads=(b_sm,), writes=(b_sm,), tag=f"rcp{kk}")
            if kk > 0:
                wv_gate(kk - 1)
            p.dma("scalar", gt2_s[:], g_park[kk * 128 : (kk + 1) * 128, :],
                  deps=(b_gpark[kk].w,), writes=(b_gt2,), tag=f"gld{kk}")
            nrounds = (nch + 7) // 8
            for r in range(nrounds):
                c0 = r * 8
                c1 = min(c0 + 8, nch)
                rb = r % 2
                bview = ps_ba[:, rb, :].bitcast(BF16)
                for ci in range(c0, c1):
                    p.step("tensor", lambda e, ci=ci, c0=c0, rb=rb, bview=bview: e.transpose(
                        bview[:, (ci - c0) * 128 : (ci - c0 + 1) * 128],
                        c_s[:, ci * 128 : (ci + 1) * 128], ident_s[:]),
                        reads=(b_c, b_ident), writes=(b_psba[rb],), tag=f"ptt{kk}.{ci}")
                p.step("vector", lambda e, c0=c0, c1=c1, rb=rb, bview=bview: e.tensor_copy(
                    pt_s[:, c0:c1, :], bview[:, 0 : (c1 - c0) * 128]),
                    reads=(b_psba[rb],), writes=(b_pt,), tag=f"ptev{kk}.{r}")
            for ci in range(nch):
                for n2 in range(2):
                    p.step("tensor", lambda e, ci=ci, n2=n2, nch=nch: e.matmul(
                        ps_px[:, n2, :], pt_s[:, ci, :],
                        xt_tm[:, ci, n2 * 512 : (n2 + 1) * 512],
                        start=(ci == 0), stop=(ci == nch - 1), skip_group_check=True),
                        reads=(b_pt, b_xt[ci]), writes=(b_pspx,), tag=f"pxmm{kk}.{ci}.{n2}")
            if DEBUG and kk == 5:
                p.dma("sync", dbg["c15"], c_s[:], reads=(b_c,), tag="dbgc")
            p.step("scalar", lambda e: e.activation(
                out=pxn_s[:], in_=ps_px[:, :, :], func=AF.Identity, scale=sm_s[:, 2:3]),
                reads=(b_pspx, b_sm), writes=(b_pxn,), tag=f"pxn{kk}")
            if DEBUG and kk == 5:
                p.dma("sync", dbg["px15"], pxn_s[:], reads=(b_pxn,), tag="dbgpx")
            for hh in range(2):
                rb = hh % 2
                bview = ps_ba[:, rb, :].bitcast(BF16)
                for c2 in range(4):
                    c = hh * 4 + c2
                    p.step("tensor", lambda e, c=c, c2=c2, bview=bview: e.transpose(
                        bview[:, c2 * 128 : (c2 + 1) * 128], pxn_s[:, c * 128 : (c + 1) * 128], ident_s[:]),
                        reads=(b_pxn, b_ident), writes=(b_psba[rb],), tag=f"pxt{kk}.{c}")
                p.step("vector", lambda e, hh=hh, bview=bview, rb=rb: e.tensor_copy(
                    pxf_s[:, hh * 4 : (hh + 1) * 4, :], bview[:, 0:512]),
                    reads=(b_psba[rb],), writes=(b_pxf,), tag=f"pxtev{kk}.{hh}")

        if DEBUG:
            for kk in range(NQ):
                p.dma("scalar", dbg["z"][kk * 128 : (kk + 1) * 128, :],
                      z_park[kk * 128 : (kk + 1) * 128, :], deps=(b_zpark[kk].w,), tag="dbgz")
                p.dma("scalar", dbg["z2"][kk * 128 : (kk + 1) * 128, :],
                      z_park2[kk * 128 : (kk + 1) * 128, :], deps=(b_zpark[kk].w,), tag="dbgz2")

        wv_mms(NQ - 1)
        wv_gate(NQ - 1)

    bctx.close()

    octx = contextlib.ExitStack()
    w1_b = octx.enter_context(nc.sbuf_tensor("w1_b", [128, 8, HD], BF16))
    w2_b = octx.enter_context(nc.sbuf_tensor("w2_b", [128, 32, D], BF16))
    r1a1_b = octx.enter_context(nc.sbuf_tensor("r1a1_b", [128, 8, 32], BF16))
    r2a2_b = octx.enter_context(nc.sbuf_tensor("r2a2_b", [128, 32, 32], BF16))
    b1l_s = octx.enter_context(nc.sbuf_tensor("b1l_s", [128, 32], F32))
    b1f_b = octx.enter_context(nc.sbuf_tensor("b1f_b", [24, HD], BF16))
    b2f_b = octx.enter_context(nc.sbuf_tensor("b2f_b", [24, D], BF16))
    b2r_b = octx.enter_context(nc.sbuf_tensor("b2r_b", [1, D], BF16))
    ones1_s = octx.enter_context(nc.sbuf_tensor("ones1_s", [1, 128], BF16))
    b_w1, b_w2, b_r1, b_r2, b_b1l, b_b1f, b_b2f, b_b2r, b_ones = (
        Buf(), Buf(), Buf(), Buf(), Buf(), Buf(), Buf(), Buf(), Buf())

    p.next_phase()
    p.fence()
    with nc.sbuf_tensor("wo_b", [128, 16, D], BF16) as wo_b, \
         nc.sbuf_tensor("zl_s", [128, 2, V], BF16) as zl_s, \
         nc.sbuf_tensor("zf_s", [128, 2, 16, 128], BF16) as zf_s, \
         nc.sbuf_tensor("xo2_s", [128, 2, D], F32) as xo2_s, \
         nc.sbuf_tensor("xn_s", [128, D], F32) as xn_s, \
         nc.psum_tensor("ps_zt", [128, 2, 8, 128], BF16) as ps_zt, \
         nc.psum_tensor("ps_o", [128, 2, 2, 512], F32) as ps_o:

        b_wo = Buf()
        b_zl = [Buf(), Buf()]
        b_zf = [Buf(), Buf()]
        b_xoo2 = [Buf(), Buf()]
        b_xn = Buf()
        b_pszt = [Buf(), Buf()]
        b_pso = [Buf(), Buf()]
        p.dma("sync", wo_b[:], wo_st.rearrange("(c p) f -> p c f", p=128),
              deps=(b_wost.w,), writes=(b_wo,), tag="ldwo")
        p.dma("scalar", w1_b[:], w1_st.rearrange("(c p) f -> p c f", p=128),
              deps=(b_w1st.w,), writes=(b_w1,), tag="ldw1")
        p.dma("sync", w2_b[:], w2_st.rearrange("(c p) f -> p c f", p=128),
              deps=(b_w2st.w,), writes=(b_w2,), tag="ldw2")
        p.dma("gpsimd", r1a1_b[:], r1a1.rearrange("(c p) f -> p c f", p=128), writes=(b_r1,), tag="ldr1")
        p.dma("gpsimd", r2a2_b[:], r2a2.rearrange("(c p) f -> p c f", p=128), writes=(b_r2,), tag="ldr2")
        p.dma("sync", b1l_s[:], b1L, writes=(b_b1l,), tag="ldb1l")
        p.dma("gpsimd", b1f_b[:], b1f, writes=(b_b1f,), tag="ldb1f")
        p.dma("gpsimd", b2f_b[:], b2f, writes=(b_b2f,), tag="ldb2f")
        p.dma("gpsimd", b2r_b[:], b2L, writes=(b_b2r,), tag="ldb2r")
        p.dma("sync", ones1_s[:], ones1, writes=(b_ones,), tag="ldones")

        def p2b_load(kk):
            kb = kk % 2
            p.dma("sync", zl_s[:, kb], z_park[kk * 128 : (kk + 1) * 128, :],
                  deps=(b_zpark[kk].w,), reads=(b_zf[kb],), writes=(b_zl[kb],), tag=f"zld{kk}")
            p.dma("scalar", xo_s[:], x_own[kk * 128 : (kk + 1) * 128, :],
                  writes=(b_xoo,), tag=f"xold{kk}") if False else None

        def p2b_xload(kk, slot):
            p.dma("scalar", xo2_s[:, slot], x_own[kk * 128 : (kk + 1) * 128, :],
                  writes=(b_xoo2[slot],), tag=f"xold{kk}")

        p2b_load(0)
        p2b_xload(0, 0)
        for kk in range(NQ):
            kb = kk % 2
            for h in range(2):
                for c in range(8):
                    cc = h * 8 + c
                    p.step("tensor", lambda e, kb=kb, h=h, c=c, cc=cc: e.transpose(
                        ps_zt[:, kb, c, :], zl_s[:, kb, cc * 128 : (cc + 1) * 128], ident_s[:]),
                        reads=(b_zl[kb], b_ident), writes=(b_pszt[kb],), tag=f"ztt{kk}.{cc}")
                p.step("vector", lambda e, kb=kb, h=h: e.tensor_copy(
                    zf_s[:, kb, h * 8 : (h + 1) * 8, :], ps_zt[:, kb]),
                    reads=(b_pszt[kb],), writes=(b_zf[kb],), tag=f"ztev{kk}.{h}")
            if kk + 1 < NQ:
                p2b_load(kk + 1)
                p2b_xload(kk + 1, (kk + 1) % 2)
            for cc in range(16):
                for n2 in range(2):
                    p.step("tensor", lambda e, kb=kb, cc=cc, n2=n2: e.matmul(
                        ps_o[:, kb, n2, :], zf_s[:, kb, cc, :], wo_b[:, cc, n2 * 512 : (n2 + 1) * 512],
                        start=(cc == 0), stop=(cc == 15), skip_group_check=True),
                        reads=(b_zf[kb], b_wo), writes=(b_pso[kb],), tag=f"omm{kk}.{cc}.{n2}")
            p.step("vector", lambda e, kb=kb, kk=kk: e.tensor_add(
                xn_s[:], ps_o[:, kb].rearrange("p a b -> p (a b)"), xo2_s[:, kk % 2]),
                reads=(b_pso[kb], b_xoo2[kk % 2]), writes=(b_xn,), tag=f"resid{kk}")
            p.dma("sync", xnew_d[kk * 128 : (kk + 1) * 128, :], xn_s[:],
                  reads=(b_xn,), writes=(b_xnew[kk],), tag=f"xnst{kk}")

    if DEBUG:
        p.next_phase()
        for kk in range(NQ):
            p.dma("scalar", dbg["xnew"][kk * 128 : (kk + 1) * 128, :],
                  xnew_d[kk * 128 : (kk + 1) * 128, :], deps=(b_xnew[kk].w,), tag="dbgxn")

    octx_close_marker = None

    p.next_phase()
    p.fence()
    p3ctx = contextlib.ExitStack()
    x2n = p3ctx.enter_context(nc.sbuf_tensor("x2n", [128, 2, D], F32))
    ybf = p3ctx.enter_context(nc.sbuf_tensor("ybf", [128, D], BF16))
    yfm = p3ctx.enter_context(nc.sbuf_tensor("yfm", [128, 8, 512], BF16))
    hfm = p3ctx.enter_context(nc.sbuf_tensor("hfm", [128, 32, 512], BF16))
    rz1 = p3ctx.enter_context(nc.sbuf_tensor("rz1", [32, 512], BF16))
    rt = p3ctx.enter_context(nc.sbuf_tensor("rt", [128, 4, 32], BF16))
    gatef = p3ctx.enter_context(nc.sbuf_tensor("gatef", [128, 4, 4], F32))
    gex = p3ctx.enter_context(nc.sbuf_tensor("gex", [128, 4, 24], F32))
    zgj = p3ctx.enter_context(nc.sbuf_tensor("zgj", [128, 4, 24], BF16))
    zg1t = p3ctx.enter_context(nc.sbuf_tensor("zg1t", [24, 512], BF16))
    zg2t = p3ctx.enter_context(nc.sbuf_tensor("zg2t", [24, 512], BF16))
    lnsc3 = p3ctx.enter_context(nc.sbuf_tensor("lnsc3", [128, 4, 4], F32))
    outst = p3ctx.enter_context(nc.sbuf_tensor("outst", [128, D], F32))
    sm3 = p3ctx.enter_context(nc.sbuf_tensor("sm3", [128, 4, 2], F32))
    ps3_tr = p3ctx.enter_context(nc.psum_tensor("ps3_tr", [128, 8, 128], BF16))
    ps_r = p3ctx.enter_context(nc.psum_tensor("ps_r", [128, 512], F32))
    ps_h = p3ctx.enter_context(nc.psum_tensor("ps_h", [128, 2, 512], F32))
    ps_w2 = p3ctx.enter_context(nc.psum_tensor("ps_w2", [128, 2, 2, 512], F32))
    if True:
        b_x2n = Buf()
        b_x2n = Buf()
        b_ybf = Buf()
        b_yfm = Buf()
        b_hfm = Buf()
        b_rz1 = Buf()
        b_rt = Buf()
        b_gate = Buf()
        b_gex = Buf()
        b_zgj = Buf()
        b_zg1t = Buf()
        b_zg2t = Buf()
        b_ln3 = [Buf() for _ in range(4)]
        b_outst = Buf()
        b_sm3 = Buf()
        b_pstr3 = Buf()
        b_psr = Buf()
        b_psh = [Buf(), Buf()]
        b_psw2 = [Buf(), Buf()]

        def router_dance(g3, zgT, b_zgT, b_src_fm, what):
            p.step("vector", lambda e: e.tensor_copy(rz1[:], ps_r[0:32, :]),
                   reads=(b_psr,), writes=(b_rz1,), tag=f"rzev{what}{g3}")
            for j in range(4):
                p.step("tensor", lambda e, j=j: e.transpose(
                    ps3_tr[:, j, 0:32], rz1[:, j * 128 : (j + 1) * 128], ident_s[0:32, 0:32]),
                    reads=(b_rz1, b_ident), writes=(b_pstr3,), tag=f"rtt{what}{g3}.{j}")
            p.step("vector", lambda e: e.tensor_copy(rt[:, :, :], ps3_tr[:, 0:4, 0:32]),
                   reads=(b_pstr3,), writes=(b_rt,), tag=f"rtev{what}{g3}")
            for j in range(4):
                p.step("vector", lambda e, j=j: e.tensor_reduce(
                    sm3[:, j, 0:1], rt[:, j, 0:3], axis=AX.X, op=ALU.max, negate=True),
                    reads=(b_rt,), writes=(b_sm3,), tag=f"rmax{what}{g3}.{j}")
            for j in range(4):
                p.step("scalar", lambda e, j=j: e.activation(
                    out=gatef[:, j, 0:3], in_=rt[:, j, 0:3], func=AF.Exp, bias=sm3[:, j, 0:1],
                    accum_out=sm3[:, j, 1:2]),
                    reads=(b_rt, b_sm3), writes=(b_gate, b_sm3), tag=f"rexp{what}{g3}.{j}")
            p.step("vector", lambda e: e.reciprocal(sm3[:, :, 1], sm3[:, :, 1]),
                   reads=(b_sm3,), writes=(b_sm3,), tag=f"rrcp{what}{g3}")
            for j in range(4):
                p.step("vector", lambda e, j=j: e.tensor_scalar_mul(
                    gatef[:, j, 0:3], rt[:, j, 0:3] if False else gatef[:, j, 0:3], sm3[:, j, 1:2]),
                    reads=(b_gate, b_sm3), writes=(b_gate,), tag=f"rnorm{what}{g3}.{j}")
            for j in range(4):
                for ee in range(3):
                    p.step("vector", lambda e, j=j, ee=ee: e.tensor_copy(
                        gex[:, j, ee * 8 : (ee + 1) * 8],
                        gatef[:, j, ee : ee + 1].broadcast_to([128, 8])),
                        reads=(b_gate,), writes=(b_gex,), tag=f"rgex{what}{g3}.{j}.{ee}")
            for j in range(4):
                p.step("vector", lambda e, j=j: e.scalar_tensor_tensor(
                    out=zgj[:, j, :], in0=rt[:, j, 3:27], scalar=0.125, in1=gex[:, j, :],
                    op0=ALU.mult, op1=ALU.mult),
                    reads=(b_rt, b_gex), writes=(b_zgj,), tag=f"rzg{what}{g3}.{j}")
            for j in range(4):
                p.step("tensor", lambda e, j=j: e.transpose(
                    ps3_tr[0:24, 4 + j, :], zgj[:, j, 0:24], ident_s[:]),
                    reads=(b_zgj, b_ident), writes=(b_pstr3,), tag=f"zgt{what}{g3}.{j}")
            p.step("vector", lambda e, zgT=zgT: e.tensor_copy(
                zgT.rearrange("q (j x) -> q j x", x=128), ps3_tr[0:24, 4:8, :]),
                reads=(b_pstr3,), writes=(b_zgT,), tag=f"zgev{what}{g3}")

        def load_xn(g3, half, tag):
            rows0 = g3 * 512 + half * 256
            src3 = xnew_d[rows0 : rows0 + 256, :].rearrange("(j p) d -> p j d", p=128)
            p.dma("sync", x2n[:], src3, writes=(b_x2n,),
                  deps=[b_xnew[rows0 // 128 + jj].w for jj in range(2)], tag=f"{tag}{g3}.{half}")

        for g3 in range(4):
            for j in range(4):
                if j % 2 == 0:
                    load_xn(g3, j // 2, "ldxn")
                xt3 = x2n[:, j % 2, :]
                sc = lnsc3[:, j, :]
                p.step("vector", lambda e, xt3=xt3, sc=sc: e.reduce_sum(sc[:, 0:1], xt3, axis=AX.X),
                       reads=(b_x2n,), writes=(b_ln3[j],), tag=f"fsum{g3}.{j}")
                p.step("vector", lambda e, sc=sc: e.tensor_scalar_mul(sc[:, 1:2], sc[:, 0:1], -1.0 / D),
                       reads=(b_ln3[j],), writes=(b_ln3[j],), tag=f"fnegm{g3}.{j}")
                p.step("scalar", lambda e, xt3=xt3, sc=sc: e.activation(
                    out=ybf[:], in_=xt3, func=AF.Square, bias=sc[:, 1:2], scale=1.0, accum_out=sc[:, 2:3]),
                    reads=(b_x2n, b_ln3[j]), writes=(b_ln3[j], b_ybf), tag=f"fsq{g3}.{j}")
                p.step("scalar", lambda e, sc=sc: e.activation(
                    out=sc[:, 3:4], in_=sc[:, 2:3], func=AF.Ln, bias=1e-5, scale=1.0 / D),
                    reads=(b_ln3[j], b_consts), writes=(b_ln3[j],), tag=f"fln{g3}.{j}")
                p.step("scalar", lambda e, sc=sc: e.activation(
                    out=sc[:, 3:4], in_=sc[:, 3:4], func=AF.Exp, scale=-0.5),
                    reads=(b_ln3[j], b_consts), writes=(b_ln3[j],), tag=f"fexp{g3}.{j}")
                p.step("vector", lambda e, sc=sc: e.tensor_mul(sc[:, 1:2], sc[:, 1:2], sc[:, 3:4]),
                       reads=(b_ln3[j],), writes=(b_ln3[j],), tag=f"fnegms{g3}.{j}")
                p.step("scalar", lambda e, xt3=xt3, sc=sc: e.activation(
                    out=ybf[:], in_=xt3, func=AF.Identity, bias=sc[:, 1:2], scale=sc[:, 3:4]),
                    reads=(b_ln3[j], b_x2n), writes=(b_ybf,), tag=f"fy{g3}.{j}")
                for c in range(8):
                    p.step("tensor", lambda e, c=c: e.transpose(
                        ps3_tr[:, c, :], ybf[:, c * 128 : (c + 1) * 128], ident_s[:]),
                        reads=(b_ybf, b_ident), writes=(b_pstr3,), tag=f"ftr{g3}.{j}.{c}")
                p.step("vector", lambda e, j=j: e.tensor_copy(
                    yfm[:, :, j * 128 : (j + 1) * 128], ps3_tr[:, :, :]),
                    reads=(b_pstr3,), writes=(b_yfm,), tag=f"ftrev{g3}.{j}")
            for c in range(8):
                p.step("tensor", lambda e, c=c: e.matmul(
                    ps_r[0:32, :], r1a1_b[:, c, :], yfm[:, c, :], start=(c == 0), stop=(c == 7),
                    skip_group_check=True),
                    reads=(b_r1, b_yfm), writes=(b_psr,), tag=f"r1mm{g3}.{c}")
            router_dance(g3, zg1t, b_zg1t, b_yfm, "a")
            for hc in range(32):
                hb = hc % 2
                for c in range(8):
                    p.step("tensor", lambda e, hc=hc, hb=hb, c=c: e.matmul(
                        ps_h[:, hb, :], w1_b[:, c, hc * 128 : (hc + 1) * 128], yfm[:, c, :],
                        start=(c == 0), stop=False, skip_group_check=True),
                        reads=(b_w1, b_yfm), writes=(b_psh[hb],), tag=f"w1mm{g3}.{hc}.{c}")
                p.step("tensor", lambda e, hc=hc, hb=hb: e.matmul(
                    ps_h[:, hb, :], b1f_b[0:24, hc * 128 : (hc + 1) * 128], zg1t[:],
                    start=False, stop=True, skip_group_check=True),
                    reads=(b_b1f, b_zg1t), writes=(b_psh[hb],), tag=f"b1mm{g3}.{hc}")
                p.step("scalar", lambda e, hc=hc, hb=hb: e.activation(
                    out=hfm[:, hc, :], in_=ps_h[:, hb, :], func=AF.Relu, bias=b1l_s[:, hc : hc + 1]),
                    reads=(b_psh[hb], b_b1l), writes=(b_hfm,), tag=f"hev{g3}.{hc}")
            for hc in range(32):
                p.step("tensor", lambda e, hc=hc: e.matmul(
                    ps_r[0:32, :], r2a2_b[:, hc, :], hfm[:, hc, :], start=(hc == 0), stop=(hc == 31),
                    skip_group_check=True),
                    reads=(b_r2, b_hfm), writes=(b_psr,), tag=f"r2mm{g3}.{hc}")
            router_dance(g3, zg2t, b_zg2t, b_hfm, "b")
            for j in range(4):
                jb = j % 2
                if j % 2 == 0:
                    load_xn(g3, j // 2, "rldxn")
                for hc in range(32):
                    for n2 in range(2):
                        p.step("tensor", lambda e, hc=hc, n2=n2, j=j, jb=jb: e.matmul(
                            ps_w2[:, jb, n2, :], hfm[:, hc, j * 128 : (j + 1) * 128],
                            w2_b[:, hc, n2 * 512 : (n2 + 1) * 512],
                            start=(hc == 0), stop=False, skip_group_check=True),
                            reads=(b_w2, b_hfm), writes=(b_psw2[jb],), tag=f"w2mm{g3}.{j}.{hc}.{n2}")
                for n2 in range(2):
                    p.step("tensor", lambda e, n2=n2, j=j, jb=jb: e.matmul(
                        ps_w2[:, jb, n2, :], zg2t[:, j * 128 : (j + 1) * 128], b2f_b[0:24, n2 * 512 : (n2 + 1) * 512],
                        start=False, stop=False, skip_group_check=True),
                        reads=(b_zg2t, b_b2f), writes=(b_psw2[jb],), tag=f"b2mm{g3}.{j}.{n2}")
                    p.step("tensor", lambda e, n2=n2, jb=jb: e.matmul(
                        ps_w2[:, jb, n2, :], ones1_s[:], b2r_b[0:1, n2 * 512 : (n2 + 1) * 512],
                        start=False, stop=True, skip_group_check=True),
                        reads=(b_ones, b_b2r), writes=(b_psw2[jb],), tag=f"bbmm{g3}.{j}.{n2}")
                p.step("vector", lambda e, j=j, jb=jb: e.tensor_add(
                    outst[:], ps_w2[:, jb].rearrange("p a b -> p (a b)"), x2n[:, j % 2, :]),
                    reads=(b_psw2[jb], b_x2n), writes=(b_outst,), tag=f"oev{g3}.{j}")
                p.dma("sync", out_own[(g3 * 4 + j) * 128 : (g3 * 4 + j + 1) * 128, :], outst[:],
                      reads=(b_outst,), tag=f"ost{g3}.{j}")

    p3ctx.close()
    octx.close()

    final_cnt = p.finalize(nc, sems)

    with nc.Block() as blk:
        def fin(e):
            for eng in ("sync", "scalar", "gpsimd"):
                for lane in QLANES[eng]:
                    if final_cnt[lane]:
                        e.wait_ge(sems[lane], final_cnt[lane])

        blk.sync(fin)

    ctx.close()
    return nc



def _inputs_for_cores(ins):
    x = np.asarray(ins["L0_attn_x"], np.float32)

    def f32(a):
        return np.ascontiguousarray(np.asarray(a, np.float32))

    A1 = f32(ins["L0_ffn_A1"]).transpose(1, 0, 2).reshape(D, 24)
    A2 = f32(ins["L0_ffn_A2"]).transpose(1, 0, 2).reshape(HD, 24)
    common = {
        "wq": f32(ins["L0_attn_W_q"]),
        "wk": f32(ins["L0_attn_W_k"]),
        "wr": f32(ins["L0_attn_W_r"]),
        "wv": f32(ins["L0_attn_W_v"]),
        "wg": f32(ins["L0_attn_W_g"]),
        "wo": f32(ins["L0_attn_W_o"]),
        "w1": f32(ins["L0_ffn_W1"]),
        "w2": f32(ins["L0_ffn_W2"]),
        "g_lnL": np.ascontiguousarray(f32(ins["L0_attn_g_ln"]).reshape(8, 128).T),
        "xl_u": f32(ins["L0_attn_xl_u"]).reshape(128, 1),
        "xl_v": f32(ins["L0_attn_xl_v"]).reshape(128, 1),
        "r1a1": np.ascontiguousarray(np.concatenate(
            [f32(ins["L0_ffn_R1"]), A1, np.zeros((D, 5), np.float32)], axis=1)),
        "b1f": f32(ins["L0_ffn_B1"]).reshape(24, HD),
        "b1L": np.ascontiguousarray(f32(ins["L0_ffn_b1"]).reshape(32, 128).T),
        "r2a2": np.ascontiguousarray(np.concatenate(
            [f32(ins["L0_ffn_R2"]), A2, np.zeros((HD, 5), np.float32)], axis=1)),
        "b2f": f32(ins["L0_ffn_B2"]).reshape(24, D),
        "b2L": f32(ins["L0_ffn_b2"]).reshape(1, D),
    }
    masks = [_mask256(0), _mask256(1)]
    sins = [_sinusoid_fm(0), _sinusoid_fm(1)]
    in_maps = []
    for core in range(8):
        b, parity = core // 2, core % 2
        xb = np.ascontiguousarray(x[b])
        xow = np.ascontiguousarray(xb.reshape(NT, 128, D)[parity::2].reshape(NQ * 128, D))
        m = dict(common)
        m["x_full"] = xb
        m["x_own"] = xow
        m["mask256"] = masks[parity]
        m["sin_fm"] = sins[parity]
        in_maps.append(m)
    return in_maps


def kernel(**inputs):
    in_maps = _inputs_for_cores(inputs)
    nc = build_program()
    res = run_bass_kernel_spmd(nc, in_maps, core_ids=list(range(8)))
    out = np.empty((B, T, D), np.float32)
    for core in range(8):
        b, parity = core // 2, core % 2
        o = np.asarray(res.results[core]["out_own"], np.float32).reshape(NQ, 128, D)
        out.reshape(B, NT, 128, D)[b, parity::2] = o
    return out


# version 12
# speedup vs baseline: 1.1563x; 1.1563x over previous
import contextlib

import numpy as np
import ml_dtypes
import concourse.bass as bass
import concourse.mybir as mybir
from concourse.bass_utils import run_bass_kernel_spmd

F32 = mybir.dt.float32
BF16 = mybir.dt.bfloat16
I32 = mybir.dt.int32
AF = mybir.ActivationFunctionType
ALU = mybir.AluOpType
AX = mybir.AxisListType
ds = bass.ds

B, T, D, KD, V, HD = 4, 4096, 1024, 128, 2048, 4096
NT = T // 128
NQ = 16
TAU = float(np.sqrt(np.float32(KD)))
EPS1, EPS2 = 1e-5, 1e-6
PW = 4608
PR = PW - 1
NEG = -1e30

DEBUG = False


ENGS = ("tensor", "scalar", "vector", "sync", "gpsimd")
NLANES = 4
QSEM = {"sync": "q_s", "scalar": "q_a", "gpsimd": "q_g", "vector": "q_v", "tensor": "q_t"}
QLANES = {e: [f"{q}{i}" for i in range(NLANES)] for e, q in QSEM.items()}
ESEM = {"tensor": "s_pe", "scalar": "s_act", "vector": "s_dve", "sync": "s_sp", "gpsimd": "s_pl"}
ALL_SEMS = sorted({s for ls in QLANES.values() for s in ls} | set(ESEM.values()))


class Step:
    __slots__ = ("i", "eng", "emit", "deps", "raw", "dma", "phase", "inc", "tag")

    def __init__(self, i, eng, emit, deps, raw, dma, phase, tag):
        self.i = i
        self.eng = eng
        self.emit = emit
        self.deps = deps
        self.raw = raw
        self.dma = dma
        self.phase = phase
        self.inc = None
        self.tag = tag


class Buf:
    __slots__ = ("w", "rs")

    def __init__(self):
        self.w = None
        self.rs = []


class Prog:
    def __init__(self):
        self.steps = []
        self.phase = 0

    def next_phase(self):
        self.phase += 1

    def fence(self):
        dmas = [st for st in self.steps if st.dma]
        for eng in ENGS:
            self.step(eng, lambda e: None, deps=dmas, tag="fence")

    def step(self, eng, emit, reads=(), writes=(), deps=(), dma=False, tag=""):
        dd = []
        raw = set()
        for b in reads:
            if b.w is not None:
                dd.append(b.w)
                raw.add(b.w.i)
        for b in writes:
            if b.w is not None:
                dd.append(b.w)
            dd.extend(b.rs)
        for d in deps:
            if d is not None:
                dd.append(d)
                raw.add(d.i)
        st = Step(len(self.steps), eng, emit, dd, raw, dma, self.phase, tag)
        self.steps.append(st)
        for b in reads:
            b.rs.append(st)
        for b in writes:
            b.w = st
            b.rs = []
        return st

    def dma(self, eng, out_ap, in_ap, reads=(), writes=(), deps=(), tag=""):
        def emit(e, o=out_ap, i=in_ap):
            return e.dma_start(out=o() if callable(o) else o, in_=i() if callable(i) else i)

        return self.step(eng, emit, reads=reads, writes=writes, deps=deps, dma=True, tag=tag)

    def _needs_wait(self, st, d):
        if d.inc is None:
            return False
        if d.dma:
            return True
        if d.phase != st.phase:
            return False
        if d.eng != st.eng:
            return True
        if st.dma:
            return True
        return d.i in st.raw and st.eng in ("vector", "scalar", "gpsimd")

    def finalize(self, nc, sems):
        consumed = [False] * len(self.steps)
        for st in self.steps:
            for d in st.deps:
                if d.dma:
                    consumed[d.i] = True
                elif d.phase != st.phase:
                    pass
                elif d.eng != st.eng or st.dma or (
                    d.i in st.raw and st.eng in ("vector", "scalar", "gpsimd")
                ):
                    consumed[d.i] = True
        cnt = {name: 0 for name in sems}
        qrr = {e: 0 for e in QSEM}
        for st in self.steps:
            if st.dma:
                lane = QLANES[st.eng][qrr[st.eng] % NLANES]
                qrr[st.eng] += 1
                cnt[lane] += 16
                st.inc = (lane, cnt[lane])
            elif consumed[st.i]:
                s = ESEM[st.eng]
                cnt[s] += 1
                st.inc = (s, cnt[s])
        nph = self.phase + 1
        by_phase = [[] for _ in range(nph)]
        for st in self.steps:
            by_phase[st.phase].append(st)
        for ph in range(nph):
            plist = by_phase[ph]
            if not plist:
                continue
            with nc.Block() as blk:
                for eng in ENGS:
                    mine = [st for st in plist if st.eng == eng]
                    if not mine:
                        continue

                    def make(mine=mine, eng=eng):
                        def f(e):
                            seen = {}
                            for st in mine:
                                for d in st.deps:
                                    if not self._needs_wait(st, d):
                                        continue
                                    sname, val = d.inc
                                    if seen.get(sname, -1) >= val:
                                        continue
                                    e.wait_ge(sems[sname], val)
                                    seen[sname] = val
                                if st.dma:
                                    lname, lval = st.inc
                                    if lval > 16 and seen.get(lname, -1) < lval - 16:
                                        e.wait_ge(sems[lname], lval - 16)
                                        seen[lname] = lval - 16
                                ins = st.emit(e)
                                if st.inc is not None:
                                    amt = 16 if st.dma else 1
                                    ins.then_inc(sems[st.inc[0]], amt)

                        return f

                    getattr(blk, eng)(make())
        return cnt



NSIN = 4352

def _sinusoid_fm(parity):
    LAM = 10000.0
    pos = np.arange(T, dtype=np.float32)
    inv_lams = (1.0 / LAM ** (np.arange(0, D, 2, dtype=np.float32) / D)).astype(np.float32)
    pre = pos[:, None] * inv_lams[None, :]
    cat = np.concatenate([np.sin(pre), np.cos(pre)], axis=-1).astype(np.float32)
    S = np.flip(cat, axis=0)
    rolled = np.roll(S, 128 * parity, axis=0)
    table = np.concatenate([rolled, rolled[:NSIN - T]], axis=0)
    return np.ascontiguousarray(table.T).reshape(8, 128, NSIN).astype(ml_dtypes.bfloat16)


def _mask256(parity):
    pp = np.arange(128)[:, None]
    cc = np.arange(256)[None, :]
    keep = cc <= pp + 128 * parity
    return np.where(keep, np.float32(0.0), np.float32(NEG)).astype(np.float32)



def build_program():
    nc = bass.Bass("TRN2", target_bir_lowering=False)

    x_full = nc.dram_tensor("x_full", [T, D], F32, kind="ExternalInput").ap()
    x_own = nc.dram_tensor("x_own", [NQ * 128, D], F32, kind="ExternalInput").ap()
    wq = nc.dram_tensor("wq", [D, KD], F32, kind="ExternalInput").ap()
    wk = nc.dram_tensor("wk", [D, KD], F32, kind="ExternalInput").ap()
    wr = nc.dram_tensor("wr", [D, KD], F32, kind="ExternalInput").ap()
    wv = nc.dram_tensor("wv", [D, V], F32, kind="ExternalInput").ap()
    wg = nc.dram_tensor("wg", [D, V], F32, kind="ExternalInput").ap()
    wo = nc.dram_tensor("wo", [V, D], F32, kind="ExternalInput").ap()
    w1 = nc.dram_tensor("w1", [D, HD], F32, kind="ExternalInput").ap()
    w2 = nc.dram_tensor("w2", [HD, D], F32, kind="ExternalInput").ap()
    g_lnL = nc.dram_tensor("g_lnL", [128, 8], F32, kind="ExternalInput").ap()
    xl_u = nc.dram_tensor("xl_u", [128, 1], F32, kind="ExternalInput").ap()
    xl_v = nc.dram_tensor("xl_v", [128, 1], F32, kind="ExternalInput").ap()
    r1a1 = nc.dram_tensor("r1a1", [D, 32], F32, kind="ExternalInput").ap()
    b1f = nc.dram_tensor("b1f", [24, HD], F32, kind="ExternalInput").ap()
    b1L = nc.dram_tensor("b1L", [128, 32], F32, kind="ExternalInput").ap()
    r2a2 = nc.dram_tensor("r2a2", [HD, 32], F32, kind="ExternalInput").ap()
    b2f = nc.dram_tensor("b2f", [24, D], F32, kind="ExternalInput").ap()
    b2L = nc.dram_tensor("b2L", [1, D], F32, kind="ExternalInput").ap()
    sin_fm = nc.dram_tensor("sin_fm", [8, 128, NSIN], BF16, kind="ExternalInput").ap()
    mask256 = nc.dram_tensor("mask256", [128, 256], F32, kind="ExternalInput").ap()

    out_own = nc.dram_tensor("out_own", [NQ * 128, D], F32, kind="ExternalOutput").ap()

    ident = nc.inline_tensor(np.eye(128, dtype=ml_dtypes.bfloat16), name="ident").ap()
    ones1 = nc.inline_tensor(np.ones((1, 128), dtype=ml_dtypes.bfloat16), name="ones1").ap()

    g_park = nc.dram_tensor("g_park", [NQ * 128, V], BF16, kind="Internal").ap()
    z_park = nc.dram_tensor("z_park", [NQ * 128, V], BF16, kind="Internal").ap()
    z_park2 = nc.dram_tensor("z_park2", [NQ * 128, V], BF16, kind="Internal").ap()
    xnew_d = nc.dram_tensor("xnew_d", [NQ * 128, D], F32, kind="Internal").ap()
    wo_st = nc.dram_tensor("wo_st", [V, D], BF16, kind="Internal").ap()
    w1_st = nc.dram_tensor("w1_st", [D, HD], BF16, kind="Internal").ap()
    w2_st = nc.dram_tensor("w2_st", [HD, D], BF16, kind="Internal").ap()
    bounce = nc.dram_tensor("bounce", [129 * PW], F32, kind="Internal")
    bwr = bounce.ap()[0 : 128 * PW].rearrange("(p w) -> p w", w=PW)
    brd = bounce.ap()[0 : 129 * PR].rearrange("(p w) -> p w", w=PR)

    dbg = {}
    if DEBUG:
        dbg["xt"] = nc.dram_tensor("dbg_xt", [T, D], BF16, kind="ExternalOutput").ap()
        dbg["k"] = nc.dram_tensor("dbg_k", [128, T], BF16, kind="ExternalOutput").ap()
        dbg["g"] = nc.dram_tensor("dbg_g", [NQ * 128, V], BF16, kind="ExternalOutput").ap()
        dbg["qu"] = nc.dram_tensor("dbg_qu", [128, NQ * 128], BF16, kind="ExternalOutput").ap()
        dbg["qv"] = nc.dram_tensor("dbg_qv", [128, NQ * 128], BF16, kind="ExternalOutput").ap()
        dbg["xlr"] = nc.dram_tensor("dbg_xlr", [128, NSIN], BF16, kind="ExternalOutput").ap()
        dbg["xnew"] = nc.dram_tensor("dbg_xnew", [NQ * 128, D], F32, kind="ExternalOutput").ap()
        dbg["z"] = nc.dram_tensor("dbg_z", [NQ * 128, V], BF16, kind="ExternalOutput").ap()
        dbg["z2"] = nc.dram_tensor("dbg_z2", [NQ * 128, V], BF16, kind="ExternalOutput").ap()
        dbg["a15"] = nc.dram_tensor("dbg_a15", [128, T], F32, kind="ExternalOutput").ap()
        dbg["c15"] = nc.dram_tensor("dbg_c15", [128, T], BF16, kind="ExternalOutput").ap()
        dbg["bb15"] = nc.dram_tensor("dbg_bb15", [128, T], F32, kind="ExternalOutput").ap()
        dbg["px15"] = nc.dram_tensor("dbg_px15", [128, D], BF16, kind="ExternalOutput").ap()
        dbg["pxf0"] = nc.dram_tensor("dbg_pxf0", [128, D], BF16, kind="ExternalOutput").ap()
        dbg["gt0"] = nc.dram_tensor("dbg_gt0", [128, V], BF16, kind="ExternalOutput").ap()
        dbg["wv0"] = nc.dram_tensor("dbg_wv0", [128, V], F32, kind="ExternalOutput").ap()

    p = Prog()
    ctx = contextlib.ExitStack()
    sems = {n: ctx.enter_context(nc.semaphore(n)) for n in ALL_SEMS}

    ident_s = ctx.enter_context(nc.sbuf_tensor("ident_s", [128, 128], BF16))
    mask_s = ctx.enter_context(nc.sbuf_tensor("mask_s", [128, 256], F32))
    g_sb = ctx.enter_context(nc.sbuf_tensor("g_sb", [128, 8], F32))
    xlu_s = ctx.enter_context(nc.sbuf_tensor("xlu_s", [128, 1], F32))
    xlv_s = ctx.enter_context(nc.sbuf_tensor("xlv_s", [128, 1], F32))
    CONSTS = [EPS1 * EPS2, (1.0 + EPS2) / D, 1e-5, 1.0 / D, -0.5, 1.0 / TAU]
    consts_sb = ctx.enter_context(nc.sbuf_tensor("consts_sb", [128, len(CONSTS)], F32))
    for j, val in enumerate(CONSTS):
        nc.const_aps.register(F32, float(val), consts_sb[:, j : j + 1])

    bctx = contextlib.ExitStack()
    xt_tm = bctx.enter_context(nc.sbuf_tensor("xt_tm", [128, NT, D], BF16))
    k_fm = bctx.enter_context(nc.sbuf_tensor("k_fm", [128, T], BF16))
    qu_fm = bctx.enter_context(nc.sbuf_tensor("qu_fm", [128, NQ * 128], BF16))
    qv_fm = bctx.enter_context(nc.sbuf_tensor("qv_fm", [128, NQ * 128], BF16))
    xlr = bctx.enter_context(nc.sbuf_tensor("xlr", [128, NSIN], BF16))
    wv_b = bctx.enter_context(nc.sbuf_tensor("wv_b", [128, 8, V], BF16))
    wstg = bctx.enter_context(nc.sbuf_tensor("wstg", [128, V], BF16))
    wq_b = bctx.enter_context(nc.sbuf_tensor("wq_b", [128, 8, KD], BF16))
    wk_b = bctx.enter_context(nc.sbuf_tensor("wk_b", [128, 8, KD], BF16))

    b_misc = Buf()
    b_ident, b_mask, b_zeros, b_gsb, b_xlu, b_xlv, b_ro, b_consts = (
        Buf(), Buf(), Buf(), Buf(), Buf(), Buf(), Buf(), Buf())
    b_wq, b_wk, b_wv, b_wr, b_wg = Buf(), Buf(), Buf(), Buf(), Buf()
    b_xt = [Buf() for _ in range(NT)]
    b_kfm = [Buf() for _ in range(8)]
    b_qu = [Buf() for _ in range(4)]
    b_xlr = Buf()
    b_gpark = [Buf() for _ in range(NQ)]
    b_zpark = [Buf() for _ in range(NQ)]
    b_xnew = [Buf() for _ in range(NQ)]

    for dst, src, bf_ in ((ident_s[:], ident, b_ident), (mask_s[:], mask256, b_mask),
                          (g_sb[:], g_lnL, b_gsb), (xlu_s[:], xl_u, b_xlu),
                          (xlv_s[:], xl_v, b_xlv)):
        p.dma("sync", dst, src, writes=(bf_,), tag="p0")
    for j, val in enumerate(CONSTS):
        p.step("vector", lambda e, j=j, val=val: e.memset(consts_sb[:, j : j + 1], float(val)),
               writes=(b_consts,), tag=f"p0c{j}")

    p.next_phase()


    with nc.sbuf_tensor("x4", [128, 2, 4, D], F32) as x4, \
         nc.sbuf_tensor("xfm4", [128, 2, 8, 512], BF16) as xfm4, \
         nc.sbuf_tensor("lnsc", [128, 8, 4], F32) as lnsc, \
         nc.sbuf_tensor("junk", [128, D], BF16) as junk, \
         nc.sbuf_tensor("wr_b", [128, 8, KD], BF16) as wr_b, \
         nc.sbuf_tensor("sin_sb", [128, 2, T], BF16) as sin_sb, \
         nc.sbuf_tensor("sin_rem", [128, 8, NSIN - T], BF16) as sin_rem, \
         nc.sbuf_tensor("_p1a_end", [1, 4], F32) as _p1a_end:

        b_x4 = [Buf(), Buf()]
        b_xfm4 = [Buf(), Buf()]
        b_pstr1 = [Buf(), Buf()]
        b_psqk = [Buf(), Buf()]
        b_lnj = [Buf() for _ in range(8)]
        b_junk = Buf()
        b_sin = [Buf(), Buf()]
        b_sinr = Buf()
        b_psxl = [Buf(), Buf()]

        p.dma("gpsimd", wr_b[:], wr.rearrange("(c p) f -> p c f", p=128), writes=(b_wr,), tag="ldwr")
        b_wstg = Buf()
        p.dma("gpsimd", wstg[:, 0 : 8 * KD].rearrange("p (c f) -> p c f", f=KD),
              wq.rearrange("(c p) f -> p c f", p=128), writes=(b_wstg,), tag="ldwqr")
        for c in range(8):
            p.step("vector", lambda e, c=c: e.tensor_scalar_mul(
                wq_b[:, c, :], wstg[:, c * KD : (c + 1) * KD], g_sb[:, c : c + 1]),
                reads=(b_wstg, b_gsb), writes=(b_wq,), tag=f"foldq{c}")
        p.dma("gpsimd", wstg[:, 8 * KD : 16 * KD].rearrange("p (c f) -> p c f", f=KD),
              wk.rearrange("(c p) f -> p c f", p=128), writes=(b_wstg,), tag="ldwkr")
        for c in range(8):
            p.step("vector", lambda e, c=c: e.tensor_scalar_mul(
                wk_b[:, c, :], wstg[:, (8 + c) * KD : (9 + c) * KD], g_sb[:, c : c + 1]),
                reads=(b_wstg, b_gsb), writes=(b_wk,), tag=f"foldk{c}")
        with nc.psum_tensor("ps_xlr8", [128, 8, 512], F32) as ps_xlr8:
            b_psxl = Buf()
            for c in range(8):
                cb = c % 2
                p.dma("scalar", sin_sb[:, cb], sin_fm[c, :, 0:T], writes=(b_sin[cb],), tag=f"ldsin{c}")
                p.dma("scalar", sin_rem[:, c, :], sin_fm[c, :, T:NSIN], writes=(b_sinr,), tag=f"ldsinr{c}")
                for n in range(8):
                    p.step("tensor", lambda e, c=c, n=n, cb=cb: e.matmul(
                        ps_xlr8[:, n], wr_b[:, c, :], sin_sb[:, cb, n * 512 : (n + 1) * 512],
                        start=(c == 0), stop=(c == 7), skip_group_check=True),
                        reads=(b_wr, b_sin[cb]), writes=(b_psxl,), tag=f"xlrmm{c}.{n}")
            for n in range(8):
                p.step("scalar", lambda e, n=n: e.activation(
                    out=xlr[:, n * 512 : (n + 1) * 512], in_=ps_xlr8[:, n], func=AF.Identity),
                    reads=(b_psxl,), writes=(b_xlr,), tag=f"xlrev{n}")
            for c in range(8):
                p.step("tensor", lambda e, c=c: e.matmul(
                    ps_xlr8[:, 0, 0 : NSIN - T], wr_b[:, c, :], sin_rem[:, c, :],
                    start=(c == 0), stop=(c == 7), skip_group_check=True),
                    reads=(b_wr, b_sinr), writes=(b_psxl,), tag=f"xlrmr{c}")
            p.step("scalar", lambda e: e.activation(
                out=xlr[:, T:NSIN], in_=ps_xlr8[:, 0, 0 : NSIN - T], func=AF.Identity),
                reads=(b_psxl,), writes=(b_xlr,), tag="xlrevr")

        p1actx = contextlib.ExitStack()
        ps_tr1 = p1actx.enter_context(nc.psum_tensor("ps_tr1", [128, 2, 8, 128], BF16))
        ps_qk = p1actx.enter_context(nc.psum_tensor("ps_qk", [128, 2, 512], F32))

        def p1a_load(g):
            gb = g % 2
            src0 = x_full[g * 512 : (g + 1) * 512, :].rearrange("(j p) d -> p j d", p=128)
            p.dma("sync", x4[:, gb], src0, writes=(b_x4[gb],), tag=f"ldx{g}")

        def p1a_stats(g):
            gb = g % 2
            sl = (g % 2) * 4
            for j in range(4):
                p.step("vector", lambda e, gb=gb, j=j, sl=sl: e.reduce_sum(
                    lnsc[:, sl + j, 0:1], x4[:, gb, j, :], axis=AX.X),
                    reads=(b_x4[gb],), writes=(b_lnj[sl + j],), tag=f"sum{g}.{j}")
            for j in range(4):
                p.step("vector", lambda e, j=j, sl=sl: e.tensor_scalar_mul(
                    lnsc[:, sl + j, 1:2], lnsc[:, sl + j, 0:1], -1.0 / D),
                    reads=(b_lnj[sl + j],), writes=(b_lnj[sl + j],), tag=f"negm{g}.{j}")
            for j in range(4):
                p.step("scalar", lambda e, gb=gb, j=j, sl=sl: e.activation(
                    out=junk[:], in_=x4[:, gb, j, :], func=AF.Square, bias=lnsc[:, sl + j, 1:2],
                    scale=1.0, accum_out=lnsc[:, sl + j, 2:3]),
                    reads=(b_x4[gb], b_lnj[sl + j]), writes=(b_lnj[sl + j], b_junk), tag=f"sq{g}.{j}")
            for j in range(4):
                p.step("scalar", lambda e, j=j, sl=sl: e.activation(
                    out=lnsc[:, sl + j, 3:4], in_=lnsc[:, sl + j, 2:3], func=AF.Ln,
                    bias=EPS1 * EPS2, scale=(1.0 + EPS2) / D),
                    reads=(b_lnj[sl + j], b_consts), writes=(b_lnj[sl + j],), tag=f"ln{g}.{j}")
            for j in range(4):
                p.step("scalar", lambda e, j=j, sl=sl: e.activation(
                    out=lnsc[:, sl + j, 3:4], in_=lnsc[:, sl + j, 3:4], func=AF.Exp, scale=-0.5),
                    reads=(b_lnj[sl + j], b_consts), writes=(b_lnj[sl + j],), tag=f"exp{g}.{j}")
            for j in range(4):
                p.step("vector", lambda e, j=j, sl=sl: e.tensor_mul(
                    lnsc[:, sl + j, 1:2], lnsc[:, sl + j, 1:2], lnsc[:, sl + j, 3:4]),
                    reads=(b_lnj[sl + j],), writes=(b_lnj[sl + j],), tag=f"negms{g}.{j}")

        p1a_load(0)
        p1a_stats(0)
        for g in range(8):
            gb = g % 2
            sl = (g % 2) * 4
            if g + 1 < 8:
                p1a_load(g + 1)
            for j in range(4):
                t = 4 * g + j
                p.step("scalar", lambda e, gb=gb, j=j, t=t, sl=sl: e.activation(
                    out=xt_tm[:, t, :], in_=x4[:, gb, j, :], func=AF.Identity,
                    bias=lnsc[:, sl + j, 1:2], scale=lnsc[:, sl + j, 3:4]),
                    reads=(b_lnj[sl + j], b_x4[gb]), writes=(b_xt[t],), tag=f"xt{t}")
                for c in range(8):
                    p.step("tensor", lambda e, t=t, c=c, gb=gb: e.transpose(
                        ps_tr1[:, gb, c, :], xt_tm[:, t, c * 128 : (c + 1) * 128], ident_s[:]),
                        reads=(b_xt[t], b_ident), writes=(b_pstr1[gb],), tag=f"tr{t}.{c}")
                p.step("vector", lambda e, gb=gb, j=j: e.tensor_copy(
                    xfm4[:, gb, :, j * 128 : (j + 1) * 128], ps_tr1[:, gb]),
                    reads=(b_pstr1[gb],), writes=(b_xfm4[gb],), tag=f"trev{t}")
            for c in range(8):
                p.step("tensor", lambda e, c=c, gb=gb: e.matmul(
                    ps_qk[:, gb], wk_b[:, c, :], xfm4[:, gb, c, :],
                    start=(c == 0), stop=(c == 7), skip_group_check=True),
                    reads=(b_wk, b_xfm4[gb]), writes=(b_psqk[gb],), tag=f"kmm{g}.{c}")
            p.step("scalar", lambda e, g=g, gb=gb: e.activation(
                out=k_fm[:, g * 512 : (g + 1) * 512], in_=ps_qk[:, gb], func=AF.Identity),
                reads=(b_psqk[gb],), writes=(b_kfm[g],), tag=f"kev{g}")
            if g + 1 < 8:
                p1a_stats(g + 1)

        for c in range(8):
            p.dma("gpsimd", wstg[:, 0:V], wv[c * 128 : (c + 1) * 128, :],
                  writes=(b_wstg,), tag=f"ldwv{c}")
            p.step("vector", lambda e, c=c: e.tensor_scalar_mul(
                wv_b[:, c, :], wstg[:, 0:V], g_sb[:, c : c + 1]),
                reads=(b_wstg, b_gsb), writes=(b_wv,), tag=f"foldv{c}")

        if DEBUG:
            for t in range(NT):
                p.dma("scalar", dbg["xt"][t * 128 : (t + 1) * 128, :], xt_tm[:, t, :], reads=(b_xt[t],), tag="dbgxt")
            for g in range(8):
                p.dma("scalar", dbg["k"][:, g * 512 : (g + 1) * 512], k_fm[:, g * 512 : (g + 1) * 512],
                      reads=(b_kfm[g],), tag="dbgk")
            p.dma("scalar", dbg["xlr"], xlr[:], reads=(b_xlr,), tag="dbgxlr")
        p1actx.close()

    p.next_phase()
    p.fence()
    with nc.sbuf_tensor("wg_b", [128, 8, V], BF16) as wg_b, \
         nc.sbuf_tensor("xo4", [128, 4, D], F32) as xo4, \
         nc.sbuf_tensor("xof4", [128, 2, 8, 512], BF16) as xof4, \
         nc.sbuf_tensor("xob", [128, 2, D], BF16) as xob, \
         nc.sbuf_tensor("gt", [128, 2, V], BF16) as gt, \
         nc.sbuf_tensor("lnst", [128, NQ, 2], F32) as lnst, \
         nc.sbuf_tensor("lnsg", [128, NQ, 2], F32) as lnsg, \
         nc.psum_tensor("ps_tr2", [128, 2, 8, 128], BF16) as ps_tr2, \
         nc.psum_tensor("ps_g", [128, 2, 512], F32) as ps_g, \
         nc.psum_tensor("ps_q", [128, 2, 512], F32) as ps_q:

        b_wstg2 = Buf()
        for c in range(8):
            p.dma("gpsimd", wstg[:, 0:V], wg[c * 128 : (c + 1) * 128, :],
                  writes=(b_wstg2,), tag=f"ldwg{c}")
            p.step("vector", lambda e, c=c: e.tensor_scalar_mul(
                wg_b[:, c, :], wstg[:, 0:V], g_sb[:, c : c + 1]),
                reads=(b_wstg2, b_gsb), writes=(b_wg,), tag=f"foldg{c}")

        b_xo = [Buf()]
        b_xof = [Buf(), Buf()]
        b_xob = [Buf(), Buf()]
        b_gt = [Buf(), Buf()]
        b_pstr2 = [Buf(), Buf()]
        b_psg = [Buf(), Buf()]
        b_psq = [Buf(), Buf()]
        b_lnk = [Buf() for _ in range(NQ)]
        b_lns = Buf()

        for g2 in range(4):
            src2 = x_own[g2 * 512 : (g2 + 1) * 512, :].rearrange("(j p) d -> p j d", p=128)
            p.dma("sync", xo4[:], src2, writes=(b_xo[0],), tag=f"ldxo{g2}")
            for j in range(4):
                k = 4 * g2 + j
                p.step("vector", lambda e, j=j, k=k: e.reduce_sum(
                    lnst[:, k, 0:1], xo4[:, j, :], axis=AX.X),
                    reads=(b_xo[0],), writes=(b_lnk[k],), tag=f"bsum{k}")
            for j in range(4):
                k = 4 * g2 + j
                p.step("vector", lambda e, k=k: e.tensor_scalar_mul(
                    lnst[:, k, 0:1], lnst[:, k, 0:1], -1.0 / D),
                    reads=(b_lnk[k],), writes=(b_lnk[k],), tag=f"bnegm{k}")
            for j in range(4):
                k = 4 * g2 + j
                p.step("scalar", lambda e, j=j, k=k: e.activation(
                    out=xob[:, 0], in_=xo4[:, j, :], func=AF.Square, bias=lnst[:, k, 0:1],
                    scale=1.0, accum_out=lnst[:, k, 1:2]),
                    reads=(b_xo[0], b_lnk[k]), writes=(b_lnk[k], b_xob[0]), tag=f"bsq{k}")
        p.step("scalar", lambda e: e.activation(
            out=lnsg[:, :, 1], in_=lnst[:, :, 1], func=AF.Ln, bias=EPS1 * EPS2,
            scale=(1.0 + EPS2) / D),
            reads=(b_consts,), deps=[b.w for b in b_lnk], writes=(b_lns,), tag="bln")
        p.step("scalar", lambda e: e.activation(
            out=lnsg[:, :, 1], in_=lnsg[:, :, 1], func=AF.Exp, scale=-0.5),
            reads=(b_lns, b_consts), writes=(b_lns,), tag="bexp")
        p.step("vector", lambda e: e.tensor_mul(lnsg[:, :, 0], lnst[:, :, 0], lnsg[:, :, 1]),
               reads=(b_lns,), deps=[b.w for b in b_lnk], writes=(b_lns,), tag="bnegms")

        def p1b_apply(k):
            g2, j = k // 4, k % 4
            kb2 = k % 2
            if j == 0:
                src2 = x_own[g2 * 512 : (g2 + 1) * 512, :].rearrange("(j p) d -> p j d", p=128)
                p.dma("sync", xo4[:], src2, writes=(b_xo[0],), tag=f"ldxo2_{g2}")
            p.step("scalar", lambda e, j=j, k=k, kb2=kb2: e.activation(
                out=xob[:, kb2], in_=xo4[:, j, :], func=AF.Identity, bias=lnsg[:, k, 0:1],
                scale=lnsg[:, k, 1:2]),
                reads=(b_lns, b_xo[0]), writes=(b_xob[kb2],), tag=f"bxt{k}")

        def p1b_tr(k):
            g2, j = k // 4, k % 4
            kb2 = k % 2
            gb = g2 % 2
            for c in range(8):
                p.step("tensor", lambda e, gb=gb, c=c, kb2=kb2: e.transpose(
                    ps_tr2[:, gb, c, :], xob[:, kb2, c * 128 : (c + 1) * 128], ident_s[:]),
                    reads=(b_xob[kb2], b_ident), writes=(b_pstr2[gb],), tag=f"btr{k}.{c}")
            p.step("vector", lambda e, gb=gb, j=j: e.tensor_copy(
                xof4[:, gb, :, j * 128 : (j + 1) * 128], ps_tr2[:, gb]),
                reads=(b_pstr2[gb],), writes=(b_xof[gb],), tag=f"btrev{k}")

        p1b_apply(0)
        p1b_tr(0)
        for k in range(NQ):
            g2, j = k // 4, k % 4
            gb = g2 % 2
            if k + 1 < NQ and (k + 1) % 4 != 0:
                p1b_apply(k + 1)
            for n in range(4):
                nb = n % 2
                for c in range(8):
                    p.step("tensor", lambda e, n=n, c=c, gb=gb, j=j, nb=nb: e.matmul(
                        ps_g[:, nb], xof4[:, gb, c, j * 128 : (j + 1) * 128],
                        wg_b[:, c, n * 512 : (n + 1) * 512],
                        start=(c == 0), stop=(c == 7), skip_group_check=True),
                        reads=(b_wg, b_xof[gb]), writes=(b_psg[nb],), tag=f"gmm{k}.{n}.{c}")
                p.step("scalar", lambda e, n=n, gb=gb, nb=nb: e.activation(
                    out=gt[:, gb, n * 512 : (n + 1) * 512], in_=ps_g[:, nb], func=AF.Silu),
                    reads=(b_psg[nb],), writes=(b_gt[gb],), tag=f"gev{k}.{n}")
            if k + 1 < NQ and (k + 1) % 4 == 0:
                p1b_apply(k + 1)
            if k + 1 < NQ:
                p1b_tr(k + 1)
            p.dma("scalar", g_park[k * 128 : (k + 1) * 128, :], gt[:, gb], reads=(b_gt[gb],),
                  writes=(b_gpark[k],), tag=f"gpark{k}")
            if k % 4 == 3:
                for c in range(8):
                    p.step("tensor", lambda e, c=c, gb=gb: e.matmul(
                        ps_q[:, gb], wq_b[:, c, :], xof4[:, gb, c, :], start=(c == 0), stop=(c == 7),
                        skip_group_check=True),
                        reads=(b_wq, b_xof[gb]), writes=(b_psq[gb],), tag=f"qmm{g2}.{c}")
                p.step("scalar", lambda e, g2=g2, gb=gb: e.activation(
                    out=qu_fm[:, g2 * 512 : (g2 + 1) * 512], in_=ps_q[:, gb], func=AF.Identity,
                    bias=xlu_s[:], scale=1.0 / TAU),
                    reads=(b_psq[gb], b_xlu, b_consts), writes=(b_qu[g2],), tag=f"quev{g2}")
                p.step("scalar", lambda e, g2=g2, gb=gb: e.activation(
                    out=qv_fm[:, g2 * 512 : (g2 + 1) * 512], in_=ps_q[:, gb], func=AF.Identity,
                    bias=xlv_s[:], scale=1.0 / TAU),
                    reads=(b_psq[gb], b_xlv, b_consts), writes=(b_qu[g2],), tag=f"qvev{g2}")

        if DEBUG:
            for k in range(NQ):
                p.dma("scalar", dbg["g"][k * 128 : (k + 1) * 128, :], g_park[k * 128 : (k + 1) * 128, :],
                      deps=(b_gpark[k].w,), tag="dbgg")
            p.dma("scalar", dbg["qu"], qu_fm[:], deps=[b.w for b in b_qu[:4]], tag="dbgqu")
            p.dma("scalar", dbg["qv"], qv_fm[:], deps=[b.w for b in b_qu[:4]], tag="dbgqv")

    p.next_phase()
    p.fence()
    b_wost, b_w1st, b_w2st = Buf(), Buf(), Buf()
    p.dma("gpsimd", wo_st, wo, writes=(b_wost,), tag="stgwo")
    p.dma("gpsimd", w1_st, w1, writes=(b_w1st,), tag="stgw1")
    p.dma("gpsimd", w2_st, w2, writes=(b_w2st,), tag="stgw2")
    with nc.sbuf_tensor("dst_s", [128, 4224], F32) as dst_s, \
         nc.sbuf_tensor("a_s", [128, T], F32) as a_s, \
         nc.sbuf_tensor("bb_s", [128, T], F32) as bb_s, \
         nc.sbuf_tensor("mb_s", [128, 256], F32) as mb_s, \
         nc.sbuf_tensor("c_s", [128, T], BF16) as c_s, \
         nc.sbuf_tensor("pt_s", [128, 32, 128], BF16) as pt_s, \
         nc.sbuf_tensor("pxn_s", [128, D], BF16) as pxn_s, \
         nc.sbuf_tensor("pxf_s", [128, 8, 128], BF16) as pxf_s, \
         nc.sbuf_tensor("gt2_s", [128, V], BF16) as gt2_s, \
         nc.sbuf_tensor("zt_s", [128, V], BF16) as zt_s, \
         nc.sbuf_tensor("sm_s", [128, 4], F32) as sm_s, \
         nc.psum_tensor("ps_ba", [128, 2, 512], F32) as ps_ba, \
         nc.psum_tensor("ps_px", [128, 2, 512], F32) as ps_px, \
         nc.psum_tensor("ps_wv", [128, 4, 512], F32) as ps_wv:

        b_dst = Buf()
        b_a = Buf()
        b_bb = Buf()
        b_mb = Buf()
        b_c = Buf()
        b_pt = Buf()
        b_pxn = Buf()
        b_pxf = Buf()
        b_gt2 = Buf()
        b_zt = Buf()
        b_sm = Buf()
        b_psba = [Buf(), Buf()]
        b_pspx = Buf()
        b_pswv = Buf()
        b_bounce = Buf()

        def bd_block(kk):
            k = kk + 1
            Wb = 256 * k + 128
            Wq_ = 256 * k
            ce_lo = 4224 - 256 * k
            nbd = (Wb + 511) // 512
            for m in range(nbd):
                n0 = m * 512
                n1 = min(n0 + 512, Wb)
                mb2 = m % 2
                p.step("tensor", lambda e, kk=kk, mb2=mb2, n0=n0, n1=n1, ce_lo=ce_lo: e.matmul(
                    ps_ba[:, mb2, 0 : n1 - n0], qv_fm[:, kk * 128 : (kk + 1) * 128],
                    xlr[:, ce_lo + n0 : ce_lo + n1], start=True, stop=True,
                    skip_group_check=True),
                    reads=(b_qu[kk // 4], b_xlr), writes=(b_psba[mb2],), tag=f"bdmm{kk}.{m}")
                p.step("scalar", lambda e, mb2=mb2, n0=n0, n1=n1: e.activation(
                    out=dst_s[:, n0:n1], in_=ps_ba[:, mb2, 0 : n1 - n0], func=AF.Identity),
                    reads=(b_psba[mb2],), writes=(b_dst,), tag=f"bdev{kk}.{m}")
            p.dma("sync", bwr[:, 128 : 128 + Wb], dst_s[:, 0:Wb], reads=(b_dst,), writes=(b_bounce,), tag=f"bwr{kk}")
            p.dma("sync", lambda Wq_=Wq_: bb_s[:, 0:Wq_],
                  lambda Wq_=Wq_: brd[0:128, 255 : 255 + Wq_],
                  reads=(b_bounce,), writes=(b_bb,), tag=f"bdiag{kk}")

        def wv_mms(kk):
            for c in range(8):
                for n2 in range(4):
                    p.step("tensor", lambda e, kk=kk, c=c, n2=n2: e.matmul(
                        ps_wv[:, n2, :], pxf_s[:, c, :], wv_b[:, c, n2 * 512 : (n2 + 1) * 512],
                        start=(c == 0), stop=(c == 7), skip_group_check=True),
                        reads=(b_pxf, b_wv), writes=(b_pswv,), tag=f"wvmm{kk}.{c}.{n2}")

        def wv_gate(kk):
            p.step("vector", lambda e: e.tensor_mul(
                zt_s[:], ps_wv[:, :, :].rearrange("p a b -> p (a b)"), gt2_s[:]),
                reads=(b_pswv, b_gt2), writes=(b_zt,), tag=f"gate{kk}")
            p.dma("scalar", z_park[kk * 128 : (kk + 1) * 128, :], zt_s[:],
                  reads=(b_zt,), writes=(b_zpark[kk],), tag=f"zpark{kk}")

        bd_block(0)

        for kk in range(NQ):
            k = kk + 1
            Wq_ = 256 * k
            kb = kk % 2
            nch = 2 * k
            nac = (Wq_ + 511) // 512
            p.step("vector", lambda e, Wq_=Wq_: e.tensor_add(
                mb_s[:], bb_s[:, Wq_ - 256 : Wq_], mask_s[:]),
                reads=(b_bb, b_mask), writes=(b_mb,), tag=f"mbadd{kk}")
            for m in range(nac):
                n0 = m * 512
                n1 = min(n0 + 512, Wq_)
                mb2 = m % 2
                p.step("tensor", lambda e, kk=kk, mb2=mb2, n0=n0, n1=n1: e.matmul(
                    ps_ba[:, mb2, 0 : n1 - n0], qu_fm[:, kk * 128 : (kk + 1) * 128],
                    k_fm[:, n0:n1], start=True, stop=True, skip_group_check=True),
                    reads=(b_qu[kk // 4], b_kfm[n0 // 512]), writes=(b_psba[mb2],), tag=f"acmm{kk}.{m}")
                if n1 <= Wq_ - 256:
                    p.step("vector", lambda e, mb2=mb2, n0=n0, n1=n1: e.tensor_add(
                        a_s[:, n0:n1], ps_ba[:, mb2, 0 : n1 - n0], bb_s[:, n0:n1]),
                        reads=(b_psba[mb2], b_bb), writes=(b_a,), tag=f"acev{kk}.{m}")
                else:
                    lo = n0
                    mid = max(n0, Wq_ - 256)
                    if mid > lo:
                        p.step("vector", lambda e, mb2=mb2, lo=lo, mid=mid, n0=n0: e.tensor_add(
                            a_s[:, lo:mid], ps_ba[:, mb2, lo - n0 : mid - n0], bb_s[:, lo:mid]),
                            reads=(b_psba[mb2], b_bb), writes=(b_a,), tag=f"acevl{kk}.{m}")
                    p.step("vector", lambda e, mb2=mb2, mid=mid, n0=n0, n1=n1, Wq_=Wq_: e.tensor_add(
                        a_s[:, mid:n1], ps_ba[:, mb2, mid - n0 : n1 - n0],
                        mb_s[:, mid - (Wq_ - 256) : n1 - (Wq_ - 256)]),
                        reads=(b_psba[mb2], b_mb), writes=(b_a,), tag=f"acevm{kk}.{m}")
            if DEBUG and kk == 5:
                p.dma("sync", dbg["bb15"], bb_s[:], reads=(b_bb,), tag="dbgbb")
                p.dma("sync", dbg["a15"], a_s[:], reads=(b_a,), tag="dbga")
            if kk + 1 < NQ:
                bd_block(kk + 1)
            if kk > 0:
                wv_mms(kk - 1)
            p.step("vector", lambda e, Wq_=Wq_: e.tensor_reduce(
                sm_s[:, 0:1], a_s[:, 0:Wq_], axis=AX.X, op=ALU.max, negate=True),
                reads=(b_a,), writes=(b_sm,), tag=f"max{kk}")
            p.step("scalar", lambda e, Wq_=Wq_: e.activation(
                out=c_s[:, 0:Wq_], in_=a_s[:, 0:Wq_], func=AF.Exp, bias=sm_s[:, 0:1],
                accum_out=sm_s[:, 1:2]),
                reads=(b_a, b_sm), writes=(b_c, b_sm), tag=f"exp{kk}")
            p.step("vector", lambda e: e.reciprocal(sm_s[:, 2:3], sm_s[:, 1:2]),
                   reads=(b_sm,), writes=(b_sm,), tag=f"rcp{kk}")
            if kk > 0:
                wv_gate(kk - 1)
            p.dma("scalar", gt2_s[:], g_park[kk * 128 : (kk + 1) * 128, :],
                  deps=(b_gpark[kk].w,), writes=(b_gt2,), tag=f"gld{kk}")
            nrounds = (nch + 7) // 8
            for r in range(nrounds):
                c0 = r * 8
                c1 = min(c0 + 8, nch)
                rb = r % 2
                bview = ps_ba[:, rb, :].bitcast(BF16)
                for ci in range(c0, c1):
                    p.step("tensor", lambda e, ci=ci, c0=c0, rb=rb, bview=bview: e.transpose(
                        bview[:, (ci - c0) * 128 : (ci - c0 + 1) * 128],
                        c_s[:, ci * 128 : (ci + 1) * 128], ident_s[:]),
                        reads=(b_c, b_ident), writes=(b_psba[rb],), tag=f"ptt{kk}.{ci}")
                p.step("vector", lambda e, c0=c0, c1=c1, rb=rb, bview=bview: e.tensor_copy(
                    pt_s[:, c0:c1, :], bview[:, 0 : (c1 - c0) * 128]),
                    reads=(b_psba[rb],), writes=(b_pt,), tag=f"ptev{kk}.{r}")
            for ci in range(nch):
                for n2 in range(2):
                    p.step("tensor", lambda e, ci=ci, n2=n2, nch=nch: e.matmul(
                        ps_px[:, n2, :], pt_s[:, ci, :],
                        xt_tm[:, ci, n2 * 512 : (n2 + 1) * 512],
                        start=(ci == 0), stop=(ci == nch - 1), skip_group_check=True),
                        reads=(b_pt, b_xt[ci]), writes=(b_pspx,), tag=f"pxmm{kk}.{ci}.{n2}")
            if DEBUG and kk == 5:
                p.dma("sync", dbg["c15"], c_s[:], reads=(b_c,), tag="dbgc")
            p.step("scalar", lambda e: e.activation(
                out=pxn_s[:], in_=ps_px[:, :, :], func=AF.Identity, scale=sm_s[:, 2:3]),
                reads=(b_pspx, b_sm), writes=(b_pxn,), tag=f"pxn{kk}")
            if DEBUG and kk == 5:
                p.dma("sync", dbg["px15"], pxn_s[:], reads=(b_pxn,), tag="dbgpx")
            for hh in range(2):
                rb = hh % 2
                bview = ps_ba[:, rb, :].bitcast(BF16)
                for c2 in range(4):
                    c = hh * 4 + c2
                    p.step("tensor", lambda e, c=c, c2=c2, bview=bview: e.transpose(
                        bview[:, c2 * 128 : (c2 + 1) * 128], pxn_s[:, c * 128 : (c + 1) * 128], ident_s[:]),
                        reads=(b_pxn, b_ident), writes=(b_psba[rb],), tag=f"pxt{kk}.{c}")
                p.step("vector", lambda e, hh=hh, bview=bview, rb=rb: e.tensor_copy(
                    pxf_s[:, hh * 4 : (hh + 1) * 4, :], bview[:, 0:512]),
                    reads=(b_psba[rb],), writes=(b_pxf,), tag=f"pxtev{kk}.{hh}")

        if DEBUG:
            for kk in range(NQ):
                p.dma("scalar", dbg["z"][kk * 128 : (kk + 1) * 128, :],
                      z_park[kk * 128 : (kk + 1) * 128, :], deps=(b_zpark[kk].w,), tag="dbgz")
                p.dma("scalar", dbg["z2"][kk * 128 : (kk + 1) * 128, :],
                      z_park2[kk * 128 : (kk + 1) * 128, :], deps=(b_zpark[kk].w,), tag="dbgz2")

        wv_mms(NQ - 1)
        wv_gate(NQ - 1)

    bctx.close()

    octx = contextlib.ExitStack()
    w1_b = octx.enter_context(nc.sbuf_tensor("w1_b", [128, 8, HD], BF16))
    w2_b = octx.enter_context(nc.sbuf_tensor("w2_b", [128, 32, D], BF16))
    r1a1_b = octx.enter_context(nc.sbuf_tensor("r1a1_b", [128, 8, 32], BF16))
    r2a2_b = octx.enter_context(nc.sbuf_tensor("r2a2_b", [128, 32, 32], BF16))
    b1l_s = octx.enter_context(nc.sbuf_tensor("b1l_s", [128, 32], F32))
    b1f_b = octx.enter_context(nc.sbuf_tensor("b1f_b", [24, HD], BF16))
    b2f_b = octx.enter_context(nc.sbuf_tensor("b2f_b", [24, D], BF16))
    b2r_b = octx.enter_context(nc.sbuf_tensor("b2r_b", [1, D], BF16))
    ones1_s = octx.enter_context(nc.sbuf_tensor("ones1_s", [1, 128], BF16))
    b_w1, b_w2, b_r1, b_r2, b_b1l, b_b1f, b_b2f, b_b2r, b_ones = (
        Buf(), Buf(), Buf(), Buf(), Buf(), Buf(), Buf(), Buf(), Buf())

    p.next_phase()
    p.fence()
    with nc.sbuf_tensor("wo_b", [128, 16, D], BF16) as wo_b, \
         nc.sbuf_tensor("zl_s", [128, 2, V], BF16) as zl_s, \
         nc.sbuf_tensor("zf_s", [128, 2, 16, 128], BF16) as zf_s, \
         nc.sbuf_tensor("xo2_s", [128, 2, D], F32) as xo2_s, \
         nc.sbuf_tensor("xn_s", [128, D], F32) as xn_s, \
         nc.psum_tensor("ps_zt", [128, 2, 8, 128], BF16) as ps_zt, \
         nc.psum_tensor("ps_o", [128, 2, 2, 512], F32) as ps_o:

        b_wo = Buf()
        b_zl = [Buf(), Buf()]
        b_zf = [Buf(), Buf()]
        b_xoo2 = [Buf(), Buf()]
        b_xn = Buf()
        b_pszt = [Buf(), Buf()]
        b_pso = [Buf(), Buf()]

        p.dma("gpsimd", r1a1_b[:], r1a1.rearrange("(c p) f -> p c f", p=128), writes=(b_r1,), tag="ldr1")
        p.dma("gpsimd", r2a2_b[:], r2a2.rearrange("(c p) f -> p c f", p=128), writes=(b_r2,), tag="ldr2")
        p.dma("sync", b1l_s[:], b1L, writes=(b_b1l,), tag="ldb1l")
        p.dma("gpsimd", b1f_b[:], b1f, writes=(b_b1f,), tag="ldb1f")
        p.dma("gpsimd", b2f_b[:], b2f, writes=(b_b2f,), tag="ldb2f")
        p.dma("gpsimd", b2r_b[:], b2L, writes=(b_b2r,), tag="ldb2r")
        p.dma("sync", ones1_s[:], ones1, writes=(b_ones,), tag="ldones")

        def p2b_load(kk):
            kb = kk % 2
            p.dma("sync", zl_s[:, kb], z_park[kk * 128 : (kk + 1) * 128, :],
                  deps=(b_zpark[kk].w,), reads=(b_zf[kb],), writes=(b_zl[kb],), tag=f"zld{kk}")
            p.dma("scalar", xo_s[:], x_own[kk * 128 : (kk + 1) * 128, :],
                  writes=(b_xoo,), tag=f"xold{kk}") if False else None

        def p2b_xload(kk, slot):
            p.dma("scalar", xo2_s[:, slot], x_own[kk * 128 : (kk + 1) * 128, :],
                  writes=(b_xoo2[slot],), tag=f"xold{kk}")

        p2b_load(0)
        p2b_xload(0, 0)
        p2b_load(1)
        p2b_xload(1, 1)
        p.dma("scalar", wo_b[:], wo_st.rearrange("(c p) f -> p c f", p=128),
              deps=(b_wost.w,), writes=(b_wo,), tag="ldwo")
        p.dma("scalar", w1_b[:], w1_st.rearrange("(c p) f -> p c f", p=128),
              deps=(b_w1st.w,), writes=(b_w1,), tag="ldw1")
        p.dma("sync", w2_b[:], w2_st.rearrange("(c p) f -> p c f", p=128),
              deps=(b_w2st.w,), writes=(b_w2,), tag="ldw2")
        for kk in range(NQ):
            kb = kk % 2
            for h in range(2):
                for c in range(8):
                    cc = h * 8 + c
                    p.step("tensor", lambda e, kb=kb, h=h, c=c, cc=cc: e.transpose(
                        ps_zt[:, kb, c, :], zl_s[:, kb, cc * 128 : (cc + 1) * 128], ident_s[:]),
                        reads=(b_zl[kb], b_ident), writes=(b_pszt[kb],), tag=f"ztt{kk}.{cc}")
                p.step("vector", lambda e, kb=kb, h=h: e.tensor_copy(
                    zf_s[:, kb, h * 8 : (h + 1) * 8, :], ps_zt[:, kb]),
                    reads=(b_pszt[kb],), writes=(b_zf[kb],), tag=f"ztev{kk}.{h}")
            if kk + 2 < NQ:
                p2b_load(kk + 2)
            for cc in range(16):
                for n2 in range(2):
                    p.step("tensor", lambda e, kb=kb, cc=cc, n2=n2: e.matmul(
                        ps_o[:, kb, n2, :], zf_s[:, kb, cc, :], wo_b[:, cc, n2 * 512 : (n2 + 1) * 512],
                        start=(cc == 0), stop=(cc == 15), skip_group_check=True),
                        reads=(b_zf[kb], b_wo), writes=(b_pso[kb],), tag=f"omm{kk}.{cc}.{n2}")
            p.step("vector", lambda e, kb=kb, kk=kk: e.tensor_add(
                xn_s[:], ps_o[:, kb].rearrange("p a b -> p (a b)"), xo2_s[:, kk % 2]),
                reads=(b_pso[kb], b_xoo2[kk % 2]), writes=(b_xn,), tag=f"resid{kk}")
            p.dma("sync", xnew_d[kk * 128 : (kk + 1) * 128, :], xn_s[:],
                  reads=(b_xn,), writes=(b_xnew[kk],), tag=f"xnst{kk}")
            if kk + 2 < NQ:
                p2b_xload(kk + 2, (kk + 2) % 2)

    if DEBUG:
        p.next_phase()
        for kk in range(NQ):
            p.dma("scalar", dbg["xnew"][kk * 128 : (kk + 1) * 128, :],
                  xnew_d[kk * 128 : (kk + 1) * 128, :], deps=(b_xnew[kk].w,), tag="dbgxn")

    octx_close_marker = None

    p.next_phase()
    p.fence()
    p3ctx = contextlib.ExitStack()
    x2n = p3ctx.enter_context(nc.sbuf_tensor("x2n", [128, 2, D], F32))
    ybf = p3ctx.enter_context(nc.sbuf_tensor("ybf", [128, D], BF16))
    yfm = p3ctx.enter_context(nc.sbuf_tensor("yfm", [128, 8, 512], BF16))
    hfm = p3ctx.enter_context(nc.sbuf_tensor("hfm", [128, 32, 512], BF16))
    rz1 = p3ctx.enter_context(nc.sbuf_tensor("rz1", [32, 512], BF16))
    rt = p3ctx.enter_context(nc.sbuf_tensor("rt", [128, 4, 32], BF16))
    gatef = p3ctx.enter_context(nc.sbuf_tensor("gatef", [128, 4, 4], F32))
    gex = p3ctx.enter_context(nc.sbuf_tensor("gex", [128, 4, 24], F32))
    zgj = p3ctx.enter_context(nc.sbuf_tensor("zgj", [128, 4, 24], BF16))
    zg1t = p3ctx.enter_context(nc.sbuf_tensor("zg1t", [24, 512], BF16))
    zg2t = p3ctx.enter_context(nc.sbuf_tensor("zg2t", [24, 512], BF16))
    lnsc3 = p3ctx.enter_context(nc.sbuf_tensor("lnsc3", [128, 4, 4], F32))
    outst = p3ctx.enter_context(nc.sbuf_tensor("outst", [128, D], F32))
    sm3 = p3ctx.enter_context(nc.sbuf_tensor("sm3", [128, 4, 2], F32))
    ps3_tr = p3ctx.enter_context(nc.psum_tensor("ps3_tr", [128, 8, 128], BF16))
    ps_r = p3ctx.enter_context(nc.psum_tensor("ps_r", [128, 512], F32))
    ps_h = p3ctx.enter_context(nc.psum_tensor("ps_h", [128, 4, 512], F32))
    ps_w2 = p3ctx.enter_context(nc.psum_tensor("ps_w2", [128, 1, 2, 512], F32))
    if True:
        b_x2n = Buf()
        b_x2n = Buf()
        b_ybf = Buf()
        b_yfm = Buf()
        b_hfm = Buf()
        b_rz1 = Buf()
        b_rt = Buf()
        b_gate = Buf()
        b_gex = Buf()
        b_zgj = Buf()
        b_zg1t = Buf()
        b_zg2t = Buf()
        b_ln3 = [Buf() for _ in range(4)]
        b_outst = Buf()
        b_sm3 = Buf()
        b_pstr3 = Buf()
        b_psr = Buf()
        b_psh = [Buf(), Buf(), Buf(), Buf()]
        b_psw2 = [Buf()]

        def router_dance(g3, zgT, b_zgT, b_src_fm, what):
            p.step("vector", lambda e: e.tensor_copy(rz1[:], ps_r[0:32, :]),
                   reads=(b_psr,), writes=(b_rz1,), tag=f"rzev{what}{g3}")
            for j in range(4):
                p.step("tensor", lambda e, j=j: e.transpose(
                    ps3_tr[:, j, 0:32], rz1[:, j * 128 : (j + 1) * 128], ident_s[0:32, 0:32]),
                    reads=(b_rz1, b_ident), writes=(b_pstr3,), tag=f"rtt{what}{g3}.{j}")
            p.step("vector", lambda e: e.tensor_copy(rt[:, :, :], ps3_tr[:, 0:4, 0:32]),
                   reads=(b_pstr3,), writes=(b_rt,), tag=f"rtev{what}{g3}")
            for j in range(4):
                p.step("vector", lambda e, j=j: e.tensor_reduce(
                    sm3[:, j, 0:1], rt[:, j, 0:3], axis=AX.X, op=ALU.max, negate=True),
                    reads=(b_rt,), writes=(b_sm3,), tag=f"rmax{what}{g3}.{j}")
            for j in range(4):
                p.step("scalar", lambda e, j=j: e.activation(
                    out=gatef[:, j, 0:3], in_=rt[:, j, 0:3], func=AF.Exp, bias=sm3[:, j, 0:1],
                    accum_out=sm3[:, j, 1:2]),
                    reads=(b_rt, b_sm3), writes=(b_gate, b_sm3), tag=f"rexp{what}{g3}.{j}")
            p.step("vector", lambda e: e.reciprocal(sm3[:, :, 1], sm3[:, :, 1]),
                   reads=(b_sm3,), writes=(b_sm3,), tag=f"rrcp{what}{g3}")
            for j in range(4):
                p.step("vector", lambda e, j=j: e.tensor_scalar_mul(
                    gatef[:, j, 0:3], rt[:, j, 0:3] if False else gatef[:, j, 0:3], sm3[:, j, 1:2]),
                    reads=(b_gate, b_sm3), writes=(b_gate,), tag=f"rnorm{what}{g3}.{j}")
            for j in range(4):
                for ee in range(3):
                    p.step("vector", lambda e, j=j, ee=ee: e.tensor_copy(
                        gex[:, j, ee * 8 : (ee + 1) * 8],
                        gatef[:, j, ee : ee + 1].broadcast_to([128, 8])),
                        reads=(b_gate,), writes=(b_gex,), tag=f"rgex{what}{g3}.{j}.{ee}")
            for j in range(4):
                p.step("vector", lambda e, j=j: e.scalar_tensor_tensor(
                    out=zgj[:, j, :], in0=rt[:, j, 3:27], scalar=0.125, in1=gex[:, j, :],
                    op0=ALU.mult, op1=ALU.mult),
                    reads=(b_rt, b_gex), writes=(b_zgj,), tag=f"rzg{what}{g3}.{j}")
            for j in range(4):
                p.step("tensor", lambda e, j=j: e.transpose(
                    ps3_tr[0:24, 4 + j, :], zgj[:, j, 0:24], ident_s[:]),
                    reads=(b_zgj, b_ident), writes=(b_pstr3,), tag=f"zgt{what}{g3}.{j}")
            p.step("vector", lambda e, zgT=zgT: e.tensor_copy(
                zgT.rearrange("q (j x) -> q j x", x=128), ps3_tr[0:24, 4:8, :]),
                reads=(b_pstr3,), writes=(b_zgT,), tag=f"zgev{what}{g3}")

        def load_xn(g3, half, tag):
            rows0 = g3 * 512 + half * 256
            src3 = xnew_d[rows0 : rows0 + 256, :].rearrange("(j p) d -> p j d", p=128)
            p.dma("sync", x2n[:], src3, writes=(b_x2n,),
                  deps=[b_xnew[rows0 // 128 + jj].w for jj in range(2)], tag=f"{tag}{g3}.{half}")

        for g3 in range(4):
            for j in range(4):
                if j % 2 == 0:
                    load_xn(g3, j // 2, "ldxn")
                xt3 = x2n[:, j % 2, :]
                sc = lnsc3[:, j, :]
                p.step("vector", lambda e, xt3=xt3, sc=sc: e.reduce_sum(sc[:, 0:1], xt3, axis=AX.X),
                       reads=(b_x2n,), writes=(b_ln3[j],), tag=f"fsum{g3}.{j}")
                p.step("vector", lambda e, sc=sc: e.tensor_scalar_mul(sc[:, 1:2], sc[:, 0:1], -1.0 / D),
                       reads=(b_ln3[j],), writes=(b_ln3[j],), tag=f"fnegm{g3}.{j}")
                p.step("scalar", lambda e, xt3=xt3, sc=sc: e.activation(
                    out=ybf[:], in_=xt3, func=AF.Square, bias=sc[:, 1:2], scale=1.0, accum_out=sc[:, 2:3]),
                    reads=(b_x2n, b_ln3[j]), writes=(b_ln3[j], b_ybf), tag=f"fsq{g3}.{j}")
                p.step("scalar", lambda e, sc=sc: e.activation(
                    out=sc[:, 3:4], in_=sc[:, 2:3], func=AF.Ln, bias=1e-5, scale=1.0 / D),
                    reads=(b_ln3[j], b_consts), writes=(b_ln3[j],), tag=f"fln{g3}.{j}")
                p.step("scalar", lambda e, sc=sc: e.activation(
                    out=sc[:, 3:4], in_=sc[:, 3:4], func=AF.Exp, scale=-0.5),
                    reads=(b_ln3[j], b_consts), writes=(b_ln3[j],), tag=f"fexp{g3}.{j}")
                p.step("vector", lambda e, sc=sc: e.tensor_mul(sc[:, 1:2], sc[:, 1:2], sc[:, 3:4]),
                       reads=(b_ln3[j],), writes=(b_ln3[j],), tag=f"fnegms{g3}.{j}")
                p.step("scalar", lambda e, xt3=xt3, sc=sc: e.activation(
                    out=ybf[:], in_=xt3, func=AF.Identity, bias=sc[:, 1:2], scale=sc[:, 3:4]),
                    reads=(b_ln3[j], b_x2n), writes=(b_ybf,), tag=f"fy{g3}.{j}")
                for c in range(8):
                    p.step("tensor", lambda e, c=c: e.transpose(
                        ps3_tr[:, c, :], ybf[:, c * 128 : (c + 1) * 128], ident_s[:]),
                        reads=(b_ybf, b_ident), writes=(b_pstr3,), tag=f"ftr{g3}.{j}.{c}")
                p.step("vector", lambda e, j=j: e.tensor_copy(
                    yfm[:, :, j * 128 : (j + 1) * 128], ps3_tr[:, :, :]),
                    reads=(b_pstr3,), writes=(b_yfm,), tag=f"ftrev{g3}.{j}")
            for c in range(8):
                p.step("tensor", lambda e, c=c: e.matmul(
                    ps_r[0:32, :], r1a1_b[:, c, :], yfm[:, c, :], start=(c == 0), stop=(c == 7),
                    skip_group_check=True),
                    reads=(b_r1, b_yfm), writes=(b_psr,), tag=f"r1mm{g3}.{c}")
            router_dance(g3, zg1t, b_zg1t, b_yfm, "a")
            for hc in range(32):
                hb = hc % 4
                for c in range(8):
                    p.step("tensor", lambda e, hc=hc, hb=hb, c=c: e.matmul(
                        ps_h[:, hb, :], w1_b[:, c, hc * 128 : (hc + 1) * 128], yfm[:, c, :],
                        start=(c == 0), stop=False, skip_group_check=True),
                        reads=(b_w1, b_yfm), writes=(b_psh[hb],), tag=f"w1mm{g3}.{hc}.{c}")
                p.step("tensor", lambda e, hc=hc, hb=hb: e.matmul(
                    ps_h[:, hb, :], b1f_b[0:24, hc * 128 : (hc + 1) * 128], zg1t[:],
                    start=False, stop=True, skip_group_check=True),
                    reads=(b_b1f, b_zg1t), writes=(b_psh[hb],), tag=f"b1mm{g3}.{hc}")
                p.step("scalar", lambda e, hc=hc, hb=hb: e.activation(
                    out=hfm[:, hc, :], in_=ps_h[:, hb, :], func=AF.Relu, bias=b1l_s[:, hc : hc + 1]),
                    reads=(b_psh[hb], b_b1l), writes=(b_hfm,), tag=f"hev{g3}.{hc}")
            for hc in range(32):
                p.step("tensor", lambda e, hc=hc: e.matmul(
                    ps_r[0:32, :], r2a2_b[:, hc, :], hfm[:, hc, :], start=(hc == 0), stop=(hc == 31),
                    skip_group_check=True),
                    reads=(b_r2, b_hfm), writes=(b_psr,), tag=f"r2mm{g3}.{hc}")
            router_dance(g3, zg2t, b_zg2t, b_hfm, "b")
            for j in range(4):
                jb = 0
                if j % 2 == 0:
                    load_xn(g3, j // 2, "rldxn")
                for hc in range(32):
                    for n2 in range(2):
                        p.step("tensor", lambda e, hc=hc, n2=n2, j=j, jb=jb: e.matmul(
                            ps_w2[:, jb, n2, :], hfm[:, hc, j * 128 : (j + 1) * 128],
                            w2_b[:, hc, n2 * 512 : (n2 + 1) * 512],
                            start=(hc == 0), stop=False, skip_group_check=True),
                            reads=(b_w2, b_hfm), writes=(b_psw2[jb],), tag=f"w2mm{g3}.{j}.{hc}.{n2}")
                for n2 in range(2):
                    p.step("tensor", lambda e, n2=n2, j=j, jb=jb: e.matmul(
                        ps_w2[:, jb, n2, :], zg2t[:, j * 128 : (j + 1) * 128], b2f_b[0:24, n2 * 512 : (n2 + 1) * 512],
                        start=False, stop=False, skip_group_check=True),
                        reads=(b_zg2t, b_b2f), writes=(b_psw2[jb],), tag=f"b2mm{g3}.{j}.{n2}")
                    p.step("tensor", lambda e, n2=n2, jb=jb: e.matmul(
                        ps_w2[:, jb, n2, :], ones1_s[:], b2r_b[0:1, n2 * 512 : (n2 + 1) * 512],
                        start=False, stop=True, skip_group_check=True),
                        reads=(b_ones, b_b2r), writes=(b_psw2[jb],), tag=f"bbmm{g3}.{j}.{n2}")
                p.step("vector", lambda e, j=j, jb=jb: e.tensor_add(
                    outst[:], ps_w2[:, jb].rearrange("p a b -> p (a b)"), x2n[:, j % 2, :]),
                    reads=(b_psw2[jb], b_x2n), writes=(b_outst,), tag=f"oev{g3}.{j}")
                p.dma("sync", out_own[(g3 * 4 + j) * 128 : (g3 * 4 + j + 1) * 128, :], outst[:],
                      reads=(b_outst,), tag=f"ost{g3}.{j}")

    p3ctx.close()
    octx.close()

    final_cnt = p.finalize(nc, sems)

    with nc.Block() as blk:
        def fin(e):
            for eng in ("sync", "scalar", "gpsimd"):
                for lane in QLANES[eng]:
                    if final_cnt[lane]:
                        e.wait_ge(sems[lane], final_cnt[lane])

        blk.sync(fin)

    ctx.close()
    return nc



def _inputs_for_cores(ins):
    x = np.asarray(ins["L0_attn_x"], np.float32)

    def f32(a):
        return np.ascontiguousarray(np.asarray(a, np.float32))

    A1 = f32(ins["L0_ffn_A1"]).transpose(1, 0, 2).reshape(D, 24)
    A2 = f32(ins["L0_ffn_A2"]).transpose(1, 0, 2).reshape(HD, 24)
    common = {
        "wq": f32(ins["L0_attn_W_q"]),
        "wk": f32(ins["L0_attn_W_k"]),
        "wr": f32(ins["L0_attn_W_r"]),
        "wv": f32(ins["L0_attn_W_v"]),
        "wg": f32(ins["L0_attn_W_g"]),
        "wo": f32(ins["L0_attn_W_o"]),
        "w1": f32(ins["L0_ffn_W1"]),
        "w2": f32(ins["L0_ffn_W2"]),
        "g_lnL": np.ascontiguousarray(f32(ins["L0_attn_g_ln"]).reshape(8, 128).T),
        "xl_u": f32(ins["L0_attn_xl_u"]).reshape(128, 1),
        "xl_v": f32(ins["L0_attn_xl_v"]).reshape(128, 1),
        "r1a1": np.ascontiguousarray(np.concatenate(
            [f32(ins["L0_ffn_R1"]), A1, np.zeros((D, 5), np.float32)], axis=1)),
        "b1f": f32(ins["L0_ffn_B1"]).reshape(24, HD),
        "b1L": np.ascontiguousarray(f32(ins["L0_ffn_b1"]).reshape(32, 128).T),
        "r2a2": np.ascontiguousarray(np.concatenate(
            [f32(ins["L0_ffn_R2"]), A2, np.zeros((HD, 5), np.float32)], axis=1)),
        "b2f": f32(ins["L0_ffn_B2"]).reshape(24, D),
        "b2L": f32(ins["L0_ffn_b2"]).reshape(1, D),
    }
    masks = [_mask256(0), _mask256(1)]
    sins = [_sinusoid_fm(0), _sinusoid_fm(1)]
    in_maps = []
    for core in range(8):
        b, parity = core // 2, core % 2
        xb = np.ascontiguousarray(x[b])
        xow = np.ascontiguousarray(xb.reshape(NT, 128, D)[parity::2].reshape(NQ * 128, D))
        m = dict(common)
        m["x_full"] = xb
        m["x_own"] = xow
        m["mask256"] = masks[parity]
        m["sin_fm"] = sins[parity]
        in_maps.append(m)
    return in_maps


def kernel(**inputs):
    in_maps = _inputs_for_cores(inputs)
    nc = build_program()
    res = run_bass_kernel_spmd(nc, in_maps, core_ids=list(range(8)))
    out = np.empty((B, T, D), np.float32)
    for core in range(8):
        b, parity = core // 2, core % 2
        o = np.asarray(res.results[core]["out_own"], np.float32).reshape(NQ, 128, D)
        out.reshape(B, NT, 128, D)[b, parity::2] = o
    return out


# version 14 (final)
# speedup vs baseline: 1.1709x; 1.0126x over previous
import contextlib

import numpy as np
import ml_dtypes
import concourse.bass as bass
import concourse.mybir as mybir
from concourse.bass_utils import run_bass_kernel_spmd

F32 = mybir.dt.float32
BF16 = mybir.dt.bfloat16
I32 = mybir.dt.int32
AF = mybir.ActivationFunctionType
ALU = mybir.AluOpType
AX = mybir.AxisListType
ds = bass.ds

B, T, D, KD, V, HD = 4, 4096, 1024, 128, 2048, 4096
NT = T // 128
NQ = 16
TAU = float(np.sqrt(np.float32(KD)))
EPS1, EPS2 = 1e-5, 1e-6
PW = 4608
PR = PW - 1
NEG = -1e30

DEBUG = False


ENGS = ("tensor", "scalar", "vector", "sync", "gpsimd")
NLANES = 4
QSEM = {"sync": "q_s", "scalar": "q_a", "gpsimd": "q_g", "vector": "q_v", "tensor": "q_t"}
QLANES = {e: [f"{q}{i}" for i in range(NLANES)] for e, q in QSEM.items()}
ESEM = {"tensor": "s_pe", "scalar": "s_act", "vector": "s_dve", "sync": "s_sp", "gpsimd": "s_pl"}
ALL_SEMS = sorted({s for ls in QLANES.values() for s in ls} | set(ESEM.values()))


class Step:
    __slots__ = ("i", "eng", "emit", "deps", "raw", "waw", "dma", "phase", "inc", "tag")

    def __init__(self, i, eng, emit, deps, raw, waw, dma, phase, tag):
        self.i = i
        self.eng = eng
        self.emit = emit
        self.deps = deps
        self.raw = raw
        self.waw = waw
        self.dma = dma
        self.phase = phase
        self.inc = None
        self.tag = tag


class Buf:
    __slots__ = ("w", "rs")

    def __init__(self):
        self.w = None
        self.rs = []


class Prog:
    def __init__(self):
        self.steps = []
        self.phase = 0

    def next_phase(self):
        self.phase += 1

    def fence(self):
        dmas = [st for st in self.steps if st.dma]
        for eng in ENGS:
            self.step(eng, lambda e: None, deps=dmas, tag="fence")

    def step(self, eng, emit, reads=(), writes=(), deps=(), dma=False, tag=""):
        dd = []
        raw = set()
        waw = set()
        for b in reads:
            if b.w is not None:
                dd.append(b.w)
                raw.add(b.w.i)
        for b in writes:
            if b.w is not None:
                dd.append(b.w)
                waw.add(b.w.i)
            dd.extend(b.rs)
        for d in deps:
            if d is not None:
                dd.append(d)
                raw.add(d.i)
        st = Step(len(self.steps), eng, emit, dd, raw, waw, dma, self.phase, tag)
        self.steps.append(st)
        for b in reads:
            b.rs.append(st)
        for b in writes:
            b.w = st
            b.rs = []
        return st

    def dma(self, eng, out_ap, in_ap, reads=(), writes=(), deps=(), tag=""):
        def emit(e, o=out_ap, i=in_ap):
            return e.dma_start(out=o() if callable(o) else o, in_=i() if callable(i) else i)

        return self.step(eng, emit, reads=reads, writes=writes, deps=deps, dma=True, tag=tag)

    def _needs_wait(self, st, d):
        if d.inc is None:
            return False
        if d.dma:
            return True
        if d.phase != st.phase:
            return False
        if d.eng != st.eng:
            return True
        if st.dma:
            return True
        return (d.i in st.raw or d.i in st.waw) and st.eng in ("vector", "scalar", "gpsimd")

    def finalize(self, nc, sems):
        consumed = [False] * len(self.steps)
        for st in self.steps:
            for d in st.deps:
                if d.dma:
                    consumed[d.i] = True
                elif d.phase != st.phase:
                    pass
                elif d.eng != st.eng or st.dma or (
                    (d.i in st.raw or d.i in st.waw)
                    and st.eng in ("vector", "scalar", "gpsimd")
                ):
                    consumed[d.i] = True
        cnt = {name: 0 for name in sems}
        qrr = {e: 0 for e in QSEM}
        for st in self.steps:
            if st.dma:
                lane = QLANES[st.eng][qrr[st.eng] % NLANES]
                qrr[st.eng] += 1
                cnt[lane] += 16
                st.inc = (lane, cnt[lane])
            elif consumed[st.i]:
                s = ESEM[st.eng]
                cnt[s] += 1
                st.inc = (s, cnt[s])
        nph = self.phase + 1
        by_phase = [[] for _ in range(nph)]
        for st in self.steps:
            by_phase[st.phase].append(st)
        for ph in range(nph):
            plist = by_phase[ph]
            if not plist:
                continue
            with nc.Block() as blk:
                for eng in ENGS:
                    mine = [st for st in plist if st.eng == eng]
                    if not mine:
                        continue

                    def make(mine=mine, eng=eng):
                        def f(e):
                            seen = {}
                            for st in mine:
                                for d in st.deps:
                                    if not self._needs_wait(st, d):
                                        continue
                                    sname, val = d.inc
                                    if seen.get(sname, -1) >= val:
                                        continue
                                    e.wait_ge(sems[sname], val)
                                    seen[sname] = val
                                if st.dma:
                                    lname, lval = st.inc
                                    if lval > 16 and seen.get(lname, -1) < lval - 16:
                                        e.wait_ge(sems[lname], lval - 16)
                                        seen[lname] = lval - 16
                                ins = st.emit(e)
                                if st.inc is not None:
                                    amt = 16 if st.dma else 1
                                    ins.then_inc(sems[st.inc[0]], amt)

                        return f

                    getattr(blk, eng)(make())
        return cnt



NSIN = 4352

def _sinusoid_fm(parity):
    LAM = 10000.0
    pos = np.arange(T, dtype=np.float32)
    inv_lams = (1.0 / LAM ** (np.arange(0, D, 2, dtype=np.float32) / D)).astype(np.float32)
    pre = pos[:, None] * inv_lams[None, :]
    cat = np.concatenate([np.sin(pre), np.cos(pre)], axis=-1).astype(np.float32)
    S = np.flip(cat, axis=0)
    rolled = np.roll(S, 128 * parity, axis=0)
    table = np.concatenate([rolled, rolled[:NSIN - T]], axis=0)
    return np.ascontiguousarray(table.T).reshape(8, 128, NSIN).astype(ml_dtypes.bfloat16)


def _mask256(parity):
    pp = np.arange(128)[:, None]
    cc = np.arange(256)[None, :]
    keep = cc <= pp + 128 * parity
    return np.where(keep, np.float32(0.0), np.float32(NEG)).astype(np.float32)



def build_program():
    nc = bass.Bass("TRN2", target_bir_lowering=False)

    x_full = nc.dram_tensor("x_full", [T, D], F32, kind="ExternalInput").ap()
    x_own = nc.dram_tensor("x_own", [NQ * 128, D], F32, kind="ExternalInput").ap()
    wq = nc.dram_tensor("wq", [D, KD], F32, kind="ExternalInput").ap()
    wk = nc.dram_tensor("wk", [D, KD], F32, kind="ExternalInput").ap()
    wr = nc.dram_tensor("wr", [D, KD], F32, kind="ExternalInput").ap()
    wv = nc.dram_tensor("wv", [D, V], F32, kind="ExternalInput").ap()
    wg = nc.dram_tensor("wg", [D, V], F32, kind="ExternalInput").ap()
    wo = nc.dram_tensor("wo", [V, D], F32, kind="ExternalInput").ap()
    w1 = nc.dram_tensor("w1", [D, HD], F32, kind="ExternalInput").ap()
    w2 = nc.dram_tensor("w2", [HD, D], F32, kind="ExternalInput").ap()
    g_lnL = nc.dram_tensor("g_lnL", [128, 8], F32, kind="ExternalInput").ap()
    xl_u = nc.dram_tensor("xl_u", [128, 1], F32, kind="ExternalInput").ap()
    xl_v = nc.dram_tensor("xl_v", [128, 1], F32, kind="ExternalInput").ap()
    r1a1 = nc.dram_tensor("r1a1", [D, 32], F32, kind="ExternalInput").ap()
    b1f = nc.dram_tensor("b1f", [24, HD], F32, kind="ExternalInput").ap()
    b1L = nc.dram_tensor("b1L", [128, 32], F32, kind="ExternalInput").ap()
    r2a2 = nc.dram_tensor("r2a2", [HD, 32], F32, kind="ExternalInput").ap()
    b2f = nc.dram_tensor("b2f", [24, D], F32, kind="ExternalInput").ap()
    b2L = nc.dram_tensor("b2L", [1, D], F32, kind="ExternalInput").ap()
    sin_fm = nc.dram_tensor("sin_fm", [8, 128, NSIN], BF16, kind="ExternalInput").ap()
    mask256 = nc.dram_tensor("mask256", [128, 256], F32, kind="ExternalInput").ap()

    out_own = nc.dram_tensor("out_own", [NQ * 128, D], F32, kind="ExternalOutput").ap()

    ident = nc.inline_tensor(np.eye(128, dtype=ml_dtypes.bfloat16), name="ident").ap()
    ones1 = nc.inline_tensor(np.ones((1, 128), dtype=ml_dtypes.bfloat16), name="ones1").ap()

    g_park = nc.dram_tensor("g_park", [NQ * 128, V], BF16, kind="Internal").ap()
    z_park = nc.dram_tensor("z_park", [NQ * 128, V], BF16, kind="Internal").ap()
    z_park2 = nc.dram_tensor("z_park2", [NQ * 128, V], BF16, kind="Internal").ap()
    xnew_d = nc.dram_tensor("xnew_d", [NQ * 128, D], F32, kind="Internal").ap()
    wo_st = nc.dram_tensor("wo_st", [V, D], BF16, kind="Internal").ap()
    w1_st = nc.dram_tensor("w1_st", [D, HD], BF16, kind="Internal").ap()
    w2_st = nc.dram_tensor("w2_st", [HD, D], BF16, kind="Internal").ap()
    bounce = nc.dram_tensor("bounce", [129 * PW], F32, kind="Internal")
    bwr = bounce.ap()[0 : 128 * PW].rearrange("(p w) -> p w", w=PW)
    brd = bounce.ap()[0 : 129 * PR].rearrange("(p w) -> p w", w=PR)

    dbg = {}
    if DEBUG:
        dbg["xt"] = nc.dram_tensor("dbg_xt", [T, D], BF16, kind="ExternalOutput").ap()
        dbg["k"] = nc.dram_tensor("dbg_k", [128, T], BF16, kind="ExternalOutput").ap()
        dbg["g"] = nc.dram_tensor("dbg_g", [NQ * 128, V], BF16, kind="ExternalOutput").ap()
        dbg["qu"] = nc.dram_tensor("dbg_qu", [128, NQ * 128], BF16, kind="ExternalOutput").ap()
        dbg["qv"] = nc.dram_tensor("dbg_qv", [128, NQ * 128], BF16, kind="ExternalOutput").ap()
        dbg["xlr"] = nc.dram_tensor("dbg_xlr", [128, NSIN], BF16, kind="ExternalOutput").ap()
        dbg["xnew"] = nc.dram_tensor("dbg_xnew", [NQ * 128, D], F32, kind="ExternalOutput").ap()
        dbg["z"] = nc.dram_tensor("dbg_z", [NQ * 128, V], BF16, kind="ExternalOutput").ap()
        dbg["z2"] = nc.dram_tensor("dbg_z2", [NQ * 128, V], BF16, kind="ExternalOutput").ap()
        dbg["a15"] = nc.dram_tensor("dbg_a15", [128, T], F32, kind="ExternalOutput").ap()
        dbg["c15"] = nc.dram_tensor("dbg_c15", [128, T], BF16, kind="ExternalOutput").ap()
        dbg["bb15"] = nc.dram_tensor("dbg_bb15", [128, T], F32, kind="ExternalOutput").ap()
        dbg["px15"] = nc.dram_tensor("dbg_px15", [128, D], BF16, kind="ExternalOutput").ap()
        dbg["pxf0"] = nc.dram_tensor("dbg_pxf0", [128, D], BF16, kind="ExternalOutput").ap()
        dbg["gt0"] = nc.dram_tensor("dbg_gt0", [128, V], BF16, kind="ExternalOutput").ap()
        dbg["wv0"] = nc.dram_tensor("dbg_wv0", [128, V], F32, kind="ExternalOutput").ap()

    p = Prog()
    ctx = contextlib.ExitStack()
    sems = {n: ctx.enter_context(nc.semaphore(n)) for n in ALL_SEMS}

    ident_s = ctx.enter_context(nc.sbuf_tensor("ident_s", [128, 128], BF16))
    mask_s = ctx.enter_context(nc.sbuf_tensor("mask_s", [128, 256], F32))
    g_sb = ctx.enter_context(nc.sbuf_tensor("g_sb", [128, 8], F32))
    xlu_s = ctx.enter_context(nc.sbuf_tensor("xlu_s", [128, 1], F32))
    xlv_s = ctx.enter_context(nc.sbuf_tensor("xlv_s", [128, 1], F32))
    CONSTS = [EPS1 * EPS2, (1.0 + EPS2) / D, 1e-5, 1.0 / D, -0.5, 1.0 / TAU]
    consts_sb = ctx.enter_context(nc.sbuf_tensor("consts_sb", [128, len(CONSTS)], F32))
    for j, val in enumerate(CONSTS):
        nc.const_aps.register(F32, float(val), consts_sb[:, j : j + 1])

    bctx = contextlib.ExitStack()
    xt_tm = bctx.enter_context(nc.sbuf_tensor("xt_tm", [128, NT, D], BF16))
    k_fm = bctx.enter_context(nc.sbuf_tensor("k_fm", [128, T], BF16))
    qu_fm = bctx.enter_context(nc.sbuf_tensor("qu_fm", [128, NQ * 128], BF16))
    qv_fm = bctx.enter_context(nc.sbuf_tensor("qv_fm", [128, NQ * 128], BF16))
    xlr = bctx.enter_context(nc.sbuf_tensor("xlr", [128, NSIN], BF16))
    wv_b = bctx.enter_context(nc.sbuf_tensor("wv_b", [128, 8, V], BF16))
    wstg = bctx.enter_context(nc.sbuf_tensor("wstg", [128, V], BF16))
    wq_b = bctx.enter_context(nc.sbuf_tensor("wq_b", [128, 8, KD], BF16))
    wk_b = bctx.enter_context(nc.sbuf_tensor("wk_b", [128, 8, KD], BF16))

    b_misc = Buf()
    b_ident, b_mask, b_zeros, b_gsb, b_xlu, b_xlv, b_ro, b_consts = (
        Buf(), Buf(), Buf(), Buf(), Buf(), Buf(), Buf(), Buf())
    b_wq, b_wk, b_wv, b_wr, b_wg = Buf(), Buf(), Buf(), Buf(), Buf()
    b_xt = [Buf() for _ in range(NT)]
    b_kfm = [Buf() for _ in range(8)]
    b_qu = [Buf() for _ in range(4)]
    b_xlr = Buf()
    b_gpark = [Buf() for _ in range(NQ)]
    b_zpark = [Buf() for _ in range(NQ)]
    b_xnew = [Buf() for _ in range(NQ)]

    for dst, src, bf_ in ((ident_s[:], ident, b_ident), (mask_s[:], mask256, b_mask),
                          (g_sb[:], g_lnL, b_gsb), (xlu_s[:], xl_u, b_xlu),
                          (xlv_s[:], xl_v, b_xlv)):
        p.dma("sync", dst, src, writes=(bf_,), tag="p0")
    for j, val in enumerate(CONSTS):
        p.step("vector", lambda e, j=j, val=val: e.memset(consts_sb[:, j : j + 1], float(val)),
               writes=(b_consts,), tag=f"p0c{j}")

    p.next_phase()


    with nc.sbuf_tensor("x4", [128, 2, 4, D], F32) as x4, \
         nc.sbuf_tensor("xfm4", [128, 2, 8, 512], BF16) as xfm4, \
         nc.sbuf_tensor("lnsc", [128, 4, 4], F32) as lnsc, \
         nc.sbuf_tensor("junk", [128, D], BF16) as junk, \
         nc.sbuf_tensor("wr_b", [128, 8, KD], BF16) as wr_b, \
         nc.sbuf_tensor("sin_sb", [128, 2, T], BF16) as sin_sb, \
         nc.sbuf_tensor("sin_rem", [128, 8, NSIN - T], BF16) as sin_rem, \
         nc.sbuf_tensor("_p1a_end", [1, 4], F32) as _p1a_end:

        b_x4 = [Buf(), Buf()]
        b_xfm4 = [Buf(), Buf()]
        b_pstr1 = [Buf(), Buf()]
        b_psqk = [Buf(), Buf()]
        b_lnj = [Buf() for _ in range(4)]
        b_junk = Buf()
        b_sin = [Buf(), Buf()]
        b_sinr = Buf()
        b_psxl = [Buf(), Buf()]

        p.dma("gpsimd", wr_b[:], wr.rearrange("(c p) f -> p c f", p=128), writes=(b_wr,), tag="ldwr")
        b_wstg = Buf()
        p.dma("gpsimd", wstg[:, 0 : 8 * KD].rearrange("p (c f) -> p c f", f=KD),
              wq.rearrange("(c p) f -> p c f", p=128), writes=(b_wstg,), tag="ldwqr")
        for c in range(8):
            p.step("vector", lambda e, c=c: e.tensor_scalar_mul(
                wq_b[:, c, :], wstg[:, c * KD : (c + 1) * KD], g_sb[:, c : c + 1]),
                reads=(b_wstg, b_gsb), writes=(b_wq,), tag=f"foldq{c}")
        p.dma("gpsimd", wstg[:, 8 * KD : 16 * KD].rearrange("p (c f) -> p c f", f=KD),
              wk.rearrange("(c p) f -> p c f", p=128), writes=(b_wstg,), tag="ldwkr")
        for c in range(8):
            p.step("vector", lambda e, c=c: e.tensor_scalar_mul(
                wk_b[:, c, :], wstg[:, (8 + c) * KD : (9 + c) * KD], g_sb[:, c : c + 1]),
                reads=(b_wstg, b_gsb), writes=(b_wk,), tag=f"foldk{c}")
        with nc.psum_tensor("ps_xlr8", [128, 8, 512], F32) as ps_xlr8:
            b_psxl = Buf()
            for c in range(8):
                cb = c % 2
                p.dma("scalar", sin_sb[:, cb], sin_fm[c, :, 0:T], writes=(b_sin[cb],), tag=f"ldsin{c}")
                p.dma("scalar", sin_rem[:, c, :], sin_fm[c, :, T:NSIN], writes=(b_sinr,), tag=f"ldsinr{c}")
                for n in range(8):
                    p.step("tensor", lambda e, c=c, n=n, cb=cb: e.matmul(
                        ps_xlr8[:, n], wr_b[:, c, :], sin_sb[:, cb, n * 512 : (n + 1) * 512],
                        start=(c == 0), stop=(c == 7), skip_group_check=True),
                        reads=(b_wr, b_sin[cb]), writes=(b_psxl,), tag=f"xlrmm{c}.{n}")
            for n in range(8):
                p.step("scalar", lambda e, n=n: e.activation(
                    out=xlr[:, n * 512 : (n + 1) * 512], in_=ps_xlr8[:, n], func=AF.Identity),
                    reads=(b_psxl,), writes=(b_xlr,), tag=f"xlrev{n}")
            for c in range(8):
                p.step("tensor", lambda e, c=c: e.matmul(
                    ps_xlr8[:, 0, 0 : NSIN - T], wr_b[:, c, :], sin_rem[:, c, :],
                    start=(c == 0), stop=(c == 7), skip_group_check=True),
                    reads=(b_wr, b_sinr), writes=(b_psxl,), tag=f"xlrmr{c}")
            p.step("scalar", lambda e: e.activation(
                out=xlr[:, T:NSIN], in_=ps_xlr8[:, 0, 0 : NSIN - T], func=AF.Identity),
                reads=(b_psxl,), writes=(b_xlr,), tag="xlrevr")

        p1actx = contextlib.ExitStack()
        ps_tr1 = p1actx.enter_context(nc.psum_tensor("ps_tr1", [128, 2, 8, 128], BF16))
        ps_qk = p1actx.enter_context(nc.psum_tensor("ps_qk", [128, 2, 512], F32))

        for g in range(8):
            gb = g % 2
            src0 = x_full[g * 512 : (g + 1) * 512, :].rearrange("(j p) d -> p j d", p=128)
            p.dma("sync", x4[:, gb], src0, writes=(b_x4[gb],), tag=f"ldx{g}")
            for j in range(4):
                p.step("vector", lambda e, gb=gb, j=j: e.reduce_sum(
                    lnsc[:, j, 0:1], x4[:, gb, j, :], axis=AX.X),
                    reads=(b_x4[gb],), writes=(b_lnj[j],), tag=f"sum{g}.{j}")
            for j in range(4):
                p.step("vector", lambda e, j=j: e.tensor_scalar_mul(
                    lnsc[:, j, 1:2], lnsc[:, j, 0:1], -1.0 / D),
                    reads=(b_lnj[j],), writes=(b_lnj[j],), tag=f"negm{g}.{j}")
            for j in range(4):
                p.step("scalar", lambda e, gb=gb, j=j: e.activation(
                    out=junk[:], in_=x4[:, gb, j, :], func=AF.Square, bias=lnsc[:, j, 1:2],
                    scale=1.0, accum_out=lnsc[:, j, 2:3]),
                    reads=(b_x4[gb], b_lnj[j]), writes=(b_lnj[j], b_junk), tag=f"sq{g}.{j}")
            for j in range(4):
                p.step("scalar", lambda e, j=j: e.activation(
                    out=lnsc[:, j, 3:4], in_=lnsc[:, j, 2:3], func=AF.Ln,
                    bias=EPS1 * EPS2, scale=(1.0 + EPS2) / D),
                    reads=(b_lnj[j], b_consts), writes=(b_lnj[j],), tag=f"ln{g}.{j}")
            for j in range(4):
                p.step("scalar", lambda e, j=j: e.activation(
                    out=lnsc[:, j, 3:4], in_=lnsc[:, j, 3:4], func=AF.Exp, scale=-0.5),
                    reads=(b_lnj[j], b_consts), writes=(b_lnj[j],), tag=f"exp{g}.{j}")
            for j in range(4):
                p.step("vector", lambda e, j=j: e.tensor_mul(
                    lnsc[:, j, 1:2], lnsc[:, j, 1:2], lnsc[:, j, 3:4]),
                    reads=(b_lnj[j],), writes=(b_lnj[j],), tag=f"negms{g}.{j}")
            for j in range(4):
                t = 4 * g + j
                p.step("scalar", lambda e, gb=gb, j=j, t=t: e.activation(
                    out=xt_tm[:, t, :], in_=x4[:, gb, j, :], func=AF.Identity,
                    bias=lnsc[:, j, 1:2], scale=lnsc[:, j, 3:4]),
                    reads=(b_lnj[j], b_x4[gb]), writes=(b_xt[t],), tag=f"xt{t}")
            for j in range(4):
                t = 4 * g + j
                for c in range(8):
                    p.step("tensor", lambda e, t=t, c=c, gb=gb: e.transpose(
                        ps_tr1[:, gb, c, :], xt_tm[:, t, c * 128 : (c + 1) * 128], ident_s[:]),
                        reads=(b_xt[t], b_ident), writes=(b_pstr1[gb],), tag=f"tr{t}.{c}")
                p.step("vector", lambda e, gb=gb, j=j: e.tensor_copy(
                    xfm4[:, gb, :, j * 128 : (j + 1) * 128], ps_tr1[:, gb]),
                    reads=(b_pstr1[gb],), writes=(b_xfm4[gb],), tag=f"trev{t}")
            for c in range(8):
                p.step("tensor", lambda e, c=c, gb=gb: e.matmul(
                    ps_qk[:, gb], wk_b[:, c, :], xfm4[:, gb, c, :],
                    start=(c == 0), stop=(c == 7), skip_group_check=True),
                    reads=(b_wk, b_xfm4[gb]), writes=(b_psqk[gb],), tag=f"kmm{g}.{c}")
            p.step("scalar", lambda e, g=g, gb=gb: e.activation(
                out=k_fm[:, g * 512 : (g + 1) * 512], in_=ps_qk[:, gb], func=AF.Identity),
                reads=(b_psqk[gb],), writes=(b_kfm[g],), tag=f"kev{g}")

        for c in range(8):
            p.dma("gpsimd", wstg[:, 0:V], wv[c * 128 : (c + 1) * 128, :],
                  writes=(b_wstg,), tag=f"ldwv{c}")
            p.step("vector", lambda e, c=c: e.tensor_scalar_mul(
                wv_b[:, c, :], wstg[:, 0:V], g_sb[:, c : c + 1]),
                reads=(b_wstg, b_gsb), writes=(b_wv,), tag=f"foldv{c}")

        if DEBUG:
            for t in range(NT):
                p.dma("scalar", dbg["xt"][t * 128 : (t + 1) * 128, :], xt_tm[:, t, :], reads=(b_xt[t],), tag="dbgxt")
            for g in range(8):
                p.dma("scalar", dbg["k"][:, g * 512 : (g + 1) * 512], k_fm[:, g * 512 : (g + 1) * 512],
                      reads=(b_kfm[g],), tag="dbgk")
            p.dma("scalar", dbg["xlr"], xlr[:], reads=(b_xlr,), tag="dbgxlr")
        p1actx.close()

    p.next_phase()
    p.fence()
    with nc.sbuf_tensor("wg_b", [128, 8, V], BF16) as wg_b, \
         nc.sbuf_tensor("xo4", [128, 4, D], F32) as xo4, \
         nc.sbuf_tensor("xof4", [128, 2, 8, 512], BF16) as xof4, \
         nc.sbuf_tensor("xob", [128, 2, D], BF16) as xob, \
         nc.sbuf_tensor("gt", [128, 2, V], BF16) as gt, \
         nc.sbuf_tensor("lnst", [128, NQ, 2], F32) as lnst, \
         nc.sbuf_tensor("lnsg", [128, NQ, 2], F32) as lnsg, \
         nc.psum_tensor("ps_tr2", [128, 2, 8, 128], BF16) as ps_tr2, \
         nc.psum_tensor("ps_g", [128, 2, 512], F32) as ps_g, \
         nc.psum_tensor("ps_q", [128, 2, 512], F32) as ps_q:

        b_wstg2 = Buf()
        for c in range(8):
            p.dma("gpsimd", wstg[:, 0:V], wg[c * 128 : (c + 1) * 128, :],
                  writes=(b_wstg2,), tag=f"ldwg{c}")
            p.step("vector", lambda e, c=c: e.tensor_scalar_mul(
                wg_b[:, c, :], wstg[:, 0:V], g_sb[:, c : c + 1]),
                reads=(b_wstg2, b_gsb), writes=(b_wg,), tag=f"foldg{c}")

        b_xo = [Buf()]
        b_xof = [Buf(), Buf()]
        b_xob = [Buf(), Buf()]
        b_gt = [Buf(), Buf()]
        b_pstr2 = [Buf(), Buf()]
        b_psg = [Buf(), Buf()]
        b_psq = [Buf(), Buf()]
        b_lnk = [Buf() for _ in range(NQ)]
        b_lns = Buf()

        for g2 in range(4):
            src2 = x_own[g2 * 512 : (g2 + 1) * 512, :].rearrange("(j p) d -> p j d", p=128)
            p.dma("sync", xo4[:], src2, writes=(b_xo[0],), tag=f"ldxo{g2}")
            for j in range(4):
                k = 4 * g2 + j
                p.step("vector", lambda e, j=j, k=k: e.reduce_sum(
                    lnst[:, k, 0:1], xo4[:, j, :], axis=AX.X),
                    reads=(b_xo[0],), writes=(b_lnk[k],), tag=f"bsum{k}")
            for j in range(4):
                k = 4 * g2 + j
                p.step("vector", lambda e, k=k: e.tensor_scalar_mul(
                    lnst[:, k, 0:1], lnst[:, k, 0:1], -1.0 / D),
                    reads=(b_lnk[k],), writes=(b_lnk[k],), tag=f"bnegm{k}")
            for j in range(4):
                k = 4 * g2 + j
                p.step("scalar", lambda e, j=j, k=k: e.activation(
                    out=xob[:, 0], in_=xo4[:, j, :], func=AF.Square, bias=lnst[:, k, 0:1],
                    scale=1.0, accum_out=lnst[:, k, 1:2]),
                    reads=(b_xo[0], b_lnk[k]), writes=(b_lnk[k], b_xob[0]), tag=f"bsq{k}")
        p.step("scalar", lambda e: e.activation(
            out=lnsg[:, :, 1], in_=lnst[:, :, 1], func=AF.Ln, bias=EPS1 * EPS2,
            scale=(1.0 + EPS2) / D),
            reads=(b_consts,), deps=[b.w for b in b_lnk], writes=(b_lns,), tag="bln")
        p.step("scalar", lambda e: e.activation(
            out=lnsg[:, :, 1], in_=lnsg[:, :, 1], func=AF.Exp, scale=-0.5),
            reads=(b_lns, b_consts), writes=(b_lns,), tag="bexp")
        p.step("vector", lambda e: e.tensor_mul(lnsg[:, :, 0], lnst[:, :, 0], lnsg[:, :, 1]),
               reads=(b_lns,), deps=[b.w for b in b_lnk], writes=(b_lns,), tag="bnegms")

        def p1b_apply(k):
            g2, j = k // 4, k % 4
            kb2 = k % 2
            if j == 0:
                src2 = x_own[g2 * 512 : (g2 + 1) * 512, :].rearrange("(j p) d -> p j d", p=128)
                p.dma("sync", xo4[:], src2, writes=(b_xo[0],), tag=f"ldxo2_{g2}")
            p.step("scalar", lambda e, j=j, k=k, kb2=kb2: e.activation(
                out=xob[:, kb2], in_=xo4[:, j, :], func=AF.Identity, bias=lnsg[:, k, 0:1],
                scale=lnsg[:, k, 1:2]),
                reads=(b_lns, b_xo[0]), writes=(b_xob[kb2],), tag=f"bxt{k}")

        def p1b_tr(k):
            g2, j = k // 4, k % 4
            kb2 = k % 2
            gb = g2 % 2
            for c in range(8):
                p.step("tensor", lambda e, gb=gb, c=c, kb2=kb2: e.transpose(
                    ps_tr2[:, gb, c, :], xob[:, kb2, c * 128 : (c + 1) * 128], ident_s[:]),
                    reads=(b_xob[kb2], b_ident), writes=(b_pstr2[gb],), tag=f"btr{k}.{c}")
            p.step("vector", lambda e, gb=gb, j=j: e.tensor_copy(
                xof4[:, gb, :, j * 128 : (j + 1) * 128], ps_tr2[:, gb]),
                reads=(b_pstr2[gb],), writes=(b_xof[gb],), tag=f"btrev{k}")

        p1b_apply(0)
        p1b_tr(0)
        for k in range(NQ):
            g2, j = k // 4, k % 4
            gb = g2 % 2
            if k + 1 < NQ:
                p1b_apply(k + 1)
            for n in range(4):
                nb = n % 2
                for c in range(8):
                    p.step("tensor", lambda e, n=n, c=c, gb=gb, j=j, nb=nb: e.matmul(
                        ps_g[:, nb], xof4[:, gb, c, j * 128 : (j + 1) * 128],
                        wg_b[:, c, n * 512 : (n + 1) * 512],
                        start=(c == 0), stop=(c == 7), skip_group_check=True),
                        reads=(b_wg, b_xof[gb]), writes=(b_psg[nb],), tag=f"gmm{k}.{n}.{c}")
                p.step("scalar", lambda e, n=n, gb=gb, nb=nb: e.activation(
                    out=gt[:, gb, n * 512 : (n + 1) * 512], in_=ps_g[:, nb], func=AF.Silu),
                    reads=(b_psg[nb],), writes=(b_gt[gb],), tag=f"gev{k}.{n}")
            if k + 1 < NQ:
                p1b_tr(k + 1)
            p.dma("scalar", g_park[k * 128 : (k + 1) * 128, :], gt[:, gb], reads=(b_gt[gb],),
                  writes=(b_gpark[k],), tag=f"gpark{k}")
            if k % 4 == 3:
                for c in range(8):
                    p.step("tensor", lambda e, c=c, gb=gb: e.matmul(
                        ps_q[:, gb], wq_b[:, c, :], xof4[:, gb, c, :], start=(c == 0), stop=(c == 7),
                        skip_group_check=True),
                        reads=(b_wq, b_xof[gb]), writes=(b_psq[gb],), tag=f"qmm{g2}.{c}")
                p.step("scalar", lambda e, g2=g2, gb=gb: e.activation(
                    out=qu_fm[:, g2 * 512 : (g2 + 1) * 512], in_=ps_q[:, gb], func=AF.Identity,
                    bias=xlu_s[:], scale=1.0 / TAU),
                    reads=(b_psq[gb], b_xlu, b_consts), writes=(b_qu[g2],), tag=f"quev{g2}")
                p.step("scalar", lambda e, g2=g2, gb=gb: e.activation(
                    out=qv_fm[:, g2 * 512 : (g2 + 1) * 512], in_=ps_q[:, gb], func=AF.Identity,
                    bias=xlv_s[:], scale=1.0 / TAU),
                    reads=(b_psq[gb], b_xlv, b_consts), writes=(b_qu[g2],), tag=f"qvev{g2}")

        if DEBUG:
            for k in range(NQ):
                p.dma("scalar", dbg["g"][k * 128 : (k + 1) * 128, :], g_park[k * 128 : (k + 1) * 128, :],
                      deps=(b_gpark[k].w,), tag="dbgg")
            p.dma("scalar", dbg["qu"], qu_fm[:], deps=[b.w for b in b_qu[:4]], tag="dbgqu")
            p.dma("scalar", dbg["qv"], qv_fm[:], deps=[b.w for b in b_qu[:4]], tag="dbgqv")

    p.next_phase()
    p.fence()
    b_wost, b_w1st, b_w2st = Buf(), Buf(), Buf()
    p.dma("gpsimd", wo_st, wo, writes=(b_wost,), tag="stgwo")
    p.dma("gpsimd", w1_st, w1, writes=(b_w1st,), tag="stgw1")
    p.dma("gpsimd", w2_st, w2, writes=(b_w2st,), tag="stgw2")
    with nc.sbuf_tensor("dst_s", [128, 4224], F32) as dst_s, \
         nc.sbuf_tensor("a_s", [128, T], F32) as a_s, \
         nc.sbuf_tensor("bb_s", [128, T], F32) as bb_s, \
         nc.sbuf_tensor("mb_s", [128, 256], F32) as mb_s, \
         nc.sbuf_tensor("c_s", [128, T], BF16) as c_s, \
         nc.sbuf_tensor("pt_s", [128, 32, 128], BF16) as pt_s, \
         nc.sbuf_tensor("pxn_s", [128, D], BF16) as pxn_s, \
         nc.sbuf_tensor("pxf_s", [128, 8, 128], BF16) as pxf_s, \
         nc.sbuf_tensor("gt2_s", [128, V], BF16) as gt2_s, \
         nc.sbuf_tensor("zt_s", [128, V], BF16) as zt_s, \
         nc.sbuf_tensor("sm_s", [128, 4], F32) as sm_s, \
         nc.psum_tensor("ps_ba", [128, 2, 512], F32) as ps_ba, \
         nc.psum_tensor("ps_px", [128, 2, 512], F32) as ps_px, \
         nc.psum_tensor("ps_wv", [128, 4, 512], F32) as ps_wv:

        b_dst = Buf()
        b_a = Buf()
        b_bb = Buf()
        b_mb = Buf()
        b_c = Buf()
        b_pt = Buf()
        b_pxn = Buf()
        b_pxf = Buf()
        b_gt2 = Buf()
        b_zt = Buf()
        b_sm = Buf()
        b_psba = [Buf(), Buf()]
        b_pspx = Buf()
        b_pswv = Buf()
        b_bounce = Buf()

        def bd_block(kk):
            k = kk + 1
            Wb = 256 * k + 128
            Wq_ = 256 * k
            ce_lo = 4224 - 256 * k
            nbd = (Wb + 511) // 512
            for m in range(nbd):
                n0 = m * 512
                n1 = min(n0 + 512, Wb)
                mb2 = m % 2
                p.step("tensor", lambda e, kk=kk, mb2=mb2, n0=n0, n1=n1, ce_lo=ce_lo: e.matmul(
                    ps_ba[:, mb2, 0 : n1 - n0], qv_fm[:, kk * 128 : (kk + 1) * 128],
                    xlr[:, ce_lo + n0 : ce_lo + n1], start=True, stop=True,
                    skip_group_check=True),
                    reads=(b_qu[kk // 4], b_xlr), writes=(b_psba[mb2],), tag=f"bdmm{kk}.{m}")
                p.step("scalar", lambda e, mb2=mb2, n0=n0, n1=n1: e.activation(
                    out=dst_s[:, n0:n1], in_=ps_ba[:, mb2, 0 : n1 - n0], func=AF.Identity),
                    reads=(b_psba[mb2],), writes=(b_dst,), tag=f"bdev{kk}.{m}")
            p.dma("sync", bwr[:, 128 : 128 + Wb], dst_s[:, 0:Wb], reads=(b_dst,), writes=(b_bounce,), tag=f"bwr{kk}")
            p.dma("sync", lambda Wq_=Wq_: bb_s[:, 0:Wq_],
                  lambda Wq_=Wq_: brd[0:128, 255 : 255 + Wq_],
                  reads=(b_bounce,), writes=(b_bb,), tag=f"bdiag{kk}")

        def wv_mms(kk):
            for c in range(8):
                for n2 in range(4):
                    p.step("tensor", lambda e, kk=kk, c=c, n2=n2: e.matmul(
                        ps_wv[:, n2, :], pxf_s[:, c, :], wv_b[:, c, n2 * 512 : (n2 + 1) * 512],
                        start=(c == 0), stop=(c == 7), skip_group_check=True),
                        reads=(b_pxf, b_wv), writes=(b_pswv,), tag=f"wvmm{kk}.{c}.{n2}")

        def wv_gate(kk):
            p.step("vector", lambda e: e.tensor_mul(
                zt_s[:], ps_wv[:, :, :].rearrange("p a b -> p (a b)"), gt2_s[:]),
                reads=(b_pswv, b_gt2), writes=(b_zt,), tag=f"gate{kk}")
            p.dma("scalar", z_park[kk * 128 : (kk + 1) * 128, :], zt_s[:],
                  reads=(b_zt,), writes=(b_zpark[kk],), tag=f"zpark{kk}")

        bd_block(0)

        for kk in range(NQ):
            k = kk + 1
            Wq_ = 256 * k
            kb = kk % 2
            nch = 2 * k
            nac = (Wq_ + 511) // 512
            p.step("vector", lambda e, Wq_=Wq_: e.tensor_add(
                mb_s[:], bb_s[:, Wq_ - 256 : Wq_], mask_s[:]),
                reads=(b_bb, b_mask), writes=(b_mb,), tag=f"mbadd{kk}")
            for m in range(nac):
                n0 = m * 512
                n1 = min(n0 + 512, Wq_)
                mb2 = m % 2
                p.step("tensor", lambda e, kk=kk, mb2=mb2, n0=n0, n1=n1: e.matmul(
                    ps_ba[:, mb2, 0 : n1 - n0], qu_fm[:, kk * 128 : (kk + 1) * 128],
                    k_fm[:, n0:n1], start=True, stop=True, skip_group_check=True),
                    reads=(b_qu[kk // 4], b_kfm[n0 // 512]), writes=(b_psba[mb2],), tag=f"acmm{kk}.{m}")
                if n1 <= Wq_ - 256:
                    p.step("vector", lambda e, mb2=mb2, n0=n0, n1=n1: e.tensor_add(
                        a_s[:, n0:n1], ps_ba[:, mb2, 0 : n1 - n0], bb_s[:, n0:n1]),
                        reads=(b_psba[mb2], b_bb), writes=(b_a,), tag=f"acev{kk}.{m}")
                else:
                    lo = n0
                    mid = max(n0, Wq_ - 256)
                    if mid > lo:
                        p.step("vector", lambda e, mb2=mb2, lo=lo, mid=mid, n0=n0: e.tensor_add(
                            a_s[:, lo:mid], ps_ba[:, mb2, lo - n0 : mid - n0], bb_s[:, lo:mid]),
                            reads=(b_psba[mb2], b_bb), writes=(b_a,), tag=f"acevl{kk}.{m}")
                    p.step("vector", lambda e, mb2=mb2, mid=mid, n0=n0, n1=n1, Wq_=Wq_: e.tensor_add(
                        a_s[:, mid:n1], ps_ba[:, mb2, mid - n0 : n1 - n0],
                        mb_s[:, mid - (Wq_ - 256) : n1 - (Wq_ - 256)]),
                        reads=(b_psba[mb2], b_mb), writes=(b_a,), tag=f"acevm{kk}.{m}")
            if DEBUG and kk == 5:
                p.dma("sync", dbg["bb15"], bb_s[:], reads=(b_bb,), tag="dbgbb")
                p.dma("sync", dbg["a15"], a_s[:], reads=(b_a,), tag="dbga")
            if kk + 1 < NQ:
                bd_block(kk + 1)
            if kk > 0:
                wv_mms(kk - 1)
            p.step("vector", lambda e, Wq_=Wq_: e.tensor_reduce(
                sm_s[:, 0:1], a_s[:, 0:Wq_], axis=AX.X, op=ALU.max, negate=True),
                reads=(b_a,), writes=(b_sm,), tag=f"max{kk}")
            p.step("scalar", lambda e, Wq_=Wq_: e.activation(
                out=c_s[:, 0:Wq_], in_=a_s[:, 0:Wq_], func=AF.Exp, bias=sm_s[:, 0:1],
                accum_out=sm_s[:, 1:2]),
                reads=(b_a, b_sm), writes=(b_c, b_sm), tag=f"exp{kk}")
            p.step("vector", lambda e: e.reciprocal(sm_s[:, 2:3], sm_s[:, 1:2]),
                   reads=(b_sm,), writes=(b_sm,), tag=f"rcp{kk}")
            if kk > 0:
                wv_gate(kk - 1)
            p.dma("scalar", gt2_s[:], g_park[kk * 128 : (kk + 1) * 128, :],
                  deps=(b_gpark[kk].w,), writes=(b_gt2,), tag=f"gld{kk}")
            nrounds = (nch + 7) // 8
            for r in range(nrounds):
                c0 = r * 8
                c1 = min(c0 + 8, nch)
                rb = r % 2
                bview = ps_ba[:, rb, :].bitcast(BF16)
                for ci in range(c0, c1):
                    p.step("tensor", lambda e, ci=ci, c0=c0, rb=rb, bview=bview: e.transpose(
                        bview[:, (ci - c0) * 128 : (ci - c0 + 1) * 128],
                        c_s[:, ci * 128 : (ci + 1) * 128], ident_s[:]),
                        reads=(b_c, b_ident), writes=(b_psba[rb],), tag=f"ptt{kk}.{ci}")
                p.step("vector", lambda e, c0=c0, c1=c1, rb=rb, bview=bview: e.tensor_copy(
                    pt_s[:, c0:c1, :], bview[:, 0 : (c1 - c0) * 128]),
                    reads=(b_psba[rb],), writes=(b_pt,), tag=f"ptev{kk}.{r}")
            for ci in range(nch):
                for n2 in range(2):
                    p.step("tensor", lambda e, ci=ci, n2=n2, nch=nch: e.matmul(
                        ps_px[:, n2, :], pt_s[:, ci, :],
                        xt_tm[:, ci, n2 * 512 : (n2 + 1) * 512],
                        start=(ci == 0), stop=(ci == nch - 1), skip_group_check=True),
                        reads=(b_pt, b_xt[ci]), writes=(b_pspx,), tag=f"pxmm{kk}.{ci}.{n2}")
            if DEBUG and kk == 5:
                p.dma("sync", dbg["c15"], c_s[:], reads=(b_c,), tag="dbgc")
            p.step("scalar", lambda e: e.activation(
                out=pxn_s[:], in_=ps_px[:, :, :], func=AF.Identity, scale=sm_s[:, 2:3]),
                reads=(b_pspx, b_sm), writes=(b_pxn,), tag=f"pxn{kk}")
            if DEBUG and kk == 5:
                p.dma("sync", dbg["px15"], pxn_s[:], reads=(b_pxn,), tag="dbgpx")
            for hh in range(2):
                rb = hh % 2
                bview = ps_ba[:, rb, :].bitcast(BF16)
                for c2 in range(4):
                    c = hh * 4 + c2
                    p.step("tensor", lambda e, c=c, c2=c2, bview=bview: e.transpose(
                        bview[:, c2 * 128 : (c2 + 1) * 128], pxn_s[:, c * 128 : (c + 1) * 128], ident_s[:]),
                        reads=(b_pxn, b_ident), writes=(b_psba[rb],), tag=f"pxt{kk}.{c}")
                p.step("vector", lambda e, hh=hh, bview=bview, rb=rb: e.tensor_copy(
                    pxf_s[:, hh * 4 : (hh + 1) * 4, :], bview[:, 0:512]),
                    reads=(b_psba[rb],), writes=(b_pxf,), tag=f"pxtev{kk}.{hh}")

        if DEBUG:
            for kk in range(NQ):
                p.dma("scalar", dbg["z"][kk * 128 : (kk + 1) * 128, :],
                      z_park[kk * 128 : (kk + 1) * 128, :], deps=(b_zpark[kk].w,), tag="dbgz")
                p.dma("scalar", dbg["z2"][kk * 128 : (kk + 1) * 128, :],
                      z_park2[kk * 128 : (kk + 1) * 128, :], deps=(b_zpark[kk].w,), tag="dbgz2")

        wv_mms(NQ - 1)
        wv_gate(NQ - 1)

    bctx.close()

    octx = contextlib.ExitStack()
    w1_b = octx.enter_context(nc.sbuf_tensor("w1_b", [128, 8, HD], BF16))
    w2_b = octx.enter_context(nc.sbuf_tensor("w2_b", [128, 32, D], BF16))
    r1a1_b = octx.enter_context(nc.sbuf_tensor("r1a1_b", [128, 8, 32], BF16))
    r2a2_b = octx.enter_context(nc.sbuf_tensor("r2a2_b", [128, 32, 32], BF16))
    b1l_s = octx.enter_context(nc.sbuf_tensor("b1l_s", [128, 32], F32))
    b1f_b = octx.enter_context(nc.sbuf_tensor("b1f_b", [24, HD], BF16))
    b2f_b = octx.enter_context(nc.sbuf_tensor("b2f_b", [24, D], BF16))
    b2r_b = octx.enter_context(nc.sbuf_tensor("b2r_b", [1, D], BF16))
    ones1_s = octx.enter_context(nc.sbuf_tensor("ones1_s", [1, 128], BF16))
    b_w1, b_w2, b_r1, b_r2, b_b1l, b_b1f, b_b2f, b_b2r, b_ones = (
        Buf(), Buf(), Buf(), Buf(), Buf(), Buf(), Buf(), Buf(), Buf())

    p.next_phase()
    p.fence()
    with nc.sbuf_tensor("wo_b", [128, 16, D], BF16) as wo_b, \
         nc.sbuf_tensor("zl_s", [128, 2, V], BF16) as zl_s, \
         nc.sbuf_tensor("zf_s", [128, 2, 16, 128], BF16) as zf_s, \
         nc.sbuf_tensor("xo2_s", [128, 2, D], F32) as xo2_s, \
         nc.sbuf_tensor("xn_s", [128, D], F32) as xn_s, \
         nc.psum_tensor("ps_zt", [128, 2, 8, 128], BF16) as ps_zt, \
         nc.psum_tensor("ps_o", [128, 2, 2, 512], F32) as ps_o:

        b_wo = Buf()
        b_zl = [Buf(), Buf()]
        b_zf = [Buf(), Buf()]
        b_xoo2 = [Buf(), Buf()]
        b_xn = Buf()
        b_pszt = [Buf(), Buf()]
        b_pso = [Buf(), Buf()]

        p.dma("gpsimd", r1a1_b[:], r1a1.rearrange("(c p) f -> p c f", p=128), writes=(b_r1,), tag="ldr1")
        p.dma("gpsimd", r2a2_b[:], r2a2.rearrange("(c p) f -> p c f", p=128), writes=(b_r2,), tag="ldr2")
        p.dma("sync", b1l_s[:], b1L, writes=(b_b1l,), tag="ldb1l")
        p.dma("gpsimd", b1f_b[:], b1f, writes=(b_b1f,), tag="ldb1f")
        p.dma("gpsimd", b2f_b[:], b2f, writes=(b_b2f,), tag="ldb2f")
        p.dma("gpsimd", b2r_b[:], b2L, writes=(b_b2r,), tag="ldb2r")
        p.dma("sync", ones1_s[:], ones1, writes=(b_ones,), tag="ldones")

        def p2b_load(kk):
            kb = kk % 2
            p.dma("sync", zl_s[:, kb], z_park[kk * 128 : (kk + 1) * 128, :],
                  deps=(b_zpark[kk].w,), reads=(b_zf[kb],), writes=(b_zl[kb],), tag=f"zld{kk}")
            p.dma("scalar", xo_s[:], x_own[kk * 128 : (kk + 1) * 128, :],
                  writes=(b_xoo,), tag=f"xold{kk}") if False else None

        def p2b_xload(kk, slot):
            p.dma("scalar", xo2_s[:, slot], x_own[kk * 128 : (kk + 1) * 128, :],
                  writes=(b_xoo2[slot],), tag=f"xold{kk}")

        p2b_load(0)
        p2b_xload(0, 0)
        p2b_load(1)
        p2b_xload(1, 1)
        p.dma("scalar", wo_b[:], wo_st.rearrange("(c p) f -> p c f", p=128),
              deps=(b_wost.w,), writes=(b_wo,), tag="ldwo")
        p.dma("scalar", w1_b[:], w1_st.rearrange("(c p) f -> p c f", p=128),
              deps=(b_w1st.w,), writes=(b_w1,), tag="ldw1")
        p.dma("sync", w2_b[:], w2_st.rearrange("(c p) f -> p c f", p=128),
              deps=(b_w2st.w,), writes=(b_w2,), tag="ldw2")
        for kk in range(NQ):
            kb = kk % 2
            for h in range(2):
                for c in range(8):
                    cc = h * 8 + c
                    p.step("tensor", lambda e, kb=kb, h=h, c=c, cc=cc: e.transpose(
                        ps_zt[:, kb, c, :], zl_s[:, kb, cc * 128 : (cc + 1) * 128], ident_s[:]),
                        reads=(b_zl[kb], b_ident), writes=(b_pszt[kb],), tag=f"ztt{kk}.{cc}")
                p.step("vector", lambda e, kb=kb, h=h: e.tensor_copy(
                    zf_s[:, kb, h * 8 : (h + 1) * 8, :], ps_zt[:, kb]),
                    reads=(b_pszt[kb],), writes=(b_zf[kb],), tag=f"ztev{kk}.{h}")
            if kk + 2 < NQ:
                p2b_load(kk + 2)
            for cc in range(16):
                for n2 in range(2):
                    p.step("tensor", lambda e, kb=kb, cc=cc, n2=n2: e.matmul(
                        ps_o[:, kb, n2, :], zf_s[:, kb, cc, :], wo_b[:, cc, n2 * 512 : (n2 + 1) * 512],
                        start=(cc == 0), stop=(cc == 15), skip_group_check=True),
                        reads=(b_zf[kb], b_wo), writes=(b_pso[kb],), tag=f"omm{kk}.{cc}.{n2}")
            p.step("vector", lambda e, kb=kb, kk=kk: e.tensor_add(
                xn_s[:], ps_o[:, kb].rearrange("p a b -> p (a b)"), xo2_s[:, kk % 2]),
                reads=(b_pso[kb], b_xoo2[kk % 2]), writes=(b_xn,), tag=f"resid{kk}")
            p.dma("sync", xnew_d[kk * 128 : (kk + 1) * 128, :], xn_s[:],
                  reads=(b_xn,), writes=(b_xnew[kk],), tag=f"xnst{kk}")
            if kk + 2 < NQ:
                p2b_xload(kk + 2, (kk + 2) % 2)

    if DEBUG:
        p.next_phase()
        for kk in range(NQ):
            p.dma("scalar", dbg["xnew"][kk * 128 : (kk + 1) * 128, :],
                  xnew_d[kk * 128 : (kk + 1) * 128, :], deps=(b_xnew[kk].w,), tag="dbgxn")

    octx_close_marker = None

    p.next_phase()
    p.fence()
    p3ctx = contextlib.ExitStack()
    x2n = p3ctx.enter_context(nc.sbuf_tensor("x2n", [128, 2, D], F32))
    ybf = p3ctx.enter_context(nc.sbuf_tensor("ybf", [128, D], BF16))
    yfm = p3ctx.enter_context(nc.sbuf_tensor("yfm", [128, 8, 512], BF16))
    hfm = p3ctx.enter_context(nc.sbuf_tensor("hfm", [128, 32, 512], BF16))
    rz1 = p3ctx.enter_context(nc.sbuf_tensor("rz1", [32, 512], BF16))
    rt = p3ctx.enter_context(nc.sbuf_tensor("rt", [128, 4, 32], BF16))
    gatef = p3ctx.enter_context(nc.sbuf_tensor("gatef", [128, 4, 4], F32))
    gex = p3ctx.enter_context(nc.sbuf_tensor("gex", [128, 4, 24], F32))
    zgj = p3ctx.enter_context(nc.sbuf_tensor("zgj", [128, 4, 24], BF16))
    zg1t = p3ctx.enter_context(nc.sbuf_tensor("zg1t", [24, 512], BF16))
    zg2t = p3ctx.enter_context(nc.sbuf_tensor("zg2t", [24, 512], BF16))
    lnsc3 = p3ctx.enter_context(nc.sbuf_tensor("lnsc3", [128, 4, 4], F32))
    outst = p3ctx.enter_context(nc.sbuf_tensor("outst", [128, D], F32))
    sm3 = p3ctx.enter_context(nc.sbuf_tensor("sm3", [128, 4, 2], F32))
    ps3_tr = p3ctx.enter_context(nc.psum_tensor("ps3_tr", [128, 8, 128], BF16))
    ps_r = p3ctx.enter_context(nc.psum_tensor("ps_r", [128, 512], F32))
    ps_h = p3ctx.enter_context(nc.psum_tensor("ps_h", [128, 2, 512], F32))
    ps_w2 = p3ctx.enter_context(nc.psum_tensor("ps_w2", [128, 2, 2, 512], F32))
    if True:
        b_x2n = Buf()
        b_x2n = Buf()
        b_ybf = Buf()
        b_yfm = Buf()
        b_hfm = Buf()
        b_rz1 = Buf()
        b_rt = Buf()
        b_gate = Buf()
        b_gex = Buf()
        b_zgj = Buf()
        b_zg1t = Buf()
        b_zg2t = Buf()
        b_ln3 = [Buf() for _ in range(4)]
        b_outst = Buf()
        b_sm3 = Buf()
        b_pstr3 = Buf()
        b_psr = Buf()
        b_psh = [Buf(), Buf()]
        b_psw2 = [Buf(), Buf()]

        def router_dance(g3, zgT, b_zgT, b_src_fm, what):
            p.step("vector", lambda e: e.tensor_copy(rz1[:], ps_r[0:32, :]),
                   reads=(b_psr,), writes=(b_rz1,), tag=f"rzev{what}{g3}")
            for j in range(4):
                p.step("tensor", lambda e, j=j: e.transpose(
                    ps3_tr[:, j, 0:32], rz1[:, j * 128 : (j + 1) * 128], ident_s[0:32, 0:32]),
                    reads=(b_rz1, b_ident), writes=(b_pstr3,), tag=f"rtt{what}{g3}.{j}")
            p.step("vector", lambda e: e.tensor_copy(rt[:, :, :], ps3_tr[:, 0:4, 0:32]),
                   reads=(b_pstr3,), writes=(b_rt,), tag=f"rtev{what}{g3}")
            for j in range(4):
                p.step("vector", lambda e, j=j: e.tensor_reduce(
                    sm3[:, j, 0:1], rt[:, j, 0:3], axis=AX.X, op=ALU.max, negate=True),
                    reads=(b_rt,), writes=(b_sm3,), tag=f"rmax{what}{g3}.{j}")
            for j in range(4):
                p.step("scalar", lambda e, j=j: e.activation(
                    out=gatef[:, j, 0:3], in_=rt[:, j, 0:3], func=AF.Exp, bias=sm3[:, j, 0:1],
                    accum_out=sm3[:, j, 1:2]),
                    reads=(b_rt, b_sm3), writes=(b_gate, b_sm3), tag=f"rexp{what}{g3}.{j}")
            p.step("vector", lambda e: e.reciprocal(sm3[:, :, 1], sm3[:, :, 1]),
                   reads=(b_sm3,), writes=(b_sm3,), tag=f"rrcp{what}{g3}")
            for j in range(4):
                p.step("vector", lambda e, j=j: e.tensor_scalar_mul(
                    gatef[:, j, 0:3], rt[:, j, 0:3] if False else gatef[:, j, 0:3], sm3[:, j, 1:2]),
                    reads=(b_gate, b_sm3), writes=(b_gate,), tag=f"rnorm{what}{g3}.{j}")
            for j in range(4):
                for ee in range(3):
                    p.step("vector", lambda e, j=j, ee=ee: e.tensor_copy(
                        gex[:, j, ee * 8 : (ee + 1) * 8],
                        gatef[:, j, ee : ee + 1].broadcast_to([128, 8])),
                        reads=(b_gate,), writes=(b_gex,), tag=f"rgex{what}{g3}.{j}.{ee}")
            for j in range(4):
                p.step("vector", lambda e, j=j: e.scalar_tensor_tensor(
                    out=zgj[:, j, :], in0=rt[:, j, 3:27], scalar=0.125, in1=gex[:, j, :],
                    op0=ALU.mult, op1=ALU.mult),
                    reads=(b_rt, b_gex), writes=(b_zgj,), tag=f"rzg{what}{g3}.{j}")
            for j in range(4):
                p.step("tensor", lambda e, j=j: e.transpose(
                    ps3_tr[0:24, 4 + j, :], zgj[:, j, 0:24], ident_s[:]),
                    reads=(b_zgj, b_ident), writes=(b_pstr3,), tag=f"zgt{what}{g3}.{j}")
            p.step("vector", lambda e, zgT=zgT: e.tensor_copy(
                zgT.rearrange("q (j x) -> q j x", x=128), ps3_tr[0:24, 4:8, :]),
                reads=(b_pstr3,), writes=(b_zgT,), tag=f"zgev{what}{g3}")

        def load_xn(g3, half, tag):
            rows0 = g3 * 512 + half * 256
            src3 = xnew_d[rows0 : rows0 + 256, :].rearrange("(j p) d -> p j d", p=128)
            p.dma("sync", x2n[:], src3, writes=(b_x2n,),
                  deps=[b_xnew[rows0 // 128 + jj].w for jj in range(2)], tag=f"{tag}{g3}.{half}")

        for g3 in range(4):
            for j in range(4):
                if j % 2 == 0:
                    load_xn(g3, j // 2, "ldxn")
                xt3 = x2n[:, j % 2, :]
                sc = lnsc3[:, j, :]
                p.step("vector", lambda e, xt3=xt3, sc=sc: e.reduce_sum(sc[:, 0:1], xt3, axis=AX.X),
                       reads=(b_x2n,), writes=(b_ln3[j],), tag=f"fsum{g3}.{j}")
                p.step("vector", lambda e, sc=sc: e.tensor_scalar_mul(sc[:, 1:2], sc[:, 0:1], -1.0 / D),
                       reads=(b_ln3[j],), writes=(b_ln3[j],), tag=f"fnegm{g3}.{j}")
                p.step("scalar", lambda e, xt3=xt3, sc=sc: e.activation(
                    out=ybf[:], in_=xt3, func=AF.Square, bias=sc[:, 1:2], scale=1.0, accum_out=sc[:, 2:3]),
                    reads=(b_x2n, b_ln3[j]), writes=(b_ln3[j], b_ybf), tag=f"fsq{g3}.{j}")
                p.step("scalar", lambda e, sc=sc: e.activation(
                    out=sc[:, 3:4], in_=sc[:, 2:3], func=AF.Ln, bias=1e-5, scale=1.0 / D),
                    reads=(b_ln3[j], b_consts), writes=(b_ln3[j],), tag=f"fln{g3}.{j}")
                p.step("scalar", lambda e, sc=sc: e.activation(
                    out=sc[:, 3:4], in_=sc[:, 3:4], func=AF.Exp, scale=-0.5),
                    reads=(b_ln3[j], b_consts), writes=(b_ln3[j],), tag=f"fexp{g3}.{j}")
                p.step("vector", lambda e, sc=sc: e.tensor_mul(sc[:, 1:2], sc[:, 1:2], sc[:, 3:4]),
                       reads=(b_ln3[j],), writes=(b_ln3[j],), tag=f"fnegms{g3}.{j}")
                p.step("scalar", lambda e, xt3=xt3, sc=sc: e.activation(
                    out=ybf[:], in_=xt3, func=AF.Identity, bias=sc[:, 1:2], scale=sc[:, 3:4]),
                    reads=(b_ln3[j], b_x2n), writes=(b_ybf,), tag=f"fy{g3}.{j}")
                for c in range(8):
                    p.step("tensor", lambda e, c=c: e.transpose(
                        ps3_tr[:, c, :], ybf[:, c * 128 : (c + 1) * 128], ident_s[:]),
                        reads=(b_ybf, b_ident), writes=(b_pstr3,), tag=f"ftr{g3}.{j}.{c}")
                p.step("vector", lambda e, j=j: e.tensor_copy(
                    yfm[:, :, j * 128 : (j + 1) * 128], ps3_tr[:, :, :]),
                    reads=(b_pstr3,), writes=(b_yfm,), tag=f"ftrev{g3}.{j}")
            for c in range(8):
                p.step("tensor", lambda e, c=c: e.matmul(
                    ps_r[0:32, :], r1a1_b[:, c, :], yfm[:, c, :], start=(c == 0), stop=(c == 7),
                    skip_group_check=True),
                    reads=(b_r1, b_yfm), writes=(b_psr,), tag=f"r1mm{g3}.{c}")
            router_dance(g3, zg1t, b_zg1t, b_yfm, "a")
            for hc in range(32):
                hb = hc % 2
                for c in range(8):
                    p.step("tensor", lambda e, hc=hc, hb=hb, c=c: e.matmul(
                        ps_h[:, hb, :], w1_b[:, c, hc * 128 : (hc + 1) * 128], yfm[:, c, :],
                        start=(c == 0), stop=False, skip_group_check=True),
                        reads=(b_w1, b_yfm), writes=(b_psh[hb],), tag=f"w1mm{g3}.{hc}.{c}")
                p.step("tensor", lambda e, hc=hc, hb=hb: e.matmul(
                    ps_h[:, hb, :], b1f_b[0:24, hc * 128 : (hc + 1) * 128], zg1t[:],
                    start=False, stop=True, skip_group_check=True),
                    reads=(b_b1f, b_zg1t), writes=(b_psh[hb],), tag=f"b1mm{g3}.{hc}")
                p.step("scalar", lambda e, hc=hc, hb=hb: e.activation(
                    out=hfm[:, hc, :], in_=ps_h[:, hb, :], func=AF.Relu, bias=b1l_s[:, hc : hc + 1]),
                    reads=(b_psh[hb], b_b1l), writes=(b_hfm,), tag=f"hev{g3}.{hc}")
            for hc in range(32):
                p.step("tensor", lambda e, hc=hc: e.matmul(
                    ps_r[0:32, :], r2a2_b[:, hc, :], hfm[:, hc, :], start=(hc == 0), stop=(hc == 31),
                    skip_group_check=True),
                    reads=(b_r2, b_hfm), writes=(b_psr,), tag=f"r2mm{g3}.{hc}")
            router_dance(g3, zg2t, b_zg2t, b_hfm, "b")
            for j in range(4):
                jb = j % 2
                if j % 2 == 0:
                    load_xn(g3, j // 2, "rldxn")
                for hc in range(32):
                    for n2 in range(2):
                        p.step("tensor", lambda e, hc=hc, n2=n2, j=j, jb=jb: e.matmul(
                            ps_w2[:, jb, n2, :], hfm[:, hc, j * 128 : (j + 1) * 128],
                            w2_b[:, hc, n2 * 512 : (n2 + 1) * 512],
                            start=(hc == 0), stop=False, skip_group_check=True),
                            reads=(b_w2, b_hfm), writes=(b_psw2[jb],), tag=f"w2mm{g3}.{j}.{hc}.{n2}")
                for n2 in range(2):
                    p.step("tensor", lambda e, n2=n2, j=j, jb=jb: e.matmul(
                        ps_w2[:, jb, n2, :], zg2t[:, j * 128 : (j + 1) * 128], b2f_b[0:24, n2 * 512 : (n2 + 1) * 512],
                        start=False, stop=False, skip_group_check=True),
                        reads=(b_zg2t, b_b2f), writes=(b_psw2[jb],), tag=f"b2mm{g3}.{j}.{n2}")
                    p.step("tensor", lambda e, n2=n2, jb=jb: e.matmul(
                        ps_w2[:, jb, n2, :], ones1_s[:], b2r_b[0:1, n2 * 512 : (n2 + 1) * 512],
                        start=False, stop=True, skip_group_check=True),
                        reads=(b_ones, b_b2r), writes=(b_psw2[jb],), tag=f"bbmm{g3}.{j}.{n2}")
                p.step("vector", lambda e, j=j, jb=jb: e.tensor_add(
                    outst[:], ps_w2[:, jb].rearrange("p a b -> p (a b)"), x2n[:, j % 2, :]),
                    reads=(b_psw2[jb], b_x2n), writes=(b_outst,), tag=f"oev{g3}.{j}")
                p.dma("sync", out_own[(g3 * 4 + j) * 128 : (g3 * 4 + j + 1) * 128, :], outst[:],
                      reads=(b_outst,), tag=f"ost{g3}.{j}")

    p3ctx.close()
    octx.close()

    final_cnt = p.finalize(nc, sems)

    with nc.Block() as blk:
        def fin(e):
            for eng in ("sync", "scalar", "gpsimd"):
                for lane in QLANES[eng]:
                    if final_cnt[lane]:
                        e.wait_ge(sems[lane], final_cnt[lane])

        blk.sync(fin)

    ctx.close()
    return nc



def _inputs_for_cores(ins):
    x = np.asarray(ins["L0_attn_x"], np.float32)

    def f32(a):
        return np.ascontiguousarray(np.asarray(a, np.float32))

    A1 = f32(ins["L0_ffn_A1"]).transpose(1, 0, 2).reshape(D, 24)
    A2 = f32(ins["L0_ffn_A2"]).transpose(1, 0, 2).reshape(HD, 24)
    common = {
        "wq": f32(ins["L0_attn_W_q"]),
        "wk": f32(ins["L0_attn_W_k"]),
        "wr": f32(ins["L0_attn_W_r"]),
        "wv": f32(ins["L0_attn_W_v"]),
        "wg": f32(ins["L0_attn_W_g"]),
        "wo": f32(ins["L0_attn_W_o"]),
        "w1": f32(ins["L0_ffn_W1"]),
        "w2": f32(ins["L0_ffn_W2"]),
        "g_lnL": np.ascontiguousarray(f32(ins["L0_attn_g_ln"]).reshape(8, 128).T),
        "xl_u": f32(ins["L0_attn_xl_u"]).reshape(128, 1),
        "xl_v": f32(ins["L0_attn_xl_v"]).reshape(128, 1),
        "r1a1": np.ascontiguousarray(np.concatenate(
            [f32(ins["L0_ffn_R1"]), A1, np.zeros((D, 5), np.float32)], axis=1)),
        "b1f": f32(ins["L0_ffn_B1"]).reshape(24, HD),
        "b1L": np.ascontiguousarray(f32(ins["L0_ffn_b1"]).reshape(32, 128).T),
        "r2a2": np.ascontiguousarray(np.concatenate(
            [f32(ins["L0_ffn_R2"]), A2, np.zeros((HD, 5), np.float32)], axis=1)),
        "b2f": f32(ins["L0_ffn_B2"]).reshape(24, D),
        "b2L": f32(ins["L0_ffn_b2"]).reshape(1, D),
    }
    masks = [_mask256(0), _mask256(1)]
    sins = [_sinusoid_fm(0), _sinusoid_fm(1)]
    in_maps = []
    for core in range(8):
        b, parity = core // 2, core % 2
        xb = np.ascontiguousarray(x[b])
        xow = np.ascontiguousarray(xb.reshape(NT, 128, D)[parity::2].reshape(NQ * 128, D))
        m = dict(common)
        m["x_full"] = xb
        m["x_own"] = xow
        m["mask256"] = masks[parity]
        m["sin_fm"] = sins[parity]
        in_maps.append(m)
    return in_maps


def kernel(**inputs):
    in_maps = _inputs_for_cores(inputs)
    nc = build_program()
    res = run_bass_kernel_spmd(nc, in_maps, core_ids=list(range(8)))
    out = np.empty((B, T, D), np.float32)
    for core in range(8):
        b, parity = core // 2, core % 2
        o = np.asarray(res.results[core]["out_own"], np.float32).reshape(NQ, 128, D)
        out.reshape(B, NT, 128, D)[b, parity::2] = o
    return out
